# version 15
# baseline (speedup 1.0000x reference)
import contextlib
import numpy as np
import ml_dtypes
import concourse.bass as bass
import concourse.mybir as mybir
from concourse.bass_utils import run_bass_kernel_spmd

F32 = mybir.dt.float32
BF16 = mybir.dt.bfloat16
AF = mybir.ActivationFunctionType
ALU = mybir.AluOpType
AX = mybir.AxisListType

B, T, D = 16, 2048, 1024
H, DK, DV = 16, 64, 64
DFF = 2736
NE, ESL, ESZ = 16, 4, 64
NES = NE * ESL
GRP = 16
EPS_RMS, EPS_LN = 1e-6, 1e-5

NCORE = 8
SPC = B // NCORE
TC = 1024
NCH = T // TC
PT = 256
NPT = TC // PT
C = 64
NC_T = PT // C
NKC = D // 128
W3KC = (DFF + 127) // 128
EB = 16
NEB = NES // EB
NTT = TC // 128
NLN = TC // 512

DBG = {"name": None, "shape": None}


class Reg:
    __slots__ = ("wtok", "rtoks")

    def __init__(self):
        self.wtok = None
        self.rtoks = []


class Bld:
    ENGS = ("t", "v", "s", "g", "y")
    NLANES = {"g": 20, "y": 8}

    def __init__(self, nc, stack):
        self.nc = nc
        self.ops = {e: [] for e in self.ENGS}
        self.cnt = {e: 0 for e in self.ENGS}
        self.waited = {e: {} for e in self.ENGS}
        self.sems = {e: stack.enter_context(nc.semaphore("sem_" + e)) for e in ("t", "v", "s", "p")}
        self.cnt["p"] = 0
        self.rr = {"g": 0, "y": 0}
        for e in ("g", "y"):
            for i in range(self.NLANES[e]):
                k = e + str(i)
                self.sems[k] = stack.enter_context(nc.semaphore("sem_" + k))
                self.cnt[k] = 0
                for c in self.waited.values():
                    pass

    def _wait(self, eng, pe, v):
        if self.waited[eng].get(pe, 0) >= v:
            return
        self.waited[eng][pe] = v
        sem = self.sems[pe]
        self.ops[eng].append(lambda e, sem=sem, v=v: e.wait_ge(sem, v))

    def _dma_lane(self, eng):
        i = self.rr[eng] % self.NLANES[eng]
        self.rr[eng] += 1
        return eng + str(i)

    def waits(self, eng, toks):
        best = {}
        for t in toks:
            if t is None:
                continue
            pe, v = t
            if best.get(pe, 0) < v:
                best[pe] = v
        for pe, v in best.items():
            self._wait(eng, pe, v)

    def emit(self, eng, fns, reads=(), writes=(), extra=()):
        if not isinstance(fns, (list, tuple)):
            fns = [fns]
        def _add(n, t):
            if isinstance(t, list):
                n.extend(t)
            elif t is not None:
                n.append(t)
        need = []
        for t_ in extra:
            _add(need, t_)
        for r in reads:
            _add(need, r.wtok)
        for w in writes:
            _add(need, w.wtok)
            need.extend(w.rtoks)
        if eng == "p":
            self.waits("g", need)
            sem = self.sems["p"]
            for i, fn in enumerate(fns):
                if i == len(fns) - 1:
                    self.ops["g"].append(lambda e, fn=fn, sem=sem: fn(e).then_inc(sem, 1))
                else:
                    self.ops["g"].append(fn)
            self.cnt["p"] += 1
            tok = ("p", self.cnt["p"])
            for r in reads:
                r.rtoks.append(tok)
            for w in writes:
                w.wtok = tok
                w.rtoks = []
            return tok
        self.waits(eng, need)
        if eng in ("g", "y"):
            lane = self._dma_lane(eng)
            sem = self.sems[lane]
            self._wait(eng, lane, self.cnt[lane])
            for fn in fns:
                self.ops[eng].append(lambda e, fn=fn, sem=sem: fn(e).then_inc(sem, 16))
            self.cnt[lane] += 16 * len(fns)
            tok = (lane, self.cnt[lane])
        else:
            sem = self.sems[eng]
            for i, fn in enumerate(fns):
                if i == len(fns) - 1:
                    self.ops[eng].append(lambda e, fn=fn, sem=sem: fn(e).then_inc(sem, 1))
                else:
                    self.ops[eng].append(fn)
            self.cnt[eng] += 1
            tok = (eng, self.cnt[eng])
        for r in reads:
            r.rtoks.append(tok)
        for w in writes:
            w.wtok = tok
            w.rtoks = []
        return tok

    def tok(self, eng):
        return (eng, self.cnt[eng])

    def barrier(self, extra=()):
        pb = [self.tok("t"), self.tok("v"), self.tok("s"), self.tok("p")] + list(extra)
        for e in ("t", "v", "s"):
            self.waits(e, pb)
        return pb


def build_nc():
    nc = bass.Bass("TRN2", target_bir_lowering=False)
    stack = contextlib.ExitStack()

    dx = nc.dram_tensor("x", [SPC, T, D], F32, kind="ExternalInput").ap()
    dy = nc.dram_tensor("y", [SPC, T, D], F32, kind="ExternalOutput").ap()
    dW = {}
    for nm, shp in [("Wq", [D, D]), ("Wk", [D, D]), ("Wv", [D, D]), ("Wg", [D, D]),
                    ("Wo", [D, D]), ("w1", [D, DFF]), ("w2", [D, DFF]), ("w3", [DFF, D]),
                    ("ga", [D]), ("gf", [D]), ("ctl", [D, NE, ESL]),
                    ("f1", [D, NE, ESL, ESZ]), ("f2", [NE, ESL, ESZ, D]),
                    ("fb", [NE, ESL, ESZ])]:
        dW[nm] = nc.dram_tensor(nm, shp, F32, kind="ExternalInput").ap()
    dC = {}
    for nm, shp in [("ident", [128, 128]), ("maskT", [128, 128]), ("hsel", [128, NKC * 16]),
                    ("hrep", [16, NKC * 128]), ("ones1", [128, 1]), ("row1", [1, 128]),
                    ("bdP", [128, 8]), ("maskT2", [128, 64]), ("seye", [64, 128]),
                    ("bdm", [128, 128])]:
        dC[nm] = nc.dram_tensor("c_" + nm, shp, BF16, kind="ExternalInput").ap()
    dC["tie"] = nc.dram_tensor("c_tie", [NES, GRP], F32, kind="ExternalInput").ap()
    dC["eps"] = nc.dram_tensor("c_eps", [128, 3], F32, kind="ExternalInput").ap()
    ddbg = None
    if DBG["name"]:
        ddbg = nc.dram_tensor("dbg", DBG["shape"], DBG.get("dtype", F32), kind="ExternalOutput").ap()

    sbt = lambda name, shape, dt: stack.enter_context(nc.sbuf_tensor(name, shape, dt))
    WA = sbt("WA", [128, 66304], BF16)
    SB = sbt("SBs", [128, NKC * TC], BF16)
    YT = sbt("YT", [128, NTT * 1024], BF16)
    SC = sbt("SCb", [128, 4992], BF16)
    SF = sbt("SFf", [128, 3334], F32)
    STT_ = sbt("STATE", [128, NKC * 128], BF16)
    CN = {k: sbt("k" + k, list(dC[k].shape), dC[k].dtype) for k in dC}
    GA = sbt("gacol", [128, NKC], F32)
    GF = sbt("gfcol", [128, NKC], F32)
    FBIA = sbt("fbias", [ESZ, NES], F32)
    CTL = sbt("ctlw", [128, NKC * NES], BF16)
    ROWS = sbt("rows", [128, 1536], F32)
    RB = sbt("rowsb", [128, 1024], BF16)
    XS = [sbt("XS0", [128, 2 * D], BF16), sbt("XS1", [128, 2 * D], BF16)]

    pst = lambda name, shape, dt: stack.enter_context(nc.psum_tensor(name, shape, dt))
    PA = pst("PA", [128, 512], F32)
    PB = pst("PB", [128, 512], F32)
    PC = pst("PC", [128, 512], F32)
    PDB = pst("PDB", [128, 1024], BF16)
    PD = PDB.bitcast(F32)
    PS_ = pst("PS", [128, 512], F32)
    PR = pst("PR", [128, 512], F32)
    PRB = PR.bitcast(BF16)
    PG = pst("PG", [128, 512], F32)
    PGB = PG.bitcast(BF16)
    PTB = pst("PTB", [128, 1024], BF16)

    bld = Bld(nc, stack)

    S_SQ, S_IR, S_TMP, S_SCM, S_VH, S_EQ, S_EK = 0, 1024, 1280, 1536, 2048, 2304, 2816
    S_KTS = 1792
    S_VTS = 2048
    S_SMM = 3328
    S_KBD, S_VBD = 3456, 3968
    S_MR, S_VR = 1024, 1536
    S_PERM, S_PERMT = 2048, 3072
    S_P, S_PT2, S_INN, S_STG = 3584, 3712, 3840, 3968
    F_SP, F_SCUM, F_COLS = 0, 512, 1024
    F_LG, F_GM = 0, 1024
    F_YO = 1216
    F_PUT = 3270

    rWA = {}
    rXS = [Reg(), Reg()]
    rCN = Reg()
    rSB = [Reg() for _ in range(NKC)]
    rSBtm = [Reg() for _ in range(NTT)]
    rYT, rSC, rSF, rROWS, rRB = {}, {}, {}, {}, {}
    rST = Reg()
    rPB = {k: Reg() for k in ("PA", "PB", "PC", "PD", "PS", "PR", "PG", "PTB")}

    def reg(d, key):
        if key not in d:
            d[key] = Reg()
        return d[key]

    for k in dC:
        bld.emit("g", lambda e, a=CN[k], b=dC[k]: e.dma_start(out=a[:, :], in_=b), writes=[rCN])
    bld.emit("g", lambda e: e.dma_start(out=GA[:, :], in_=dW["ga"].rearrange("(c p) -> p c", p=128), allow_slow_non_contiguous=True), writes=[rCN])
    bld.emit("g", lambda e: e.dma_start(out=GF[:, :], in_=dW["gf"].rearrange("(c p) -> p c", p=128), allow_slow_non_contiguous=True), writes=[rCN])
    bld.emit("g", lambda e: e.dma_start(out=FBIA[:, :], in_=dW["fb"].rearrange("e s f -> f (e s)"), allow_slow_non_contiguous=True), writes=[rCN])
    bld.emit("g", lambda e: e.dma_start(out=CTL[:, :], in_=dW["ctl"].rearrange("(c p) e s -> p c (e s)", p=128)), writes=[rCN])

    OFF = {"Wq": 0, "Wk": 8192, "Wv": 16384, "Wg": 24576, "Wo": 32768,
           "w2": 0, "w3": 21888, "w1": 44416,
           "f1_0": 0, "f1_1": 8192, "f2_0": 16384, "f2_1": 24576,
           "sel": 32768, "inter": 49152}
    WCOLS = {"Wq": 1024, "Wk": 1024, "Wv": 1024, "Wg": 1024, "Wo": 1024,
             "w1": DFF, "w2": DFF, "w3": 1024}

    def load_w(name, src_fn, nchunks, ncols, off, extra=(), prows_fn=None):
        r = reg(rWA, name)
        war = (r.wtok if isinstance(r.wtok, list) else [r.wtok]) + r.rtoks
        toks = []
        for c in range(nchunks):
            pr = prows_fn(c) if prows_fn else 128
            toks.append(bld.emit("g", lambda e, c=c, pr=pr: e.dma_start(
                out=WA[0:pr, off + c * ncols: off + (c + 1) * ncols], in_=src_fn(c)),
                extra=(list(extra) + war) if c == 0 else []))
        r.wtok = toks
        r.rtoks = []
        return r

    def wsl(name, kc, col0, ncols, prows=128):
        o = OFF[name] + kc * WCOLS[name] + col0
        return WA[0:prows, o:o + ncols]

    def p64(hh):
        return slice(hh * 64, (hh + 1) * 64)

    dbg_toks = []

    def dbg(name, ap, deps, regs=()):
        if DBG["name"] != name or ddbg is None:
            return
        dbg_toks.append(bld.emit("g", lambda e: e.dma_start(out=ddbg, in_=ap),
                                 reads=list(regs), extra=deps))

    out_toks = []
    first_phase = True

    for seq in range(SPC):
        for ch in range(NCH):
            t0g = ch * TC
            pb = [] if first_phase else bld.barrier(extra=dbg_toks)
            first_phase = False
            if ch == 0:
                bld.emit("v", lambda e: e.memset(STT_[:, :], 0),
                         writes=[rST] + [reg(rYT, "ST%d" % hp) for hp in range(NKC)])
            bld.emit("v", lambda e: e.memset(SC[:, S_KBD:S_KBD + 512], 0),
                     writes=[reg(rSC, "KBD")], extra=pb)

            x2_done = []
            xp_emitted = [False] * NPT

            def emit_xload(ti2, seq2, base2):
                if ti2 >= NPT or xp_emitted[ti2]:
                    return
                xp_emitted[ti2] = True
                par2 = ti2 % 2
                t02 = base2 + ti2 * PT
                bld.emit("g", lambda e, par2=par2, seq2=seq2, t02=t02: e.dma_start(
                    out=XS[par2][:, :],
                    in_=dx[seq2, t02:t02 + PT, :].rearrange("(a p) d -> p a d", p=128)),
                    writes=[rXS[par2]])

            emit_xload(0, seq, t0g)
            emit_xload(1, seq, t0g)
            for wn in ("Wq", "Wk", "Wv", "Wg", "Wo"):
                load_w(wn, lambda c, wn=wn: dW[wn][c * 128:(c + 1) * 128, :], NKC, 1024,
                       OFF[wn], extra=pb)
            for ti in range(NPT):
                par = ti % 2
                t0 = t0g + ti * PT
                tl = ti * PT
                rh = reg(rYT, "A_H")
                rqh, rkh = reg(rYT, "A_QH"), reg(rYT, "A_KH")
                rvf, ro = reg(rYT, "A_VF"), reg(rYT, "A_H")
                A_H, A_QH, A_KH, A_O, A_VF = 0, 2048, 4096, 0, 6144
                for a in range(2):
                    bld.emit("t", [lambda e, a=a, kc=kc, par=par: e.transpose(
                        PTB[:, kc * 128:(kc + 1) * 128],
                        XS[par][:, a * D + kc * 128: a * D + (kc + 1) * 128],
                        CN["ident"][:, :]) for kc in range(NKC)],
                        reads=[rXS[par], rCN], writes=[rPB["PTB"]], extra=pb)
                    bld.emit("v", lambda e, a=a, tl=tl: e.tensor_copy(
                        SB[:, :].rearrange("p (c m) -> p c m", m=TC)[:, :, tl + a * 128:tl + (a + 1) * 128],
                        PTB[:, :].rearrange("p (c n) -> p c n", n=128)),
                        reads=[rPB["PTB"]],
                        writes=[rSB[kc2] for kc2 in range(NKC)] if a == 1 else [], extra=pb)
                emit_xload(ti + 2, seq, t0g)
                for kc in range(NKC):
                    rsq = reg(rSC, "SQ%d" % (kc % 2))
                    bld.emit("v", lambda e, kc=kc, tl=tl: e.tensor_tensor(
                        SC[:, S_SQ + (kc % 2) * 256:S_SQ + (kc % 2) * 256 + 256],
                        SB[:, kc * TC + tl:kc * TC + tl + 256],
                        SB[:, kc * TC + tl:kc * TC + tl + 256], ALU.mult),
                        reads=[rSB[kc]], writes=[rsq])
                    bld.emit("t", lambda e, kc=kc: e.matmul(
                        PS_[0:1, 0:256], CN["ones1"][:, :],
                        SC[:, S_SQ + (kc % 2) * 256:S_SQ + (kc % 2) * 256 + 256],
                        start=(kc == 0), stop=(kc == NKC - 1)),
                        reads=[rsq, rCN], writes=[rPB["PS"]] if kc in (0, NKC - 1) else [])
                rr1 = reg(rROWS, "r1")
                bld.emit("s", lambda e: e.activation(ROWS[0:1, 0:256], PS_[0:1, 0:256], AF.Ln,
                                                     bias=CN["eps"][0:1, 0:1], scale=1.0 / D),
                         reads=[rPB["PS"]], writes=[rr1])
                rb1 = reg(rRB, "r1")
                bld.emit("s", lambda e: e.activation(RB[0:1, 0:256], ROWS[0:1, 0:256], AF.Exp, scale=-0.5),
                         reads=[rr1], writes=[rb1])
                bld.emit("t", lambda e: e.matmul(PR[:, 0:256], CN["row1"][:, :], RB[0:1, 0:256],
                                                 start=True, stop=True),
                         reads=[rb1, rCN], writes=[rPB["PR"]])
                bld.emit("v", lambda e: e.tensor_copy(SC[:, S_IR:S_IR + 256], PR[:, 0:256]),
                         reads=[rPB["PR"]], writes=[reg(rSC, "IR")])
                bld.emit("v", [lambda e, kc=kc, tl=tl: e.scalar_tensor_tensor(
                    YT[:, A_H + kc * 256:A_H + (kc + 1) * 256],
                    SB[:, kc * TC + tl:kc * TC + tl + 256],
                    GA[:, kc:kc + 1], SC[:, S_IR:S_IR + 256], ALU.mult, ALU.mult)
                    for kc in range(NKC)],
                    reads=list(rSB) + [reg(rSC, "IR"), rCN], writes=[rh])
                dbg("h", YT[:, A_H:A_H + 2048], [rh.wtok], regs=[rh])

                accs = [("PA0", PA, 0), ("PA1", PA, 256), ("PB0", PB, 0), ("PB1", PB, 256)]
                na = [0]

                def proj_mm(wn, mc):
                    nm, pt_, c0 = accs[na[0] % 4]
                    na[0] += 1
                    sl = rPB[nm[:2]]
                    bld.emit("t", [lambda e, kc=kc, mc=mc, wn=wn, pt_=pt_, c0=c0: e.matmul(
                        pt_[:, c0:c0 + 256], wsl(wn, kc, mc * 128, 128),
                        YT[:, A_H + kc * 256:A_H + (kc + 1) * 256],
                        start=(kc == 0), stop=(kc == NKC - 1)) for kc in range(NKC)],
                        reads=[rh, reg(rWA, wn)], writes=[sl])
                    return sl, pt_, c0

                sq_toks = {}
                for wn, dst, rdst in (("Wq", A_QH, rqh), ("Wk", A_KH, rkh)):
                    for mc in range(NKC):
                        sl, pt_, c0 = proj_mm(wn, mc)
                        ev = bld.emit("s", lambda e, pt_=pt_, c0=c0, dst=dst, mc=mc: e.copy(
                            YT[:, dst + mc * 256:dst + (mc + 1) * 256], pt_[:, c0:c0 + 256]),
                            reads=[sl], writes=[rdst])
                sstat = rPB["PS"]
                for wn, dst, sc0, rdst in (("Wq", A_QH, 0, rqh), ("Wk", A_KH, 256, rkh)):
                    for mc in range(NKC):
                        off2 = S_TMP if mc % 2 == 0 else S_VH
                        rsqt = reg(rSC, "TMP%d" % (mc % 2))
                        bld.emit("s", lambda e, dst=dst, mc=mc, off2=off2: e.activation(
                            SC[:, off2:off2 + 256],
                            YT[:, dst + mc * 256:dst + (mc + 1) * 256], AF.Square),
                            reads=[rdst], writes=[rsqt])
                        bld.emit("t", lambda e, mc=mc, sc0=sc0, off2=off2: e.matmul(
                            PS_[0:16, sc0:sc0 + 256], CN["hsel"][:, mc * 16:(mc + 1) * 16],
                            SC[:, off2:off2 + 256],
                            start=(mc == 0), stop=(mc == NKC - 1)),
                            reads=[rsqt, rCN],
                            writes=[sstat] if mc in (0, NKC - 1) else [])

                for which, sc0, ro0 in (("q", 0, 256), ("k", 256, 768)):
                    rmx = reg(rROWS, "mx" + which)
                    bld.emit("v", lambda e, sc0=sc0, ro0=ro0: e.tensor_scalar_max(
                        ROWS[0:16, ro0:ro0 + 256], PS_[0:16, sc0:sc0 + 256], 1e-24),
                        reads=[rPB["PS"]], writes=[rmx])
                    bld.emit("s", lambda e, ro0=ro0: e.activation(
                        ROWS[0:16, ro0 + 256:ro0 + 512], ROWS[0:16, ro0:ro0 + 256], AF.Ln),
                        reads=[rmx], writes=[rmx])
                    rrb = reg(rRB, "rq" + which)
                    bld.emit("s", lambda e, ro0=ro0, rb0=(256 if ro0 == 256 else 512): e.activation(
                        RB[0:16, rb0:rb0 + 256], ROWS[0:16, ro0 + 256:ro0 + 512], AF.Exp, scale=-0.5),
                        reads=[rmx], writes=[rrb])

                for mc in range(NKC):
                    sl, pt_, c0 = proj_mm("Wv", mc)
                    bld.emit("s", lambda e, pt_=pt_, c0=c0, mc=mc: e.copy(
                        YT[:, A_VF + mc * 256:A_VF + (mc + 1) * 256], pt_[:, c0:c0 + 256]),
                        reads=[sl], writes=[rvf])

                gcols = []

                def gate_tail(mc):
                    rot = mc % 2
                    rbank = PR if rot == 0 else PS_
                    rslot = rPB["PR" if rot == 0 else "PS"]
                    bld.emit("t", lambda e, mc=mc, rbank=rbank: e.matmul(
                        rbank[:, 0:512], CN["hrep"][:, mc * 128:(mc + 1) * 128],
                        RB[0:16, 256:768], start=True, stop=True),
                        reads=[reg(rRB, "rqq"), reg(rRB, "rqk"), rCN], writes=[rslot])
                    for which, bc0, dst, rdst, eoff, toff, tnm in (
                            ("q", 0, A_QH, rqh, S_EQ, S_TMP, "TMP0"),
                            ("k", 256, A_KH, rkh, S_EK, S_VH, "TMP1")):
                        rexp = reg(rSC, ("EQ%d" if which == "q" else "EK%d") % rot)
                        bld.emit("v", lambda e, rot=rot, eoff=eoff, rbank=rbank, toff=toff, bc0=bc0: e.tensor_tensor(
                            SC[:, toff:toff + 256], rbank[:, bc0:bc0 + 256],
                            SC[:, eoff + rot * 256:eoff + rot * 256 + 256], ALU.mult),
                            reads=[rslot, rexp], writes=[reg(rSC, tnm)])
                        bld.emit("p", lambda e, mc=mc, dst=dst, toff=toff: e.tensor_tensor(
                            YT[:, dst + mc * 256:dst + (mc + 1) * 256],
                            YT[:, dst + mc * 256:dst + (mc + 1) * 256],
                            SC[:, toff:toff + 256], ALU.mult),
                            reads=[reg(rSC, tnm)], writes=[rdst], extra=pb)

                for mc in range(NKC):
                    rot = mc % 2
                    sl, pt_, c0 = proj_mm("Wg", mc)
                    rsp = reg(rSF, "SP%d" % rot)
                    rscu = reg(rSF, "SCUM%d" % rot)
                    bld.emit("v", lambda e, pt_=pt_, c0=c0, rot=rot: e.tensor_scalar(
                        SF[:, F_SP + rot * 256:F_SP + rot * 256 + 256], pt_[:, c0:c0 + 256],
                        -1.0, 40.0, ALU.mult, ALU.min), reads=[sl], writes=[rsp])
                    bld.emit("s", lambda e, rot=rot: e.activation(
                        SF[:, F_SCUM + rot * 256:F_SCUM + rot * 256 + 256],
                        SF[:, F_SP + rot * 256:F_SP + rot * 256 + 256], AF.Exp),
                        reads=[rsp], writes=[rscu])
                    bld.emit("s", lambda e, rot=rot: e.activation(
                        SF[:, F_SP + rot * 256:F_SP + rot * 256 + 256],
                        SF[:, F_SCUM + rot * 256:F_SCUM + rot * 256 + 256], AF.Ln,
                        bias=CN["eps"][0:128, 2:3]),
                        reads=[rscu], writes=[rsp])
                    bld.emit("v", lambda e, rot=rot: e.tensor_tensor_scan(
                        SF[:, F_SCUM + rot * 256:F_SCUM + rot * 256 + 256],
                        SF[:, F_SP + rot * 256:F_SP + rot * 256 + 256],
                        SF[:, F_SP + rot * 256:F_SP + rot * 256 + 256],
                        0.0, ALU.add, ALU.bypass), reads=[rsp], writes=[rscu])
                    base = F_COLS + mc * 24
                    rcol = reg(rSF, "COL%d" % mc)
                    bld.emit("v", [
                        lambda e, rot=rot, base=base: e.tensor_copy(
                            SF[:, base:base + NC_T],
                            SF[:, F_SCUM + rot * 256:F_SCUM + rot * 256 + 256]
                            .rearrange("p (c n) -> p c n", n=C)[:, :, C // 2 - 1]),
                        lambda e, rot=rot, base=base: e.tensor_copy(
                            SF[:, base + 4:base + 8],
                            SF[:, F_SCUM + rot * 256:F_SCUM + rot * 256 + 256]
                            .rearrange("p (c n) -> p c n", n=C)[:, :, C - 1]),
                    ], reads=[rscu], writes=[rcol])
                    bld.emit("v", [
                        lambda e, base=base: e.tensor_scalar_mul(
                            SF[:, base + 8:base + 9], SF[:, base + 4:base + 5], -1.0),
                        lambda e, base=base: e.tensor_sub(
                            SF[:, base + 9:base + 12], SF[:, base + 4:base + 7], SF[:, base + 5:base + 8]),
                    ], reads=[rcol], writes=[rcol])
                    bld.emit("s", [
                        lambda e, base=base: e.activation(SF[:, base + 12:base + 16],
                                                          SF[:, base + 8:base + 12], AF.Exp),
                        lambda e, base=base: e.activation(SF[:, base + 16:base + 20],
                                                          SF[:, base:base + 4], AF.Exp, scale=-1.0),
                    ], reads=[rcol], writes=[rcol])
                    bld.emit("v", lambda e, base=base: e.tensor_sub(
                        SF[:, base + 8:base + 12], SF[:, base:base + 4], SF[:, base + 4:base + 8]),
                        reads=[rcol], writes=[rcol])
                    bld.emit("s", lambda e, base=base: e.activation(
                        SF[:, base + 16:base + 20], SF[:, base + 8:base + 12], AF.Exp),
                        reads=[rcol], writes=[rcol])
                    bld.emit("v", [
                        lambda e, base=base: e.tensor_scalar_mul(
                            SF[:, base + 8:base + 9], SF[:, base:base + 1], -1.0),
                        lambda e, base=base: e.tensor_sub(
                            SF[:, base + 9:base + 12], SF[:, base + 4:base + 7], SF[:, base + 1:base + 4]),
                    ], reads=[rcol], writes=[rcol])
                    bld.emit("s", lambda e, base=base: e.activation(
                        SF[:, base + 20:base + 24], SF[:, base + 8:base + 12], AF.Exp),
                        reads=[rcol], writes=[rcol])
                    gcols.append((base, rcol))
                    bld.emit("v", lambda e, rot=rot, base=base: e.tensor_tensor(
                        SF[:, F_SP + rot * 256:F_SP + rot * 256 + 256].rearrange("p (c n) -> p c n", n=C),
                        SF[:, F_SCUM + rot * 256:F_SCUM + rot * 256 + 256].rearrange("p (c n) -> p c n", n=C),
                        SF[:, base:base + NC_T].rearrange("p c -> p c ()").broadcast_to([128, NC_T, C]),
                        ALU.subtract), reads=[rscu, rcol], writes=[rsp])
                    req = reg(rSC, "EQ%d" % rot)
                    rek = reg(rSC, "EK%d" % rot)
                    bld.emit("s", lambda e, rot=rot: e.activation(
                        SC[:, S_EQ + rot * 256:S_EQ + rot * 256 + 256],
                        SF[:, F_SP + rot * 256:F_SP + rot * 256 + 256], AF.Exp, scale=-1.0),
                        reads=[rsp], writes=[req])
                    bld.emit("s", lambda e, rot=rot: e.activation(
                        SC[:, S_EK + rot * 256:S_EK + rot * 256 + 256],
                        SF[:, F_SP + rot * 256:F_SP + rot * 256 + 256], AF.Exp),
                        reads=[rsp], writes=[rek])
                    if mc > 0:
                        gate_tail(mc - 1)
                gate_tail(NKC - 1)
                dbg("qt", YT[:, A_QH:A_QH + 2048], [rqh.wtok], regs=[rqh])
                dbg("kt", YT[:, A_KH:A_KH + 2048], [rkh.wtok], regs=[rkh])

                GP = [dict(kk=4096, kts=S_KTS, vts=S_VTS, vbd=S_VBD, scm=S_SCM,
                           kkn="KK0", ktsn="KTS", vtsn="VTS", vbdn="VBD", scmn="SCM0",
                           tb=PTB, tbn="PTB", scb=PG, scbn="PG", smm=S_SMM, smmn="SMM",
                           pob=PC, pobn="PC", pub=PD, pubn="PD"),
                      dict(kk=4160, kts=4224, vts=4352, vbd=4480, scm=4608,
                           kkn="KK1", ktsn="KTS1", vtsn="VTS1", vbdn="VBD1", scmn="SCM1",
                           tb=PRB, tbn="PR", scb=PS_, scbn="PS", smm=4672, smmn="SMM1",
                           pob=PA, pobn="PA", pub=PB, pubn="PB")]
                for hp in range(NKC):
                    base, rcol = gcols[hp]
                    rkbd = reg(rSC, "KBD")
                    bld.emit("p", [lambda e, hp=hp, hh=hh: e.tensor_copy(
                        SC[p64(hh), S_KBD:S_KBD + 512].rearrange("p (c n) -> p c n", n=128)[:, :, hh * 64:(hh + 1) * 64],
                        YT[p64(hh), A_KH + hp * 256:A_KH + (hp + 1) * 256].rearrange("p (c n) -> p c n", n=64))
                        for hh in range(2)],
                        reads=[rkh], writes=[rkbd], extra=pb)

                    def gla_tr(cc):
                        g = GP[cc % 2]
                        rtb = rPB[g["tbn"]]
                        bld.emit("t", [
                            lambda e, hp=hp, cc=cc, g=g: e.transpose(
                                g["tb"][0:C, 0:128],
                                YT[:, A_KH + hp * 256 + cc * C:A_KH + hp * 256 + (cc + 1) * C],
                                CN["ident"][:, :]),
                            lambda e, hp=hp, cc=cc, g=g: e.transpose(
                                g["tb"][0:C, 128:256],
                                YT[:, A_VF + hp * 256 + cc * C:A_VF + hp * 256 + (cc + 1) * C],
                                CN["ident"][:, :]),
                        ], reads=[rkh, rvf, rCN], writes=[rtb])
                        rkts = reg(rSC, g["ktsn"])
                        rvts = reg(rSC, g["vtsn"])
                        bld.emit("s", [
                            lambda e, g=g: e.copy(SC[0:C, g["kts"]:g["kts"] + 128], g["tb"][0:C, 0:128]),
                            lambda e, g=g: e.copy(SC[0:C, g["vts"]:g["vts"] + 128], g["tb"][0:C, 128:256]),
                        ], reads=[rtb], writes=[rkts, rvts])
                        return (rkts, rvts)

                    def gla_prep(cc, tr_):
                        g = GP[cc % 2]
                        rkts, rvts = tr_
                        slc = rPB[g["scbn"]]
                        bld.emit("t", lambda e, g=g: e.matmul(
                            g["scb"][:, 256:384], CN["seye"][:, :], SC[0:C, g["vts"]:g["vts"] + 128],
                            start=True, stop=True),
                            reads=[rvts, rCN], writes=[slc])
                        bld.emit("t", lambda e, hp=hp, cc=cc, g=g: e.matmul(
                            g["scb"][:, 0:C],
                            SC[:, S_KBD + cc * 128:S_KBD + (cc + 1) * 128],
                            YT[:, A_QH + hp * 256 + cc * C:A_QH + hp * 256 + (cc + 1) * C],
                            start=True, stop=True),
                            reads=[rkbd, rqh], writes=[slc])
                        rvbd = reg(rSC, g["vbdn"])
                        rscm = reg(rSC, g["scmn"])
                        rqg = reg(rSC, g["smmn"])
                        bld.emit("v", [
                            lambda e, g=g: e.tensor_tensor(
                                SC[:, g["vbd"]:g["vbd"] + 128], g["scb"][:, 256:384],
                                CN["bdm"][:, :], ALU.mult),
                            lambda e, g=g: e.tensor_tensor(
                                SC[:, g["scm"]:g["scm"] + C], g["scb"][:, 0:C],
                                CN["maskT2"][:, :], ALU.mult),
                            lambda e, hp=hp, cc=cc, base=base, g=g: e.tensor_scalar(
                                SC[:, g["smm"]:g["smm"] + C],
                                YT[:, A_QH + hp * 256 + cc * C:A_QH + hp * 256 + (cc + 1) * C],
                                SF[:, base + 20 + cc:base + 21 + cc], None, ALU.mult),
                        ], reads=[slc, rCN, rqh, rcol], writes=[rvbd, rscm, rqg])
                        return (rkts, rvts, rvbd, rscm, rqg)

                    trs = {0: gla_tr(0), 1: gla_tr(1)}
                    pregs = {0: gla_prep(0, trs[0])}
                    for cc in range(NC_T):
                        g = GP[cc % 2]
                        rkts, rvts, rvbd, rscm, rqg = pregs[cc]
                        slo = rPB[g["pobn"]]
                        poc0 = (cc & 7) * 64
                        bld.emit("t", [
                            lambda e, g=g, poc0=poc0: e.matmul(
                                g["pob"][:, poc0:poc0 + C], SC[:, g["vbd"]:g["vbd"] + 128],
                                SC[:, g["scm"]:g["scm"] + C], start=True, stop=False),
                            lambda e, hp=hp, g=g, poc0=poc0: e.matmul(
                                g["pob"][:, poc0:poc0 + C], STT_[:, hp * 128:(hp + 1) * 128],
                                SC[:, g["smm"]:g["smm"] + C],
                                start=False, stop=True),
                        ], reads=[rvbd, rscm, rqg, rST], writes=[slo])
                        slu = rPB[g["pubn"]]
                        puc0 = (cc & 7) * 64
                        bld.emit("t", [lambda e, hh=hh, g=g, puc0=puc0: e.matmul(
                            g["pub"][p64(hh), puc0:puc0 + 64],
                            SC[0:C, g["kts"] + hh * 64:g["kts"] + (hh + 1) * 64],
                            SC[0:C, g["vts"] + hh * 64:g["vts"] + (hh + 1) * 64],
                            start=True, stop=True) for hh in range(2)],
                            reads=[rkts, rvts], writes=[slu])
                        rput = reg(rSF, "PUT")
                        bld.emit("v", lambda e, base=base, cc=cc, g=g, puc0=puc0: e.tensor_scalar(
                            SF[:, F_PUT:F_PUT + 64], g["pub"][:, puc0:puc0 + 64],
                            SF[:, base + 16 + cc:base + 17 + cc], None, ALU.mult),
                            reads=[slu, rcol], writes=[rput])
                        bld.emit("v", [lambda e, hp=hp, cc=cc, base=base, hh=hh:
                                       e.scalar_tensor_tensor(
                            STT_[p64(hh), hp * 128 + hh * 64:hp * 128 + (hh + 1) * 64],
                            STT_[p64(hh), hp * 128 + hh * 64:hp * 128 + (hh + 1) * 64],
                            SF[p64(hh), base + 12 + cc:base + 13 + cc],
                            SF[p64(hh), F_PUT:F_PUT + 64],
                            ALU.mult, ALU.add) for hh in range(2)],
                            reads=[rput, rcol, rST], writes=[rST])
                        if cc + 2 < NC_T:
                            trs[cc + 2] = gla_tr(cc + 2)
                        if cc + 1 < NC_T:
                            pregs[cc + 1] = gla_prep(cc + 1, trs[cc + 1])
                        bld.emit("s", lambda e, hp=hp, cc=cc, g=g, poc0=poc0: e.copy(
                            YT[:, A_O + hp * 256 + cc * C:A_O + hp * 256 + (cc + 1) * C],
                            g["pob"][:, poc0:poc0 + C]),
                            reads=[slo], writes=[ro])
                dbg("o", YT[:, A_O:A_O + 2048], [ro.wtok], regs=[ro])

                for mc in range(NKC):
                    nm, pt_, c0 = accs[na[0] % 4]
                    na[0] += 1
                    sl = rPB[nm[:2]]
                    bld.emit("t", [lambda e, kc=kc, mc=mc, pt_=pt_, c0=c0: e.matmul(
                        pt_[:, c0:c0 + 256], wsl("Wo", kc, mc * 128, 128),
                        YT[:, A_O + kc * 256:A_O + (kc + 1) * 256],
                        start=(kc == 0), stop=(kc == NKC - 1)) for kc in range(NKC)],
                        reads=[ro, reg(rWA, "Wo")], writes=[sl])
                    tkx = bld.emit("v", lambda e, pt_=pt_, c0=c0, mc=mc, tl=tl: e.tensor_tensor(
                        SB[:, mc * TC + tl:mc * TC + tl + PT],
                        SB[:, mc * TC + tl:mc * TC + tl + PT],
                        pt_[:, c0:c0 + 256], ALU.add),
                        reads=[sl], writes=[rSB[mc]])
                    x2_done.append(tkx)
            dbg("x2", SB[:, 0:NKC * TC], x2_done, regs=list(rSB))

            pb = bld.barrier(extra=dbg_toks)
            z_done = []
            xp_emitted = [False] * NPT
            emit_xload(0, seq, t0g)
            emit_xload(1, seq, t0g)
            load_w("w1", lambda c: dW["w1"][c * 128:(c + 1) * 128, :], NKC, DFF, OFF["w1"], extra=pb)
            load_w("w2", lambda c: dW["w2"][c * 128:(c + 1) * 128, :], NKC, DFF, OFF["w2"], extra=pb)
            load_w("w3", lambda c: dW["w3"][c * 128:min(DFF, (c + 1) * 128), :], W3KC, 1024, OFF["w3"],
                   extra=pb, prows_fn=lambda c: min(128, DFF - c * 128))
            for ti in range(NPT):
                par = ti % 2
                t0 = t0g + ti * PT
                tl = ti * PT
                rh2, ru = reg(rYT, "B_H2"), reg(rYT, "B_U")
                B_H2, B_U = 0, 2048
                for kc in range(NKC):
                    rsq = reg(rSC, "SQ%d" % (kc % 2))
                    bld.emit("v", lambda e, kc=kc, tl=tl: e.tensor_tensor(
                        SC[:, S_SQ + (kc % 2) * 256:S_SQ + (kc % 2) * 256 + 256],
                        SB[:, kc * TC + tl:kc * TC + tl + 256],
                        SB[:, kc * TC + tl:kc * TC + tl + 256], ALU.mult),
                        reads=[rSB[kc]], writes=[rsq], extra=pb)
                    bld.emit("t", lambda e, kc=kc: e.matmul(
                        PS_[0:1, 0:256], CN["ones1"][:, :],
                        SC[:, S_SQ + (kc % 2) * 256:S_SQ + (kc % 2) * 256 + 256],
                        start=(kc == 0), stop=(kc == NKC - 1)),
                        reads=[rsq, rCN], writes=[rPB["PS"]] if kc in (0, NKC - 1) else [])
                rr1 = reg(rROWS, "r1")
                bld.emit("s", lambda e: e.activation(ROWS[0:1, 0:256], PS_[0:1, 0:256], AF.Ln,
                                                     bias=CN["eps"][0:1, 0:1], scale=1.0 / D),
                         reads=[rPB["PS"]], writes=[rr1])
                rb1 = reg(rRB, "r1")
                bld.emit("s", lambda e: e.activation(RB[0:1, 0:256], ROWS[0:1, 0:256], AF.Exp, scale=-0.5),
                         reads=[rr1], writes=[rb1])
                bld.emit("t", lambda e: e.matmul(PR[:, 0:256], CN["row1"][:, :], RB[0:1, 0:256],
                                                 start=True, stop=True),
                         reads=[rb1, rCN], writes=[rPB["PR"]])
                bld.emit("v", lambda e: e.tensor_copy(SC[:, S_IR:S_IR + 256], PR[:, 0:256]),
                         reads=[rPB["PR"]], writes=[reg(rSC, "IR")])
                bld.emit("v", [lambda e, kc=kc, tl=tl: e.scalar_tensor_tensor(
                    YT[:, B_H2 + kc * 256:B_H2 + (kc + 1) * 256],
                    SB[:, kc * TC + tl:kc * TC + tl + 256],
                    GF[:, kc:kc + 1], SC[:, S_IR:S_IR + 256], ALU.mult, ALU.mult)
                    for kc in range(NKC)],
                    reads=list(rSB) + [reg(rSC, "IR"), rCN], writes=[rh2])
                for a in range(2):
                    bld.emit("t", [lambda e, a=a, kc=kc, par=par: e.transpose(
                        PTB[:, kc * 128:(kc + 1) * 128],
                        XS[par][:, a * D + kc * 128: a * D + (kc + 1) * 128],
                        CN["ident"][:, :]) for kc in range(NKC)],
                        reads=[rXS[par], rCN, rh2], writes=[rPB["PTB"]], extra=pb)
                    bld.emit("v", lambda e, a=a, tl=tl: e.tensor_tensor(
                        SB[:, :].rearrange("p (c m) -> p c m", m=TC)[:, :, tl + a * 128:tl + (a + 1) * 128],
                        SB[:, :].rearrange("p (c m) -> p c m", m=TC)[:, :, tl + a * 128:tl + (a + 1) * 128],
                        PTB[:, :].rearrange("p (c n) -> p c n", n=128), ALU.add),
                        reads=[rPB["PTB"]], writes=[rSB[kc2] for kc2 in range(NKC)])
                emit_xload(ti + 2, seq, t0g)

                accsB = [("PA0", PA, 0), ("PA1", PA, 256), ("PB0", PB, 0), ("PB1", PB, 256),
                         ("PC0", PC, 0), ("PC1", PC, 256), ("PD0", PD, 0), ("PD1", PD, 256)]
                nb = [0]
                for mc in range(W3KC):
                    rows = min(128, DFF - mc * 128)
                    nm, pt_, c0 = accsB[nb[0] % 8]
                    nb[0] += 1
                    sl = rPB[nm[:2]]
                    bld.emit("t", [lambda e, kc=kc, mc=mc, pt_=pt_, c0=c0, rows=rows: e.matmul(
                        pt_[0:rows, c0:c0 + 256], wsl("w1", kc, mc * 128, rows),
                        YT[:, B_H2 + kc * 256:B_H2 + (kc + 1) * 256],
                        start=(kc == 0), stop=(kc == NKC - 1)) for kc in range(NKC)],
                        reads=[rh2, reg(rWA, "w1")], writes=[sl])
                    bld.emit("s", lambda e, pt_=pt_, c0=c0, mc=mc, rows=rows: e.activation(
                        YT[0:rows, B_U + mc * 256:B_U + (mc + 1) * 256],
                        pt_[0:rows, c0:c0 + 256], AF.Silu),
                        reads=[sl], writes=[ru] if mc == W3KC - 1 else [])
                for mc in range(W3KC):
                    rows = min(128, DFF - mc * 128)
                    nm, pt_, c0 = accsB[nb[0] % 8]
                    nb[0] += 1
                    sl = rPB[nm[:2]]
                    bld.emit("t", [lambda e, kc=kc, mc=mc, pt_=pt_, c0=c0, rows=rows: e.matmul(
                        pt_[0:rows, c0:c0 + 256], wsl("w2", kc, mc * 128, rows),
                        YT[:, B_H2 + kc * 256:B_H2 + (kc + 1) * 256],
                        start=(kc == 0), stop=(kc == NKC - 1)) for kc in range(NKC)],
                        reads=[rh2, reg(rWA, "w2")], writes=[sl])
                    bld.emit("v", lambda e, pt_=pt_, c0=c0, mc=mc, rows=rows: e.tensor_tensor(
                        YT[0:rows, B_U + mc * 256:B_U + (mc + 1) * 256],
                        YT[0:rows, B_U + mc * 256:B_U + (mc + 1) * 256],
                        pt_[0:rows, c0:c0 + 256], ALU.mult),
                        reads=[sl, ru], writes=[ru] if mc == W3KC - 1 else [])
                for mc in range(NKC):
                    nm, pt_, c0 = accsB[nb[0] % 8]
                    nb[0] += 1
                    sl = rPB[nm[:2]]
                    bld.emit("t", [lambda e, kc=kc, mc=mc, pt_=pt_, c0=c0,
                                   rows=min(128, DFF - kc * 128): e.matmul(
                        pt_[:, c0:c0 + 256], wsl("w3", kc, mc * 128, 128, prows=rows),
                        YT[0:rows, B_U + kc * 256:B_U + (kc + 1) * 256],
                        start=(kc == 0), stop=(kc == W3KC - 1)) for kc in range(W3KC)],
                        reads=[ru, reg(rWA, "w3")], writes=[sl])
                    tkz = bld.emit("v", lambda e, pt_=pt_, c0=c0, mc=mc, tl=tl: e.tensor_tensor(
                        SB[:, mc * TC + tl:mc * TC + tl + 256],
                        SB[:, mc * TC + tl:mc * TC + tl + 256], pt_[:, c0:c0 + 256], ALU.add),
                        reads=[sl], writes=[rSB[mc]])
                    z_done.append(tkz)
            dbg("z", SB[:, 0:NKC * TC], z_done, regs=list(rSB))

            pb = list(dbg_toks)
            for lam in range(NLN):
                l0 = lam * 512
                bld.emit("v", lambda e, l0=l0: e.tensor_tensor(
                    SC[:, 0:4096].rearrange("p (c n) -> p c n", n=512),
                    SB[:, :].rearrange("p (c m) -> p c m", m=TC)[:, :, l0:l0 + 512],
                    SB[:, :].rearrange("p (c m) -> p c m", m=TC)[:, :, l0:l0 + 512], ALU.mult),
                    reads=list(rSB), writes=[reg(rSC, "SQ4")], extra=pb)
                bld.emit("t", [lambda e, kc=kc, l0=l0: e.matmul(
                    PS_[0:1, 0:512], CN["ones1"][:, :], SB[:, kc * TC + l0:kc * TC + l0 + 512],
                    start=(kc == 0), stop=(kc == NKC - 1)) for kc in range(NKC)],
                    reads=list(rSB) + [rCN], writes=[rPB["PS"]], extra=pb)
                bld.emit("t", [lambda e, kc=kc: e.matmul(
                    PR[0:1, 0:512], CN["ones1"][:, :], SC[:, kc * 512:(kc + 1) * 512],
                    start=(kc == 0), stop=(kc == NKC - 1)) for kc in range(NKC)],
                    reads=[reg(rSC, "SQ4"), rCN], writes=[rPB["PR"]])
                rln = reg(rROWS, "ln")
                bld.emit("v", lambda e: e.tensor_scalar_mul(ROWS[0:1, 0:512], PS_[0:1, 0:512], 1.0 / D),
                         reads=[rPB["PS"]], writes=[rln])
                bld.emit("v", [
                    lambda e: e.tensor_tensor(ROWS[0:1, 512:1024], ROWS[0:1, 0:512],
                                              ROWS[0:1, 0:512], ALU.mult),
                    lambda e: e.tensor_scalar_mul(ROWS[0:1, 1024:1536], PR[0:1, 0:512], 1.0 / D),
                ], reads=[rln, rPB["PR"]], writes=[rln])
                bld.emit("v", lambda e: e.tensor_sub(ROWS[0:1, 1024:1536], ROWS[0:1, 1024:1536],
                                                     ROWS[0:1, 512:1024]),
                         reads=[rln], writes=[rln])
                bld.emit("s", lambda e: e.activation(ROWS[0:1, 512:1024], ROWS[0:1, 1024:1536],
                                                     AF.Ln, bias=CN["eps"][0:1, 1:2]), reads=[rln], writes=[rln])
                rlb = reg(rRB, "ln")
                bld.emit("s", [
                    lambda e: e.activation(RB[0:1, 0:512], ROWS[0:1, 512:1024], AF.Exp, scale=-0.5),
                    lambda e: e.copy(RB[0:1, 512:1024], ROWS[0:1, 0:512]),
                ], reads=[rln], writes=[rlb])
                bld.emit("t", lambda e: e.matmul(PR[:, 0:512], CN["row1"][:, :], RB[0:1, 512:1024],
                                                 start=True, stop=True),
                         reads=[rlb, rCN], writes=[rPB["PR"]])
                bld.emit("v", lambda e: e.tensor_copy(SC[:, S_MR:S_MR + 512], PR[:, 0:512]),
                         reads=[rPB["PR"]], writes=[reg(rSC, "MR")])
                bld.emit("t", lambda e: e.matmul(PS_[:, 0:512], CN["row1"][:, :], RB[0:1, 0:512],
                                                 start=True, stop=True),
                         reads=[rlb, rCN], writes=[rPB["PS"]])
                bld.emit("v", lambda e: e.tensor_copy(SC[:, S_VR:S_VR + 512], PS_[:, 0:512]),
                         reads=[rPB["PS"]], writes=[reg(rSC, "VR")])
                for kc in range(NKC):
                    bld.emit("v", lambda e, kc=kc, l0=l0: e.tensor_tensor(
                        SB[:, kc * TC + l0:kc * TC + l0 + 512],
                        SB[:, kc * TC + l0:kc * TC + l0 + 512], SC[:, S_MR:S_MR + 512], ALU.subtract),
                        reads=[reg(rSC, "MR")], writes=[rSB[kc]])
                    bld.emit("v", lambda e, kc=kc, l0=l0: e.tensor_tensor(
                        SB[:, kc * TC + l0:kc * TC + l0 + 512],
                        SB[:, kc * TC + l0:kc * TC + l0 + 512], SC[:, S_VR:S_VR + 512], ALU.mult),
                        reads=[reg(rSC, "VR")], writes=[rSB[kc]])
            dbg("y1", SB[:, 0:NKC * TC], [r.wtok for r in rSB], regs=list(rSB))

            for tt in range(NTT):
                bld.emit("t", [lambda e, tt=tt, kc=kc: e.transpose(
                    PTB[:, kc * 128:(kc + 1) * 128],
                    SB[:, kc * TC + tt * 128:kc * TC + (tt + 1) * 128],
                    CN["ident"][:, :]) for kc in range(NKC)],
                    reads=list(rSB) + [rCN], writes=[rPB["PTB"]])
                if tt % 2 == 0:
                    bld.emit("v", lambda e, tt=tt: e.tensor_copy(
                        YT[:, tt * 1024:(tt + 1) * 1024], PTB[:, :]),
                        reads=[rPB["PTB"]], writes=[rSBtm[tt]])
                else:
                    bld.emit("s", lambda e, tt=tt: e.copy(
                        YT[:, tt * 1024:(tt + 1) * 1024], PTB[:, :]),
                        reads=[rPB["PTB"]], writes=[rSBtm[tt]])
            rlg = reg(rSF, "LG")
            rpm = reg(rSC, "PERM")
            for lam in range(NLN):
                l0 = lam * 512
                bld.emit("t", [lambda e, kc=kc, l0=l0: e.matmul(
                    PG[0:NES, 0:512], CTL[:, kc * NES:(kc + 1) * NES],
                    SB[:, kc * TC + l0:kc * TC + l0 + 512],
                    start=(kc == 0), stop=(kc == NKC - 1)) for kc in range(NKC)],
                    reads=list(rSB) + [rCN], writes=[rPB["PG"]])
                bld.emit("v", lambda e, l0=l0: e.tensor_tensor(
                    SF[0:NES, F_LG + l0:F_LG + l0 + 512].rearrange("p (g n) -> p g n", n=GRP),
                    PG[0:NES, 0:512].rearrange("p (g n) -> p g n", n=GRP),
                    CN["tie"][:, :].rearrange("p n -> p () n").broadcast_to([NES, 512 // GRP, GRP]),
                    ALU.add),
                    reads=[rPB["PG"], rCN], writes=[rlg])
                bld.emit("v", lambda e, l0=l0: e.tensor_reduce(
                    SF[0:NES, F_GM + l0 // GRP:F_GM + l0 // GRP + 32],
                    SF[0:NES, F_LG + l0:F_LG + l0 + 512].rearrange("p (g n) -> p g n", n=GRP),
                    AX.X, ALU.max), reads=[rlg], writes=[rlg])
                bld.emit("v", lambda e, l0=l0: e.tensor_tensor(
                    SC[0:NES, S_PERM + l0:S_PERM + l0 + 512].rearrange("p (g n) -> p g n", n=GRP),
                    SF[0:NES, F_LG + l0:F_LG + l0 + 512].rearrange("p (g n) -> p g n", n=GRP),
                    SF[0:NES, F_GM + l0 // GRP:F_GM + l0 // GRP + 32]
                    .rearrange("p g -> p g ()").broadcast_to([NES, 32, GRP]),
                    ALU.is_equal), reads=[rlg], writes=[rpm])
            dbg("perm", SC[0:NES, S_PERM:S_PERM + TC], [rpm.wtok], regs=[rpm])
            rpt = reg(rSC, "PERMT")
            for tt in range(NTT):
                bld.emit("t", lambda e, tt=tt: e.transpose(
                    PTB[:, 0:64], SC[0:NES, S_PERM + tt * 128:S_PERM + (tt + 1) * 128],
                    CN["ident"][0:NES, 0:NES]), reads=[rpm, rCN], writes=[rPB["PTB"]])
                bld.emit("v", lambda e, tt=tt: e.tensor_copy(
                    SC[:, S_PERMT + tt * 64:S_PERMT + (tt + 1) * 64], PTB[:, 0:64]),
                    reads=[rPB["PTB"]], writes=[rpt] if tt == NTT - 1 else [])

            pc_ = bld.barrier(extra=dbg_toks)
            for blk in range(NEB):
                buf = blk % 2
                e0 = (blk * EB) // ESL
                pre2 = pc_ if blk < 2 else [bld.tok("t")]
                rf1 = load_w("f1_%d" % buf, lambda c, e0=e0: dW["f1"][c * 128:(c + 1) * 128, e0:e0 + 4, :, :]
                             .rearrange("p e s f -> p (e s f)"), NKC, EB * ESZ, OFF["f1_%d" % buf],
                             extra=pre2)
                rf2 = reg(rWA, "f2_%d" % buf)
                bld.emit("g", lambda e, e0=e0, buf=buf: e.dma_start(
                    out=WA[:, OFF["f2_%d" % buf]:OFF["f2_%d" % buf] + 8192],
                    in_=dW["f2"][e0:e0 + 4, :, :, :].rearrange("e s f d -> (e s) f d")
                    .rearrange("(j two) f d -> (two f) j d", two=2)),
                    writes=[rf2], extra=pre2)

                rsel = reg(rWA, "sel")
                rP = reg(rSC, "P")
                rPT2 = reg(rSC, "PT2")
                sel_fin = []
                for tt in range(NTT):
                    bld.emit("v", [lambda e, tt=tt, blk=blk, parity=parity: e.tensor_tensor(
                        SC[:, S_P + parity * 64:S_P + parity * 64 + 64].rearrange("p (g j) -> p g j", j=8),
                        SC[:, S_PERMT + tt * 64 + blk * 16:S_PERMT + tt * 64 + (blk + 1) * 16]
                        .rearrange("p (j two) -> p j two", two=2)[:, :, parity]
                        .rearrange("p j -> p () j").broadcast_to([128, 8, 8]),
                        CN["bdP"][:, :].rearrange("p g -> p g ()").broadcast_to([128, 8, 8]),
                        ALU.mult) for parity in range(2)],
                        reads=[rpt, rCN], writes=[rP])
                    for half in range(2):
                        pt_ = PA if half == 0 else PB
                        slg = rPB["PA" if half == 0 else "PB"]
                        bld.emit("t", [lambda e, tt=tt, q=q, half=half, pt_=pt_: e.matmul(
                            pt_[:, q * 128:(q + 1) * 128],
                            YT[:, tt * 1024 + (half * 4 + q) * 128:tt * 1024 + (half * 4 + q + 1) * 128],
                            SC[:, S_P:S_P + 128], start=True, stop=True) for q in range(4)],
                            reads=[rSBtm[tt], rP], writes=[slg])
                        eng = "v" if half == 0 else "s"
                        selw = WA[:, OFF["sel"]:OFF["sel"] + 8192].rearrange("p (d n) -> p d n", n=1024)
                        if eng == "v":
                            tk = bld.emit("v", lambda e, tt=tt, half=half, pt_=pt_, selw=selw: e.tensor_copy(
                                selw[:, half * 4:half * 4 + 4, tt * 128:(tt + 1) * 128],
                                pt_[:, :].rearrange("p (d n) -> p d n", n=128)),
                                reads=[slg], extra=rsel.rtoks + [rsel.wtok])
                        else:
                            tk = bld.emit("s", lambda e, tt=tt, half=half, pt_=pt_, selw=selw: e.copy(
                                selw[:, half * 4:half * 4 + 4, tt * 128:(tt + 1) * 128],
                                pt_[:, :].rearrange("p (d n) -> p d n", n=128)),
                                reads=[slg], extra=rsel.rtoks + [rsel.wtok])
                        sel_fin.append(tk)
                rsel.wtok = sel_fin[-1]
                rsel.rtoks = []
                if blk == 0:
                    dbg("sel", WA[:, OFF["sel"]:OFF["sel"] + 8192], sel_fin, regs=[rsel])

                rinter = reg(rWA, "inter")
                selv = WA[:, OFF["sel"]:OFF["sel"] + 8192].rearrange(
                    "p (d t pr g j2) -> p d t pr g j2", d=8, t=8, pr=2, g=8)
                inter_fin = []

                def emit_f1(j):
                    es = blk * EB + j
                    parity, jj = es % 2, j // 2
                    pcb = PC if j % 2 == 0 else PA
                    slf = rPB["PC" if j % 2 == 0 else "PA"]
                    pbase = parity * 64
                    bld.emit("t", [lambda e, kc=kc, j=j, buf=buf, parity=parity, jj=jj, pbase=pbase, pcb=pcb: e.matmul(
                        pcb[pbase:pbase + ESZ, (j % 4) * 128:(j % 4) * 128 + 64],
                        WA[:, OFF["f1_%d" % buf] + kc * 1024 + j * ESZ:
                           OFF["f1_%d" % buf] + kc * 1024 + (j + 1) * ESZ],
                        selv[:, kc, :, parity, :, jj],
                        start=(kc == 0), stop=(kc == NKC - 1)) for kc in range(NKC)],
                        reads=[rsel, rf1], writes=[slf], extra=sel_fin)
                    rin_ = reg(rSC, "INN%d" % (j % 2))
                    bld.emit("s", lambda e, j=j, es=es, pbase=pbase, pcb=pcb: e.activation(
                        SC[pbase:pbase + ESZ, S_INN + (j % 2) * 64:S_INN + (j % 2) * 64 + 64],
                        pcb[pbase:pbase + ESZ, (j % 4) * 128:(j % 4) * 128 + 64],
                        AF.Relu, bias=FBIA[:, es:es + 1]), reads=[slf, rCN], writes=[rin_])

                emit_f1(0)
                for j in range(EB):
                    es = blk * EB + j
                    parity, jj = es % 2, j // 2
                    pbase = parity * 64
                    if j + 1 < EB:
                        emit_f1(j + 1)
                    rin_ = reg(rSC, "INN%d" % (j % 2))
                    for half in range(2):
                        pt_ = PD if half == 0 else PG
                        sli = rPB["PD"] if half == 0 else rPB["PG"]
                        bld.emit("t", lambda e, j=j, buf=buf, parity=parity, jj=jj, half=half, pt_=pt_, pbase=pbase: e.matmul(
                            pt_[0:64, 0:512],
                            SC[pbase:pbase + ESZ, S_INN + (j % 2) * 64:S_INN + (j % 2) * 64 + 64],
                            WA[p64(parity), OFF["f2_%d" % buf] + jj * 1024 + half * 512:
                               OFF["f2_%d" % buf] + jj * 1024 + (half + 1) * 512],
                            start=True, stop=True),
                            reads=[rin_, rf2], writes=[sli])
                        tk = bld.emit("v", lambda e, parity=parity, jj=jj, half=half, pt_=pt_: e.tensor_copy(
                            WA[p64(parity), OFF["inter"] + jj * 1024 + half * 512:
                               OFF["inter"] + jj * 1024 + (half + 1) * 512],
                            pt_[0:64, 0:512]),
                            reads=[sli], extra=rinter.rtoks + [rinter.wtok])
                        inter_fin.append(tk)
                rinter.wtok = inter_fin[-1]
                rinter.rtoks = []
                if blk == 0:
                    dbg("inter", WA[:, OFF["inter"]:OFF["inter"] + 8192], inter_fin, regs=[rinter])

                stg_emitted = [False] * NTT

                def emit_stage(tt2):
                    if tt2 >= NTT or stg_emitted[tt2]:
                        return
                    stg_emitted[tt2] = True
                    sp2 = S_STG if tt2 % 2 == 0 else 1024
                    bld.emit("y", [lambda e, tt2=tt2, parity=parity, sp2=sp2: e.dma_start(
                        out=SC[p64(parity), sp2:sp2 + 1024],
                        in_=WA[parity * 64 + tt2 * 8:parity * 64 + (tt2 + 1) * 8,
                               OFF["inter"]:OFF["inter"] + 8192]
                        .rearrange("p (j d) -> p j d", d=1024)) for parity in range(2)],
                        reads=[rinter], writes=[reg(rSC, "STG%d" % (tt2 % 2))], extra=inter_fin)

                emit_stage(0)
                emit_stage(1)

                def scat_prep(tt2):
                    if tt2 >= NTT:
                        return
                    par2 = tt2 % 2
                    pdst = S_P if par2 == 0 else 0
                    tdst = S_PT2 if par2 == 0 else 128
                    tb2 = PTB if par2 == 0 else PGB
                    tb2n = "PTB" if par2 == 0 else "PG"
                    rP_ = reg(rSC, "P" if par2 == 0 else "P1")
                    bld.emit("v", [lambda e, tt2=tt2, blk=blk, parity=parity, pdst=pdst: e.tensor_tensor(
                        SC[:, pdst + parity * 64:pdst + parity * 64 + 64].rearrange("p (g j) -> p g j", j=8),
                        SC[:, S_PERMT + tt2 * 64 + blk * 16:S_PERMT + tt2 * 64 + (blk + 1) * 16]
                        .rearrange("p (j two) -> p j two", two=2)[:, :, parity]
                        .rearrange("p j -> p () j").broadcast_to([128, 8, 8]),
                        CN["bdP"][:, :].rearrange("p g -> p g ()").broadcast_to([128, 8, 8]),
                        ALU.mult) for parity in range(2)],
                        reads=[rpt, rCN], writes=[rP_])
                    bld.emit("t", lambda e, pdst=pdst, tb2=tb2: e.transpose(
                        tb2[:, 0:128], SC[:, pdst:pdst + 128], CN["ident"][:, :]),
                        reads=[rP_, rCN], writes=[rPB[tb2n]])
                    bld.emit("v", lambda e, tdst=tdst, tb2=tb2: e.tensor_copy(
                        SC[:, tdst:tdst + 128], tb2[:, 0:128]),
                        reads=[rPB[tb2n]],
                        writes=[reg(rSC, "PT2" if par2 == 0 else "PT21")])

                scat_prep(0)
                scat_prep(1)
                for tt in range(NTT):
                    rstg = reg(rSC, "STG%d" % (tt % 2))
                    stoff = S_STG if tt % 2 == 0 else 1024
                    tdst = S_PT2 if tt % 2 == 0 else 128
                    rPT2_ = reg(rSC, "PT2" if tt % 2 == 0 else "PT21")
                    for half in range(2):
                        pt_ = PR if half == 0 else PS_
                        slx = rPB["PR"] if half == 0 else rPB["PS"]
                        bld.emit("t", lambda e, half=half, pt_=pt_, stoff=stoff, tdst=tdst: e.matmul(
                            pt_[:, 0:512], SC[:, tdst:tdst + 128],
                            SC[:, stoff + half * 512:stoff + (half + 1) * 512],
                            start=True, stop=True),
                            reads=[rPT2_, rstg], writes=[slx])
                        if blk == 0:
                            bld.emit("v", lambda e, tt=tt, half=half, pt_=pt_: e.tensor_tensor(
                                SB[:, tt * 1024 + half * 512:tt * 1024 + (half + 1) * 512],
                                YT[:, tt * 1024 + half * 512:tt * 1024 + (half + 1) * 512],
                                pt_[:, 0:512], ALU.add),
                                reads=[slx, rSBtm[tt]], writes=[rSB[tt]])
                        else:
                            bld.emit("v", lambda e, tt=tt, half=half, pt_=pt_: e.tensor_tensor(
                                SB[:, tt * 1024 + half * 512:tt * 1024 + (half + 1) * 512],
                                SB[:, tt * 1024 + half * 512:tt * 1024 + (half + 1) * 512],
                                pt_[:, 0:512], ALU.add),
                                reads=[slx], writes=[rSB[tt]])
                    emit_stage(tt + 2)
                    scat_prep(tt + 2)
            dbg("moeacc", SB[:, 0:NKC * TC], [r.wtok for r in rSB], regs=list(rSB))

            for tt in range(NTT):
                racc = rSB[tt]
                rsq2 = reg(rSC, "SQL2")
                rlr = reg(rROWS, "l2r")
                bld.emit("v", [
                    lambda e, tt=tt: e.reduce_sum(ROWS[:, 0:1], SB[:, tt * 1024:(tt + 1) * 1024], AX.X),
                    lambda e, tt=tt: e.tensor_tensor(SC[:, 0:1024], SB[:, tt * 1024:(tt + 1) * 1024],
                                                     SB[:, tt * 1024:(tt + 1) * 1024], ALU.mult),
                ], reads=[racc], writes=[rsq2, rlr])
                bld.emit("v", lambda e: e.reduce_sum(ROWS[:, 1:2], SC[:, 0:1024], AX.X),
                         reads=[rsq2], writes=[rlr])
                bld.emit("v", lambda e: e.tensor_scalar_mul(ROWS[:, 2:3], ROWS[:, 0:1], 1.0 / D),
                         reads=[rlr], writes=[rlr])
                bld.emit("v", [
                    lambda e: e.tensor_tensor(ROWS[:, 3:4], ROWS[:, 2:3],
                                              ROWS[:, 2:3], ALU.mult),
                    lambda e: e.tensor_scalar_mul(ROWS[:, 4:5], ROWS[:, 1:2], 1.0 / D),
                ], reads=[rlr], writes=[rlr])
                bld.emit("v", lambda e: e.tensor_sub(ROWS[:, 4:5], ROWS[:, 4:5],
                                                     ROWS[:, 3:4]),
                         reads=[rlr], writes=[rlr])
                bld.emit("s", lambda e: e.activation(ROWS[:, 5:6], ROWS[:, 4:5],
                                                     AF.Ln, bias=CN["eps"][:, 1:2]), reads=[rlr], writes=[rlr])
                bld.emit("s", lambda e: e.activation(ROWS[:, 6:7], ROWS[:, 5:6],
                                                     AF.Exp, scale=-0.5), reads=[rlr], writes=[rlr])
                ryo = reg(rSF, "YO%d" % (tt % 2))
                bld.emit("v", lambda e, tt=tt: e.tensor_scalar(
                    SF[:, F_YO + (tt % 2) * 1024:F_YO + (tt % 2) * 1024 + 1024],
                    SB[:, tt * 1024:(tt + 1) * 1024],
                    ROWS[:, 2:3], ROWS[:, 6:7], ALU.subtract, ALU.mult),
                    reads=[racc, rlr], writes=[ryo])
                ot = bld.emit("y", lambda e, tt=tt, seq=seq, t0g=t0g: e.dma_start(
                    out=dy[seq, t0g + tt * 128:t0g + (tt + 1) * 128, :],
                    in_=SF[:, F_YO + (tt % 2) * 1024:F_YO + (tt % 2) * 1024 + 1024]),
                    reads=[ryo])
                out_toks.append(ot)

    bld.waits("y", out_toks + dbg_toks)

    with nc.Block() as block:
        for eng, meth in (("t", block.tensor), ("v", block.vector), ("s", block.scalar),
                          ("g", block.gpsimd), ("y", block.sync)):
            def runner(e, eng=eng):
                for fn in bld.ops[eng]:
                    fn(e)
            meth(runner)
    stack.close()
    return nc


def _host_consts():
    bf = ml_dtypes.bfloat16
    ident = np.eye(128, dtype=np.float32)
    maskT = np.zeros((128, 128), dtype=np.float32)
    for s_ in range(128):
        maskT[s_, s_:] = 1.0
    hsel = np.zeros((128, NKC * 16), dtype=np.float32)
    hrep = np.zeros((16, NKC * 128), dtype=np.float32)
    for hp in range(NKC):
        for p in range(128):
            h = 2 * hp + (1 if p >= 64 else 0)
            hsel[p, hp * 16 + h] = 1.0
            hrep[h, hp * 128 + p] = 1.0
    ones1 = np.ones((128, 1), dtype=np.float32)
    row1 = np.ones((1, 128), dtype=np.float32)
    bdP = np.zeros((128, 8), dtype=np.float32)
    for t_ in range(128):
        bdP[t_, t_ // 16] = 1.0
    maskT2 = np.zeros((128, 64), dtype=np.float32)
    for p_ in range(128):
        maskT2[p_, p_ % 64:] = 1.0
    seye = np.concatenate([np.eye(64, dtype=np.float32)] * 2, axis=1)
    bdm = np.zeros((128, 128), dtype=np.float32)
    bdm[0:64, 0:64] = 1.0
    bdm[64:128, 64:128] = 1.0
    tie = np.zeros((NES, GRP), dtype=np.float32)
    tie[:, :] = np.linspace(0.0, 1e-6, GRP, dtype=np.float32)[None, :]
    c = {"c_ident": ident, "c_maskT": maskT, "c_hsel": hsel, "c_hrep": hrep,
         "c_ones1": ones1, "c_row1": row1, "c_bdP": bdP, "c_maskT2": maskT2,
         "c_seye": seye, "c_bdm": bdm}
    c = {k: v.astype(bf) for k, v in c.items()}
    c["c_tie"] = tie
    eps = np.zeros((128, 3), dtype=np.float32)
    eps[:, 0] = EPS_RMS
    eps[:, 1] = EPS_LN
    eps[:, 2] = 1.0
    c["c_eps"] = eps
    return c


def kernel(L0_attn_x, L0_attn_Wq, L0_attn_Wk, L0_attn_Wv, L0_attn_Wg, L0_attn_Wo,
           L0_attn_w1, L0_attn_w2, L0_attn_w3, L0_attn_g_attn, L0_attn_g_ffn,
           L0_ffn_controller, L0_ffn_f1, L0_ffn_f2, L0_ffn_bias):
    nc = build_nc()
    consts = _host_consts()
    x = np.ascontiguousarray(np.asarray(L0_attn_x, dtype=np.float32))
    w = {"Wq": L0_attn_Wq, "Wk": L0_attn_Wk, "Wv": L0_attn_Wv, "Wg": L0_attn_Wg,
         "Wo": L0_attn_Wo, "w1": L0_attn_w1, "w2": L0_attn_w2, "w3": L0_attn_w3,
         "ga": L0_attn_g_attn, "gf": L0_attn_g_ffn, "ctl": L0_ffn_controller,
         "f1": L0_ffn_f1, "f2": L0_ffn_f2, "fb": L0_ffn_bias}
    w = {k: np.ascontiguousarray(np.asarray(v, dtype=np.float32)) for k, v in w.items()}
    in_maps = []
    for cid in range(NCORE):
        m = {"x": x[cid * SPC:(cid + 1) * SPC]}
        m.update(w)
        m.update(consts)
        in_maps.append(m)
    res = run_bass_kernel_spmd(nc, in_maps, core_ids=list(range(NCORE)))
    return np.concatenate([r["y"] for r in res.results], axis=0)


# version 16
# speedup vs baseline: 1.0200x; 1.0200x over previous
import contextlib
import numpy as np
import ml_dtypes
import concourse.bass as bass
import concourse.mybir as mybir
from concourse.bass_utils import run_bass_kernel_spmd

F32 = mybir.dt.float32
BF16 = mybir.dt.bfloat16
AF = mybir.ActivationFunctionType
ALU = mybir.AluOpType
AX = mybir.AxisListType

B, T, D = 16, 2048, 1024
H, DK, DV = 16, 64, 64
DFF = 2736
NE, ESL, ESZ = 16, 4, 64
NES = NE * ESL
GRP = 16
EPS_RMS, EPS_LN = 1e-6, 1e-5

NCORE = 8
SPC = B // NCORE
TC = 1024
NCH = T // TC
PT = 256
NPT = TC // PT
C = 64
NC_T = PT // C
NKC = D // 128
W3KC = (DFF + 127) // 128
EB = 16
NEB = NES // EB
NTT = TC // 128
NLN = TC // 512

DBG = {"name": None, "shape": None}


class Reg:
    __slots__ = ("wtok", "rtoks")

    def __init__(self):
        self.wtok = None
        self.rtoks = []


class Bld:
    ENGS = ("t", "v", "s", "g", "y")
    NLANES = {"g": 20, "y": 8}

    def __init__(self, nc, stack):
        self.nc = nc
        self.ops = {e: [] for e in self.ENGS}
        self.cnt = {e: 0 for e in self.ENGS}
        self.waited = {e: {} for e in self.ENGS}
        self.sems = {e: stack.enter_context(nc.semaphore("sem_" + e)) for e in ("t", "v", "s", "p")}
        self.cnt["p"] = 0
        self.rr = {"g": 0, "y": 0}
        for e in ("g", "y"):
            for i in range(self.NLANES[e]):
                k = e + str(i)
                self.sems[k] = stack.enter_context(nc.semaphore("sem_" + k))
                self.cnt[k] = 0
                for c in self.waited.values():
                    pass

    def _wait(self, eng, pe, v):
        if self.waited[eng].get(pe, 0) >= v:
            return
        self.waited[eng][pe] = v
        sem = self.sems[pe]
        self.ops[eng].append(lambda e, sem=sem, v=v: e.wait_ge(sem, v))

    def _dma_lane(self, eng):
        i = self.rr[eng] % self.NLANES[eng]
        self.rr[eng] += 1
        return eng + str(i)

    def waits(self, eng, toks):
        best = {}
        for t in toks:
            if t is None:
                continue
            pe, v = t
            if best.get(pe, 0) < v:
                best[pe] = v
        for pe, v in best.items():
            self._wait(eng, pe, v)

    def emit(self, eng, fns, reads=(), writes=(), extra=()):
        if not isinstance(fns, (list, tuple)):
            fns = [fns]
        def _add(n, t):
            if isinstance(t, list):
                n.extend(t)
            elif t is not None:
                n.append(t)
        need = []
        for t_ in extra:
            _add(need, t_)
        for r in reads:
            _add(need, r.wtok)
        for w in writes:
            _add(need, w.wtok)
            need.extend(w.rtoks)
        if eng == "p":
            self.waits("g", need)
            sem = self.sems["p"]
            for i, fn in enumerate(fns):
                if i == len(fns) - 1:
                    self.ops["g"].append(lambda e, fn=fn, sem=sem: fn(e).then_inc(sem, 1))
                else:
                    self.ops["g"].append(fn)
            self.cnt["p"] += 1
            tok = ("p", self.cnt["p"])
            for r in reads:
                r.rtoks.append(tok)
            for w in writes:
                w.wtok = tok
                w.rtoks = []
            return tok
        self.waits(eng, need)
        if eng in ("g", "y"):
            lane = self._dma_lane(eng)
            sem = self.sems[lane]
            self._wait(eng, lane, self.cnt[lane])
            for fn in fns:
                self.ops[eng].append(lambda e, fn=fn, sem=sem: fn(e).then_inc(sem, 16))
            self.cnt[lane] += 16 * len(fns)
            tok = (lane, self.cnt[lane])
        else:
            sem = self.sems[eng]
            for i, fn in enumerate(fns):
                if i == len(fns) - 1:
                    self.ops[eng].append(lambda e, fn=fn, sem=sem: fn(e).then_inc(sem, 1))
                else:
                    self.ops[eng].append(fn)
            self.cnt[eng] += 1
            tok = (eng, self.cnt[eng])
        for r in reads:
            r.rtoks.append(tok)
        for w in writes:
            w.wtok = tok
            w.rtoks = []
        return tok

    def tok(self, eng):
        return (eng, self.cnt[eng])

    def barrier(self, extra=()):
        pb = [self.tok("t"), self.tok("v"), self.tok("s"), self.tok("p")] + list(extra)
        for e in ("t", "v", "s"):
            self.waits(e, pb)
        return pb


def build_nc():
    nc = bass.Bass("TRN2", target_bir_lowering=False)
    stack = contextlib.ExitStack()

    dx = nc.dram_tensor("x", [SPC, T, D], F32, kind="ExternalInput").ap()
    dy = nc.dram_tensor("y", [SPC, T, D], F32, kind="ExternalOutput").ap()
    dW = {}
    for nm, shp in [("Wq", [D, D]), ("Wk", [D, D]), ("Wv", [D, D]), ("Wg", [D, D]),
                    ("Wo", [D, D]), ("w1", [D, DFF]), ("w2", [D, DFF]), ("w3", [DFF, D]),
                    ("ga", [D]), ("gf", [D]), ("ctl", [D, NE, ESL]),
                    ("f1", [D, NE, ESL, ESZ]), ("f2", [NE, ESL, ESZ, D]),
                    ("fb", [NE, ESL, ESZ])]:
        dW[nm] = nc.dram_tensor(nm, shp, F32, kind="ExternalInput").ap()
    dC = {}
    for nm, shp in [("ident", [128, 128]), ("maskT", [128, 128]), ("hsel", [128, NKC * 16]),
                    ("hrep", [16, NKC * 128]), ("ones1", [128, 1]), ("row1", [1, 128]),
                    ("bdP", [128, 8]), ("maskT2", [128, 64]), ("seye", [64, 128]),
                    ("bdm", [128, 128])]:
        dC[nm] = nc.dram_tensor("c_" + nm, shp, BF16, kind="ExternalInput").ap()
    dC["tie"] = nc.dram_tensor("c_tie", [NES, GRP], F32, kind="ExternalInput").ap()
    dC["eps"] = nc.dram_tensor("c_eps", [128, 3], F32, kind="ExternalInput").ap()
    ddbg = None
    if DBG["name"]:
        ddbg = nc.dram_tensor("dbg", DBG["shape"], DBG.get("dtype", F32), kind="ExternalOutput").ap()

    sbt = lambda name, shape, dt: stack.enter_context(nc.sbuf_tensor(name, shape, dt))
    WA = sbt("WA", [128, 66304], BF16)
    SB = sbt("SBs", [128, NKC * TC], BF16)
    YT = sbt("YT", [128, NTT * 1024], BF16)
    SC = sbt("SCb", [128, 4992], BF16)
    SF = sbt("SFf", [128, 3334], F32)
    STT_ = sbt("STATE", [128, NKC * 128], BF16)
    CN = {k: sbt("k" + k, list(dC[k].shape), dC[k].dtype) for k in dC}
    GA = sbt("gacol", [128, NKC], F32)
    GF = sbt("gfcol", [128, NKC], F32)
    FBIA = sbt("fbias", [ESZ, NES], F32)
    CTL = sbt("ctlw", [128, NKC * NES], BF16)
    ROWS = sbt("rows", [128, 1536], F32)
    RB = sbt("rowsb", [128, 1024], BF16)
    XS = [sbt("XS0", [128, 2 * D], BF16), sbt("XS1", [128, 2 * D], BF16)]

    pst = lambda name, shape, dt: stack.enter_context(nc.psum_tensor(name, shape, dt))
    PA = pst("PA", [128, 512], F32)
    PB = pst("PB", [128, 512], F32)
    PC = pst("PC", [128, 512], F32)
    PDB = pst("PDB", [128, 1024], BF16)
    PD = PDB.bitcast(F32)
    PS_ = pst("PS", [128, 512], F32)
    PR = pst("PR", [128, 512], F32)
    PRB = PR.bitcast(BF16)
    PG = pst("PG", [128, 512], F32)
    PGB = PG.bitcast(BF16)
    PTB = pst("PTB", [128, 1024], BF16)

    bld = Bld(nc, stack)

    S_SQ, S_IR, S_TMP, S_SCM, S_VH, S_EQ, S_EK = 0, 1024, 1280, 1536, 2048, 2304, 2816
    S_KTS = 1792
    S_VTS = 2048
    S_SMM = 3328
    S_KBD, S_VBD = 3456, 3968
    S_MR, S_VR = 1024, 1536
    S_PERM, S_PERMT = 2048, 3072
    S_P, S_PT2, S_INN, S_STG = 3584, 3712, 3840, 3968
    F_SP, F_SCUM, F_COLS = 0, 512, 1024
    F_LG, F_GM = 0, 1024
    F_YO = 1216
    F_PUT = 3270

    rWA = {}
    rXS = [Reg(), Reg()]
    rCN = Reg()
    rSB = [Reg() for _ in range(NKC)]
    rSBtm = [Reg() for _ in range(NTT)]
    rYT, rSC, rSF, rROWS, rRB = {}, {}, {}, {}, {}
    rST = Reg()
    rPB = {k: Reg() for k in ("PA", "PB", "PC", "PD", "PS", "PR", "PG", "PTB")}

    def reg(d, key):
        if key not in d:
            d[key] = Reg()
        return d[key]

    for k in dC:
        bld.emit("g", lambda e, a=CN[k], b=dC[k]: e.dma_start(out=a[:, :], in_=b), writes=[rCN])
    bld.emit("g", lambda e: e.dma_start(out=GA[:, :], in_=dW["ga"].rearrange("(c p) -> p c", p=128), allow_slow_non_contiguous=True), writes=[rCN])
    bld.emit("g", lambda e: e.dma_start(out=GF[:, :], in_=dW["gf"].rearrange("(c p) -> p c", p=128), allow_slow_non_contiguous=True), writes=[rCN])
    bld.emit("g", lambda e: e.dma_start(out=FBIA[:, :], in_=dW["fb"].rearrange("e s f -> f (e s)"), allow_slow_non_contiguous=True), writes=[rCN])
    bld.emit("g", lambda e: e.dma_start(out=CTL[:, :], in_=dW["ctl"].rearrange("(c p) e s -> p c (e s)", p=128)), writes=[rCN])

    OFF = {"Wq": 0, "Wk": 8192, "Wv": 16384, "Wg": 24576, "Wo": 32768,
           "w2": 0, "w3": 21888, "w1": 44416,
           "f1_0": 0, "f1_1": 8192, "f2_0": 16384, "f2_1": 24576,
           "sel": 32768, "inter": 49152}
    WCOLS = {"Wq": 1024, "Wk": 1024, "Wv": 1024, "Wg": 1024, "Wo": 1024,
             "w1": DFF, "w2": DFF, "w3": 1024}

    def load_w(name, src_fn, nchunks, ncols, off, extra=(), prows_fn=None):
        r = reg(rWA, name)
        war = (r.wtok if isinstance(r.wtok, list) else [r.wtok]) + r.rtoks
        toks = []
        for c in range(nchunks):
            pr = prows_fn(c) if prows_fn else 128
            toks.append(bld.emit("g", lambda e, c=c, pr=pr: e.dma_start(
                out=WA[0:pr, off + c * ncols: off + (c + 1) * ncols], in_=src_fn(c)),
                extra=(list(extra) + war) if c == 0 else []))
        r.wtok = toks
        r.rtoks = []
        return r

    def wsl(name, kc, col0, ncols, prows=128):
        o = OFF[name] + kc * WCOLS[name] + col0
        return WA[0:prows, o:o + ncols]

    def p64(hh):
        return slice(hh * 64, (hh + 1) * 64)

    dbg_toks = []

    def dbg(name, ap, deps, regs=()):
        if DBG["name"] != name or ddbg is None:
            return
        dbg_toks.append(bld.emit("g", lambda e: e.dma_start(out=ddbg, in_=ap),
                                 reads=list(regs), extra=deps))

    out_toks = []
    first_phase = True

    for seq in range(SPC):
        for ch in range(NCH):
            t0g = ch * TC
            pb = [] if first_phase else bld.barrier(extra=dbg_toks)
            first_phase = False
            if ch == 0:
                bld.emit("v", lambda e: e.memset(STT_[:, :], 0),
                         writes=[rST] + [reg(rYT, "ST%d" % hp) for hp in range(NKC)])
            bld.emit("v", lambda e: e.memset(SC[:, S_KBD:S_KBD + 512], 0),
                     writes=[reg(rSC, "KBD")], extra=pb)

            x2_done = []
            xp_emitted = [False] * NPT

            def emit_xload(ti2, seq2, base2):
                if ti2 >= NPT or xp_emitted[ti2]:
                    return
                xp_emitted[ti2] = True
                par2 = ti2 % 2
                t02 = base2 + ti2 * PT
                bld.emit("g", lambda e, par2=par2, seq2=seq2, t02=t02: e.dma_start(
                    out=XS[par2][:, :],
                    in_=dx[seq2, t02:t02 + PT, :].rearrange("(a p) d -> p a d", p=128)),
                    writes=[rXS[par2]])

            emit_xload(0, seq, t0g)
            emit_xload(1, seq, t0g)
            for wn in ("Wq", "Wk", "Wv", "Wg", "Wo"):
                load_w(wn, lambda c, wn=wn: dW[wn][c * 128:(c + 1) * 128, :], NKC, 1024,
                       OFF[wn], extra=pb)
            for ti in range(NPT):
                par = ti % 2
                t0 = t0g + ti * PT
                tl = ti * PT
                rh = reg(rYT, "A_H")
                rqh, rkh = reg(rYT, "A_QH"), reg(rYT, "A_KH")
                rvf, ro = reg(rYT, "A_VF"), reg(rYT, "A_H")
                A_H, A_QH, A_KH, A_O, A_VF = 0, 2048, 4096, 0, 6144
                for a in range(2):
                    bld.emit("t", [lambda e, a=a, kc=kc, par=par: e.transpose(
                        PTB[:, kc * 128:(kc + 1) * 128],
                        XS[par][:, a * D + kc * 128: a * D + (kc + 1) * 128],
                        CN["ident"][:, :]) for kc in range(NKC)],
                        reads=[rXS[par], rCN], writes=[rPB["PTB"]], extra=pb)
                    bld.emit("v", lambda e, a=a, tl=tl: e.tensor_copy(
                        SB[:, :].rearrange("p (c m) -> p c m", m=TC)[:, :, tl + a * 128:tl + (a + 1) * 128],
                        PTB[:, :].rearrange("p (c n) -> p c n", n=128)),
                        reads=[rPB["PTB"]],
                        writes=[rSB[kc2] for kc2 in range(NKC)] if a == 1 else [], extra=pb)
                emit_xload(ti + 2, seq, t0g)
                for kc in range(NKC):
                    rsq = reg(rSC, "SQ%d" % (kc % 2))
                    bld.emit("v", lambda e, kc=kc, tl=tl: e.tensor_tensor(
                        SC[:, S_SQ + (kc % 2) * 256:S_SQ + (kc % 2) * 256 + 256],
                        SB[:, kc * TC + tl:kc * TC + tl + 256],
                        SB[:, kc * TC + tl:kc * TC + tl + 256], ALU.mult),
                        reads=[rSB[kc]], writes=[rsq])
                    bld.emit("t", lambda e, kc=kc: e.matmul(
                        PS_[0:1, 0:256], CN["ones1"][:, :],
                        SC[:, S_SQ + (kc % 2) * 256:S_SQ + (kc % 2) * 256 + 256],
                        start=(kc == 0), stop=(kc == NKC - 1)),
                        reads=[rsq, rCN], writes=[rPB["PS"]] if kc in (0, NKC - 1) else [])
                rr1 = reg(rROWS, "r1")
                bld.emit("s", lambda e: e.activation(ROWS[0:1, 0:256], PS_[0:1, 0:256], AF.Ln,
                                                     bias=CN["eps"][0:1, 0:1], scale=1.0 / D),
                         reads=[rPB["PS"]], writes=[rr1])
                rb1 = reg(rRB, "r1")
                bld.emit("s", lambda e: e.activation(RB[0:1, 0:256], ROWS[0:1, 0:256], AF.Exp, scale=-0.5),
                         reads=[rr1], writes=[rb1])
                bld.emit("t", lambda e: e.matmul(PR[:, 0:256], CN["row1"][:, :], RB[0:1, 0:256],
                                                 start=True, stop=True),
                         reads=[rb1, rCN], writes=[rPB["PR"]])
                bld.emit("v", lambda e: e.tensor_copy(SC[:, S_IR:S_IR + 256], PR[:, 0:256]),
                         reads=[rPB["PR"]], writes=[reg(rSC, "IR")])
                bld.emit("v", [lambda e, kc=kc, tl=tl: e.scalar_tensor_tensor(
                    YT[:, A_H + kc * 256:A_H + (kc + 1) * 256],
                    SB[:, kc * TC + tl:kc * TC + tl + 256],
                    GA[:, kc:kc + 1], SC[:, S_IR:S_IR + 256], ALU.mult, ALU.mult)
                    for kc in range(NKC)],
                    reads=list(rSB) + [reg(rSC, "IR"), rCN], writes=[rh])
                dbg("h", YT[:, A_H:A_H + 2048], [rh.wtok], regs=[rh])

                accs = [("PA0", PA, 0), ("PA1", PA, 256), ("PB0", PB, 0), ("PB1", PB, 256)]
                na = [0]

                def proj_mm(wn, mc):
                    nm, pt_, c0 = accs[na[0] % 4]
                    na[0] += 1
                    sl = rPB[nm[:2]]
                    bld.emit("t", [lambda e, kc=kc, mc=mc, wn=wn, pt_=pt_, c0=c0: e.matmul(
                        pt_[:, c0:c0 + 256], wsl(wn, kc, mc * 128, 128),
                        YT[:, A_H + kc * 256:A_H + (kc + 1) * 256],
                        start=(kc == 0), stop=(kc == NKC - 1)) for kc in range(NKC)],
                        reads=[rh, reg(rWA, wn)], writes=[sl])
                    return sl, pt_, c0

                sq_toks = {}
                for wn, dst, rdst in (("Wq", A_QH, rqh), ("Wk", A_KH, rkh)):
                    for mc in range(NKC):
                        sl, pt_, c0 = proj_mm(wn, mc)
                        ev = bld.emit("s", lambda e, pt_=pt_, c0=c0, dst=dst, mc=mc: e.copy(
                            YT[:, dst + mc * 256:dst + (mc + 1) * 256], pt_[:, c0:c0 + 256]),
                            reads=[sl], writes=[rdst])
                for mc in range(NKC):
                    sl, pt_, c0 = proj_mm("Wv", mc)
                    bld.emit("s", lambda e, pt_=pt_, c0=c0, mc=mc: e.copy(
                        YT[:, A_VF + mc * 256:A_VF + (mc + 1) * 256], pt_[:, c0:c0 + 256]),
                        reads=[sl], writes=[rvf])
                sstat = rPB["PS"]
                for wn, dst, sc0, rdst in (("Wq", A_QH, 0, rqh), ("Wk", A_KH, 256, rkh)):
                    for mc in range(NKC):
                        off2 = S_TMP if mc % 2 == 0 else S_VH
                        rsqt = reg(rSC, "TMP%d" % (mc % 2))
                        bld.emit("s", lambda e, dst=dst, mc=mc, off2=off2: e.activation(
                            SC[:, off2:off2 + 256],
                            YT[:, dst + mc * 256:dst + (mc + 1) * 256], AF.Square),
                            reads=[rdst], writes=[rsqt])
                        bld.emit("t", lambda e, mc=mc, sc0=sc0, off2=off2: e.matmul(
                            PS_[0:16, sc0:sc0 + 256], CN["hsel"][:, mc * 16:(mc + 1) * 16],
                            SC[:, off2:off2 + 256],
                            start=(mc == 0), stop=(mc == NKC - 1)),
                            reads=[rsqt, rCN],
                            writes=[sstat] if mc in (0, NKC - 1) else [])

                for which, sc0, ro0 in (("q", 0, 256), ("k", 256, 768)):
                    rmx = reg(rROWS, "mx" + which)
                    bld.emit("v", lambda e, sc0=sc0, ro0=ro0: e.tensor_scalar_max(
                        ROWS[0:16, ro0:ro0 + 256], PS_[0:16, sc0:sc0 + 256], 1e-24),
                        reads=[rPB["PS"]], writes=[rmx])
                    bld.emit("s", lambda e, ro0=ro0: e.activation(
                        ROWS[0:16, ro0 + 256:ro0 + 512], ROWS[0:16, ro0:ro0 + 256], AF.Ln),
                        reads=[rmx], writes=[rmx])
                    rrb = reg(rRB, "rq" + which)
                    bld.emit("s", lambda e, ro0=ro0, rb0=(256 if ro0 == 256 else 512): e.activation(
                        RB[0:16, rb0:rb0 + 256], ROWS[0:16, ro0 + 256:ro0 + 512], AF.Exp, scale=-0.5),
                        reads=[rmx], writes=[rrb])

                gcols = []

                def gate_tail(mc):
                    rot = mc % 2
                    rbank = PR if rot == 0 else PS_
                    rslot = rPB["PR" if rot == 0 else "PS"]
                    bld.emit("t", lambda e, mc=mc, rbank=rbank: e.matmul(
                        rbank[:, 0:512], CN["hrep"][:, mc * 128:(mc + 1) * 128],
                        RB[0:16, 256:768], start=True, stop=True),
                        reads=[reg(rRB, "rqq"), reg(rRB, "rqk"), rCN], writes=[rslot])
                    for which, bc0, dst, rdst, eoff, toff, tnm in (
                            ("q", 0, A_QH, rqh, S_EQ, S_TMP, "TMP0"),
                            ("k", 256, A_KH, rkh, S_EK, S_VH, "TMP1")):
                        rexp = reg(rSC, ("EQ%d" if which == "q" else "EK%d") % rot)
                        bld.emit("v", lambda e, rot=rot, eoff=eoff, rbank=rbank, toff=toff, bc0=bc0: e.tensor_tensor(
                            SC[:, toff:toff + 256], rbank[:, bc0:bc0 + 256],
                            SC[:, eoff + rot * 256:eoff + rot * 256 + 256], ALU.mult),
                            reads=[rslot, rexp], writes=[reg(rSC, tnm)])
                        bld.emit("p", lambda e, mc=mc, dst=dst, toff=toff: e.tensor_tensor(
                            YT[:, dst + mc * 256:dst + (mc + 1) * 256],
                            YT[:, dst + mc * 256:dst + (mc + 1) * 256],
                            SC[:, toff:toff + 256], ALU.mult),
                            reads=[reg(rSC, tnm)], writes=[rdst], extra=pb)

                for mc in range(NKC):
                    rot = mc % 2
                    sl, pt_, c0 = proj_mm("Wg", mc)
                    rsp = reg(rSF, "SP%d" % rot)
                    rscu = reg(rSF, "SCUM%d" % rot)
                    bld.emit("v", lambda e, pt_=pt_, c0=c0, rot=rot: e.tensor_scalar(
                        SF[:, F_SP + rot * 256:F_SP + rot * 256 + 256], pt_[:, c0:c0 + 256],
                        -1.0, 40.0, ALU.mult, ALU.min), reads=[sl], writes=[rsp])
                    bld.emit("s", lambda e, rot=rot: e.activation(
                        SF[:, F_SCUM + rot * 256:F_SCUM + rot * 256 + 256],
                        SF[:, F_SP + rot * 256:F_SP + rot * 256 + 256], AF.Exp),
                        reads=[rsp], writes=[rscu])
                    bld.emit("s", lambda e, rot=rot: e.activation(
                        SF[:, F_SP + rot * 256:F_SP + rot * 256 + 256],
                        SF[:, F_SCUM + rot * 256:F_SCUM + rot * 256 + 256], AF.Ln,
                        bias=CN["eps"][0:128, 2:3]),
                        reads=[rscu], writes=[rsp])
                    bld.emit("v", lambda e, rot=rot: e.tensor_tensor_scan(
                        SF[:, F_SCUM + rot * 256:F_SCUM + rot * 256 + 256],
                        SF[:, F_SP + rot * 256:F_SP + rot * 256 + 256],
                        SF[:, F_SP + rot * 256:F_SP + rot * 256 + 256],
                        0.0, ALU.add, ALU.bypass), reads=[rsp], writes=[rscu])
                    base = F_COLS + mc * 24
                    rcol = reg(rSF, "COL%d" % mc)
                    bld.emit("v", [
                        lambda e, rot=rot, base=base: e.tensor_copy(
                            SF[:, base:base + NC_T],
                            SF[:, F_SCUM + rot * 256:F_SCUM + rot * 256 + 256]
                            .rearrange("p (c n) -> p c n", n=C)[:, :, C // 2 - 1]),
                        lambda e, rot=rot, base=base: e.tensor_copy(
                            SF[:, base + 4:base + 8],
                            SF[:, F_SCUM + rot * 256:F_SCUM + rot * 256 + 256]
                            .rearrange("p (c n) -> p c n", n=C)[:, :, C - 1]),
                    ], reads=[rscu], writes=[rcol])
                    bld.emit("v", [
                        lambda e, base=base: e.tensor_scalar_mul(
                            SF[:, base + 8:base + 9], SF[:, base + 4:base + 5], -1.0),
                        lambda e, base=base: e.tensor_sub(
                            SF[:, base + 9:base + 12], SF[:, base + 4:base + 7], SF[:, base + 5:base + 8]),
                    ], reads=[rcol], writes=[rcol])
                    bld.emit("s", [
                        lambda e, base=base: e.activation(SF[:, base + 12:base + 16],
                                                          SF[:, base + 8:base + 12], AF.Exp),
                        lambda e, base=base: e.activation(SF[:, base + 16:base + 20],
                                                          SF[:, base:base + 4], AF.Exp, scale=-1.0),
                    ], reads=[rcol], writes=[rcol])
                    bld.emit("v", lambda e, base=base: e.tensor_sub(
                        SF[:, base + 8:base + 12], SF[:, base:base + 4], SF[:, base + 4:base + 8]),
                        reads=[rcol], writes=[rcol])
                    bld.emit("s", lambda e, base=base: e.activation(
                        SF[:, base + 16:base + 20], SF[:, base + 8:base + 12], AF.Exp),
                        reads=[rcol], writes=[rcol])
                    bld.emit("v", [
                        lambda e, base=base: e.tensor_scalar_mul(
                            SF[:, base + 8:base + 9], SF[:, base:base + 1], -1.0),
                        lambda e, base=base: e.tensor_sub(
                            SF[:, base + 9:base + 12], SF[:, base + 4:base + 7], SF[:, base + 1:base + 4]),
                    ], reads=[rcol], writes=[rcol])
                    bld.emit("s", lambda e, base=base: e.activation(
                        SF[:, base + 20:base + 24], SF[:, base + 8:base + 12], AF.Exp),
                        reads=[rcol], writes=[rcol])
                    gcols.append((base, rcol))
                    bld.emit("v", lambda e, rot=rot, base=base: e.tensor_tensor(
                        SF[:, F_SP + rot * 256:F_SP + rot * 256 + 256].rearrange("p (c n) -> p c n", n=C),
                        SF[:, F_SCUM + rot * 256:F_SCUM + rot * 256 + 256].rearrange("p (c n) -> p c n", n=C),
                        SF[:, base:base + NC_T].rearrange("p c -> p c ()").broadcast_to([128, NC_T, C]),
                        ALU.subtract), reads=[rscu, rcol], writes=[rsp])
                    req = reg(rSC, "EQ%d" % rot)
                    rek = reg(rSC, "EK%d" % rot)
                    bld.emit("s", lambda e, rot=rot: e.activation(
                        SC[:, S_EQ + rot * 256:S_EQ + rot * 256 + 256],
                        SF[:, F_SP + rot * 256:F_SP + rot * 256 + 256], AF.Exp, scale=-1.0),
                        reads=[rsp], writes=[req])
                    bld.emit("s", lambda e, rot=rot: e.activation(
                        SC[:, S_EK + rot * 256:S_EK + rot * 256 + 256],
                        SF[:, F_SP + rot * 256:F_SP + rot * 256 + 256], AF.Exp),
                        reads=[rsp], writes=[rek])
                    if mc > 0:
                        gate_tail(mc - 1)
                gate_tail(NKC - 1)
                dbg("qt", YT[:, A_QH:A_QH + 2048], [rqh.wtok], regs=[rqh])
                dbg("kt", YT[:, A_KH:A_KH + 2048], [rkh.wtok], regs=[rkh])

                GP = [dict(kk=4096, kts=S_KTS, vts=S_VTS, vbd=S_VBD, scm=S_SCM,
                           kkn="KK0", ktsn="KTS", vtsn="VTS", vbdn="VBD", scmn="SCM0",
                           tb=PTB, tbn="PTB", scb=PG, scbn="PG", smm=S_SMM, smmn="SMM",
                           pob=PC, pobn="PC", pub=PD, pubn="PD"),
                      dict(kk=4160, kts=4224, vts=4352, vbd=4480, scm=4608,
                           kkn="KK1", ktsn="KTS1", vtsn="VTS1", vbdn="VBD1", scmn="SCM1",
                           tb=PRB, tbn="PR", scb=PS_, scbn="PS", smm=4672, smmn="SMM1",
                           pob=PA, pobn="PA", pub=PB, pubn="PB")]
                for hp in range(NKC):
                    base, rcol = gcols[hp]
                    rkbd = reg(rSC, "KBD")
                    bld.emit("p", [lambda e, hp=hp, hh=hh: e.tensor_copy(
                        SC[p64(hh), S_KBD:S_KBD + 512].rearrange("p (c n) -> p c n", n=128)[:, :, hh * 64:(hh + 1) * 64],
                        YT[p64(hh), A_KH + hp * 256:A_KH + (hp + 1) * 256].rearrange("p (c n) -> p c n", n=64))
                        for hh in range(2)],
                        reads=[rkh], writes=[rkbd], extra=pb)

                    def gla_tr(cc):
                        g = GP[cc % 2]
                        rtb = rPB[g["tbn"]]
                        bld.emit("t", [
                            lambda e, hp=hp, cc=cc, g=g: e.transpose(
                                g["tb"][0:C, 0:128],
                                YT[:, A_KH + hp * 256 + cc * C:A_KH + hp * 256 + (cc + 1) * C],
                                CN["ident"][:, :]),
                            lambda e, hp=hp, cc=cc, g=g: e.transpose(
                                g["tb"][0:C, 128:256],
                                YT[:, A_VF + hp * 256 + cc * C:A_VF + hp * 256 + (cc + 1) * C],
                                CN["ident"][:, :]),
                        ], reads=[rkh, rvf, rCN], writes=[rtb])
                        rkts = reg(rSC, g["ktsn"])
                        rvts = reg(rSC, g["vtsn"])
                        bld.emit("s", [
                            lambda e, g=g: e.copy(SC[0:C, g["kts"]:g["kts"] + 128], g["tb"][0:C, 0:128]),
                            lambda e, g=g: e.copy(SC[0:C, g["vts"]:g["vts"] + 128], g["tb"][0:C, 128:256]),
                        ], reads=[rtb], writes=[rkts, rvts])
                        return (rkts, rvts)

                    def gla_prep(cc, tr_):
                        g = GP[cc % 2]
                        rkts, rvts = tr_
                        slc = rPB[g["scbn"]]
                        bld.emit("t", lambda e, g=g: e.matmul(
                            g["scb"][:, 256:384], CN["seye"][:, :], SC[0:C, g["vts"]:g["vts"] + 128],
                            start=True, stop=True),
                            reads=[rvts, rCN], writes=[slc])
                        rvbd = reg(rSC, g["vbdn"])
                        bld.emit("v", lambda e, g=g: e.tensor_tensor(
                            SC[:, g["vbd"]:g["vbd"] + 128], g["scb"][:, 256:384], CN["bdm"][:, :], ALU.mult),
                            reads=[slc, rCN], writes=[rvbd])
                        bld.emit("t", lambda e, hp=hp, cc=cc, g=g: e.matmul(
                            g["scb"][:, 0:C],
                            SC[:, S_KBD + cc * 128:S_KBD + (cc + 1) * 128],
                            YT[:, A_QH + hp * 256 + cc * C:A_QH + hp * 256 + (cc + 1) * C],
                            start=True, stop=True),
                            reads=[rkbd, rqh], writes=[slc])
                        rscm = reg(rSC, g["scmn"])
                        bld.emit("v", lambda e, g=g: e.tensor_tensor(
                            SC[:, g["scm"]:g["scm"] + C], g["scb"][:, 0:C],
                            CN["maskT2"][:, :], ALU.mult),
                            reads=[slc, rCN], writes=[rscm])
                        rqg = reg(rSC, g["smmn"])
                        bld.emit("v", lambda e, hp=hp, cc=cc, base=base, g=g: e.tensor_scalar(
                            SC[:, g["smm"]:g["smm"] + C],
                            YT[:, A_QH + hp * 256 + cc * C:A_QH + hp * 256 + (cc + 1) * C],
                            SF[:, base + 20 + cc:base + 21 + cc], None, ALU.mult),
                            reads=[rqh, rcol], writes=[rqg])
                        return (rkts, rvts, rvbd, rscm, rqg)

                    trs = {0: gla_tr(0), 1: gla_tr(1)}
                    pregs = {0: gla_prep(0, trs[0])}
                    for cc in range(NC_T):
                        if cc + 2 < NC_T:
                            trs[cc + 2] = gla_tr(cc + 2)
                        if cc + 1 < NC_T:
                            pregs[cc + 1] = gla_prep(cc + 1, trs[cc + 1])
                        g = GP[cc % 2]
                        rkts, rvts, rvbd, rscm, rqg = pregs[cc]
                        slo = rPB[g["pobn"]]
                        poc0 = (cc & 7) * 64
                        bld.emit("t", [
                            lambda e, g=g, poc0=poc0: e.matmul(
                                g["pob"][:, poc0:poc0 + C], SC[:, g["vbd"]:g["vbd"] + 128],
                                SC[:, g["scm"]:g["scm"] + C], start=True, stop=False),
                            lambda e, hp=hp, g=g, poc0=poc0: e.matmul(
                                g["pob"][:, poc0:poc0 + C], STT_[:, hp * 128:(hp + 1) * 128],
                                SC[:, g["smm"]:g["smm"] + C],
                                start=False, stop=True),
                        ], reads=[rvbd, rscm, rqg, rST], writes=[slo])
                        slu = rPB[g["pubn"]]
                        puc0 = (cc & 7) * 64
                        bld.emit("t", [lambda e, hh=hh, g=g, puc0=puc0: e.matmul(
                            g["pub"][p64(hh), puc0:puc0 + 64],
                            SC[0:C, g["kts"] + hh * 64:g["kts"] + (hh + 1) * 64],
                            SC[0:C, g["vts"] + hh * 64:g["vts"] + (hh + 1) * 64],
                            start=True, stop=True) for hh in range(2)],
                            reads=[rkts, rvts], writes=[slu])
                        bld.emit("s", lambda e, hp=hp, cc=cc, g=g, poc0=poc0: e.copy(
                            YT[:, A_O + hp * 256 + cc * C:A_O + hp * 256 + (cc + 1) * C],
                            g["pob"][:, poc0:poc0 + C]),
                            reads=[slo], writes=[ro])
                        rput = reg(rSF, "PUT")
                        bld.emit("v", lambda e, base=base, cc=cc, g=g, puc0=puc0: e.tensor_scalar(
                            SF[:, F_PUT:F_PUT + 64], g["pub"][:, puc0:puc0 + 64],
                            SF[:, base + 16 + cc:base + 17 + cc], None, ALU.mult),
                            reads=[slu, rcol], writes=[rput])
                        bld.emit("v", [lambda e, hp=hp, cc=cc, base=base, hh=hh:
                                       e.scalar_tensor_tensor(
                            STT_[p64(hh), hp * 128 + hh * 64:hp * 128 + (hh + 1) * 64],
                            STT_[p64(hh), hp * 128 + hh * 64:hp * 128 + (hh + 1) * 64],
                            SF[p64(hh), base + 12 + cc:base + 13 + cc],
                            SF[p64(hh), F_PUT:F_PUT + 64],
                            ALU.mult, ALU.add) for hh in range(2)],
                            reads=[rput, rcol, rST], writes=[rST])
                dbg("o", YT[:, A_O:A_O + 2048], [ro.wtok], regs=[ro])

                for mc in range(NKC):
                    nm, pt_, c0 = accs[na[0] % 4]
                    na[0] += 1
                    sl = rPB[nm[:2]]
                    bld.emit("t", [lambda e, kc=kc, mc=mc, pt_=pt_, c0=c0: e.matmul(
                        pt_[:, c0:c0 + 256], wsl("Wo", kc, mc * 128, 128),
                        YT[:, A_O + kc * 256:A_O + (kc + 1) * 256],
                        start=(kc == 0), stop=(kc == NKC - 1)) for kc in range(NKC)],
                        reads=[ro, reg(rWA, "Wo")], writes=[sl])
                    tkx = bld.emit("v", lambda e, pt_=pt_, c0=c0, mc=mc, tl=tl: e.tensor_tensor(
                        SB[:, mc * TC + tl:mc * TC + tl + PT],
                        SB[:, mc * TC + tl:mc * TC + tl + PT],
                        pt_[:, c0:c0 + 256], ALU.add),
                        reads=[sl], writes=[rSB[mc]])
                    x2_done.append(tkx)
            dbg("x2", SB[:, 0:NKC * TC], x2_done, regs=list(rSB))

            pb = bld.barrier(extra=dbg_toks)
            z_done = []
            xp_emitted = [False] * NPT
            emit_xload(0, seq, t0g)
            emit_xload(1, seq, t0g)
            load_w("w1", lambda c: dW["w1"][c * 128:(c + 1) * 128, :], NKC, DFF, OFF["w1"], extra=pb)
            load_w("w2", lambda c: dW["w2"][c * 128:(c + 1) * 128, :], NKC, DFF, OFF["w2"], extra=pb)
            load_w("w3", lambda c: dW["w3"][c * 128:min(DFF, (c + 1) * 128), :], W3KC, 1024, OFF["w3"],
                   extra=pb, prows_fn=lambda c: min(128, DFF - c * 128))
            for ti in range(NPT):
                par = ti % 2
                t0 = t0g + ti * PT
                tl = ti * PT
                rh2, ru = reg(rYT, "B_H2"), reg(rYT, "B_U")
                B_H2, B_U = 0, 2048
                for kc in range(NKC):
                    rsq = reg(rSC, "SQ%d" % (kc % 2))
                    bld.emit("v", lambda e, kc=kc, tl=tl: e.tensor_tensor(
                        SC[:, S_SQ + (kc % 2) * 256:S_SQ + (kc % 2) * 256 + 256],
                        SB[:, kc * TC + tl:kc * TC + tl + 256],
                        SB[:, kc * TC + tl:kc * TC + tl + 256], ALU.mult),
                        reads=[rSB[kc]], writes=[rsq], extra=pb)
                    bld.emit("t", lambda e, kc=kc: e.matmul(
                        PS_[0:1, 0:256], CN["ones1"][:, :],
                        SC[:, S_SQ + (kc % 2) * 256:S_SQ + (kc % 2) * 256 + 256],
                        start=(kc == 0), stop=(kc == NKC - 1)),
                        reads=[rsq, rCN], writes=[rPB["PS"]] if kc in (0, NKC - 1) else [])
                rr1 = reg(rROWS, "r1")
                bld.emit("s", lambda e: e.activation(ROWS[0:1, 0:256], PS_[0:1, 0:256], AF.Ln,
                                                     bias=CN["eps"][0:1, 0:1], scale=1.0 / D),
                         reads=[rPB["PS"]], writes=[rr1])
                rb1 = reg(rRB, "r1")
                bld.emit("s", lambda e: e.activation(RB[0:1, 0:256], ROWS[0:1, 0:256], AF.Exp, scale=-0.5),
                         reads=[rr1], writes=[rb1])
                bld.emit("t", lambda e: e.matmul(PR[:, 0:256], CN["row1"][:, :], RB[0:1, 0:256],
                                                 start=True, stop=True),
                         reads=[rb1, rCN], writes=[rPB["PR"]])
                bld.emit("v", lambda e: e.tensor_copy(SC[:, S_IR:S_IR + 256], PR[:, 0:256]),
                         reads=[rPB["PR"]], writes=[reg(rSC, "IR")])
                bld.emit("v", [lambda e, kc=kc, tl=tl: e.scalar_tensor_tensor(
                    YT[:, B_H2 + kc * 256:B_H2 + (kc + 1) * 256],
                    SB[:, kc * TC + tl:kc * TC + tl + 256],
                    GF[:, kc:kc + 1], SC[:, S_IR:S_IR + 256], ALU.mult, ALU.mult)
                    for kc in range(NKC)],
                    reads=list(rSB) + [reg(rSC, "IR"), rCN], writes=[rh2])
                for a in range(2):
                    bld.emit("t", [lambda e, a=a, kc=kc, par=par: e.transpose(
                        PTB[:, kc * 128:(kc + 1) * 128],
                        XS[par][:, a * D + kc * 128: a * D + (kc + 1) * 128],
                        CN["ident"][:, :]) for kc in range(NKC)],
                        reads=[rXS[par], rCN, rh2], writes=[rPB["PTB"]], extra=pb)
                    bld.emit("v", lambda e, a=a, tl=tl: e.tensor_tensor(
                        SB[:, :].rearrange("p (c m) -> p c m", m=TC)[:, :, tl + a * 128:tl + (a + 1) * 128],
                        SB[:, :].rearrange("p (c m) -> p c m", m=TC)[:, :, tl + a * 128:tl + (a + 1) * 128],
                        PTB[:, :].rearrange("p (c n) -> p c n", n=128), ALU.add),
                        reads=[rPB["PTB"]], writes=[rSB[kc2] for kc2 in range(NKC)])
                emit_xload(ti + 2, seq, t0g)

                accsB = [("PA0", PA, 0), ("PA1", PA, 256), ("PB0", PB, 0), ("PB1", PB, 256),
                         ("PC0", PC, 0), ("PC1", PC, 256), ("PD0", PD, 0), ("PD1", PD, 256)]
                nb = [0]
                for mc in range(W3KC):
                    rows = min(128, DFF - mc * 128)
                    nm, pt_, c0 = accsB[nb[0] % 8]
                    nb[0] += 1
                    sl = rPB[nm[:2]]
                    bld.emit("t", [lambda e, kc=kc, mc=mc, pt_=pt_, c0=c0, rows=rows: e.matmul(
                        pt_[0:rows, c0:c0 + 256], wsl("w1", kc, mc * 128, rows),
                        YT[:, B_H2 + kc * 256:B_H2 + (kc + 1) * 256],
                        start=(kc == 0), stop=(kc == NKC - 1)) for kc in range(NKC)],
                        reads=[rh2, reg(rWA, "w1")], writes=[sl])
                    bld.emit("s", lambda e, pt_=pt_, c0=c0, mc=mc, rows=rows: e.activation(
                        YT[0:rows, B_U + mc * 256:B_U + (mc + 1) * 256],
                        pt_[0:rows, c0:c0 + 256], AF.Silu),
                        reads=[sl], writes=[ru] if mc == W3KC - 1 else [])
                for mc in range(W3KC):
                    rows = min(128, DFF - mc * 128)
                    nm, pt_, c0 = accsB[nb[0] % 8]
                    nb[0] += 1
                    sl = rPB[nm[:2]]
                    bld.emit("t", [lambda e, kc=kc, mc=mc, pt_=pt_, c0=c0, rows=rows: e.matmul(
                        pt_[0:rows, c0:c0 + 256], wsl("w2", kc, mc * 128, rows),
                        YT[:, B_H2 + kc * 256:B_H2 + (kc + 1) * 256],
                        start=(kc == 0), stop=(kc == NKC - 1)) for kc in range(NKC)],
                        reads=[rh2, reg(rWA, "w2")], writes=[sl])
                    bld.emit("v", lambda e, pt_=pt_, c0=c0, mc=mc, rows=rows: e.tensor_tensor(
                        YT[0:rows, B_U + mc * 256:B_U + (mc + 1) * 256],
                        YT[0:rows, B_U + mc * 256:B_U + (mc + 1) * 256],
                        pt_[0:rows, c0:c0 + 256], ALU.mult),
                        reads=[sl, ru], writes=[ru] if mc == W3KC - 1 else [])
                for mc in range(NKC):
                    nm, pt_, c0 = accsB[nb[0] % 8]
                    nb[0] += 1
                    sl = rPB[nm[:2]]
                    bld.emit("t", [lambda e, kc=kc, mc=mc, pt_=pt_, c0=c0,
                                   rows=min(128, DFF - kc * 128): e.matmul(
                        pt_[:, c0:c0 + 256], wsl("w3", kc, mc * 128, 128, prows=rows),
                        YT[0:rows, B_U + kc * 256:B_U + (kc + 1) * 256],
                        start=(kc == 0), stop=(kc == W3KC - 1)) for kc in range(W3KC)],
                        reads=[ru, reg(rWA, "w3")], writes=[sl])
                    tkz = bld.emit("v", lambda e, pt_=pt_, c0=c0, mc=mc, tl=tl: e.tensor_tensor(
                        SB[:, mc * TC + tl:mc * TC + tl + 256],
                        SB[:, mc * TC + tl:mc * TC + tl + 256], pt_[:, c0:c0 + 256], ALU.add),
                        reads=[sl], writes=[rSB[mc]])
                    z_done.append(tkz)
            dbg("z", SB[:, 0:NKC * TC], z_done, regs=list(rSB))

            pb = bld.barrier(extra=dbg_toks)
            for lam in range(NLN):
                l0 = lam * 512
                bld.emit("v", lambda e, l0=l0: e.tensor_tensor(
                    SC[:, 0:4096].rearrange("p (c n) -> p c n", n=512),
                    SB[:, :].rearrange("p (c m) -> p c m", m=TC)[:, :, l0:l0 + 512],
                    SB[:, :].rearrange("p (c m) -> p c m", m=TC)[:, :, l0:l0 + 512], ALU.mult),
                    reads=list(rSB), writes=[reg(rSC, "SQ4")], extra=pb)
                bld.emit("t", [lambda e, kc=kc, l0=l0: e.matmul(
                    PS_[0:1, 0:512], CN["ones1"][:, :], SB[:, kc * TC + l0:kc * TC + l0 + 512],
                    start=(kc == 0), stop=(kc == NKC - 1)) for kc in range(NKC)],
                    reads=list(rSB) + [rCN], writes=[rPB["PS"]], extra=pb)
                bld.emit("t", [lambda e, kc=kc: e.matmul(
                    PR[0:1, 0:512], CN["ones1"][:, :], SC[:, kc * 512:(kc + 1) * 512],
                    start=(kc == 0), stop=(kc == NKC - 1)) for kc in range(NKC)],
                    reads=[reg(rSC, "SQ4"), rCN], writes=[rPB["PR"]])
                rln = reg(rROWS, "ln")
                bld.emit("v", lambda e: e.tensor_scalar_mul(ROWS[0:1, 0:512], PS_[0:1, 0:512], 1.0 / D),
                         reads=[rPB["PS"]], writes=[rln])
                bld.emit("v", [
                    lambda e: e.tensor_tensor(ROWS[0:1, 512:1024], ROWS[0:1, 0:512],
                                              ROWS[0:1, 0:512], ALU.mult),
                    lambda e: e.tensor_scalar_mul(ROWS[0:1, 1024:1536], PR[0:1, 0:512], 1.0 / D),
                ], reads=[rln, rPB["PR"]], writes=[rln])
                bld.emit("v", lambda e: e.tensor_sub(ROWS[0:1, 1024:1536], ROWS[0:1, 1024:1536],
                                                     ROWS[0:1, 512:1024]),
                         reads=[rln], writes=[rln])
                bld.emit("s", lambda e: e.activation(ROWS[0:1, 512:1024], ROWS[0:1, 1024:1536],
                                                     AF.Ln, bias=CN["eps"][0:1, 1:2]), reads=[rln], writes=[rln])
                rlb = reg(rRB, "ln")
                bld.emit("s", [
                    lambda e: e.activation(RB[0:1, 0:512], ROWS[0:1, 512:1024], AF.Exp, scale=-0.5),
                    lambda e: e.copy(RB[0:1, 512:1024], ROWS[0:1, 0:512]),
                ], reads=[rln], writes=[rlb])
                bld.emit("t", lambda e: e.matmul(PR[:, 0:512], CN["row1"][:, :], RB[0:1, 512:1024],
                                                 start=True, stop=True),
                         reads=[rlb, rCN], writes=[rPB["PR"]])
                bld.emit("v", lambda e: e.tensor_copy(SC[:, S_MR:S_MR + 512], PR[:, 0:512]),
                         reads=[rPB["PR"]], writes=[reg(rSC, "MR")])
                bld.emit("t", lambda e: e.matmul(PS_[:, 0:512], CN["row1"][:, :], RB[0:1, 0:512],
                                                 start=True, stop=True),
                         reads=[rlb, rCN], writes=[rPB["PS"]])
                bld.emit("v", lambda e: e.tensor_copy(SC[:, S_VR:S_VR + 512], PS_[:, 0:512]),
                         reads=[rPB["PS"]], writes=[reg(rSC, "VR")])
                for kc in range(NKC):
                    bld.emit("v", lambda e, kc=kc, l0=l0: e.tensor_tensor(
                        SB[:, kc * TC + l0:kc * TC + l0 + 512],
                        SB[:, kc * TC + l0:kc * TC + l0 + 512], SC[:, S_MR:S_MR + 512], ALU.subtract),
                        reads=[reg(rSC, "MR")], writes=[rSB[kc]])
                    bld.emit("v", lambda e, kc=kc, l0=l0: e.tensor_tensor(
                        SB[:, kc * TC + l0:kc * TC + l0 + 512],
                        SB[:, kc * TC + l0:kc * TC + l0 + 512], SC[:, S_VR:S_VR + 512], ALU.mult),
                        reads=[reg(rSC, "VR")], writes=[rSB[kc]])
            dbg("y1", SB[:, 0:NKC * TC], [r.wtok for r in rSB], regs=list(rSB))

            for tt in range(NTT):
                bld.emit("t", [lambda e, tt=tt, kc=kc: e.transpose(
                    PTB[:, kc * 128:(kc + 1) * 128],
                    SB[:, kc * TC + tt * 128:kc * TC + (tt + 1) * 128],
                    CN["ident"][:, :]) for kc in range(NKC)],
                    reads=list(rSB) + [rCN], writes=[rPB["PTB"]])
                if tt % 2 == 0:
                    bld.emit("v", lambda e, tt=tt: e.tensor_copy(
                        YT[:, tt * 1024:(tt + 1) * 1024], PTB[:, :]),
                        reads=[rPB["PTB"]], writes=[rSBtm[tt]])
                else:
                    bld.emit("s", lambda e, tt=tt: e.copy(
                        YT[:, tt * 1024:(tt + 1) * 1024], PTB[:, :]),
                        reads=[rPB["PTB"]], writes=[rSBtm[tt]])
            rlg = reg(rSF, "LG")
            rpm = reg(rSC, "PERM")
            for lam in range(NLN):
                l0 = lam * 512
                bld.emit("t", [lambda e, kc=kc, l0=l0: e.matmul(
                    PG[0:NES, 0:512], CTL[:, kc * NES:(kc + 1) * NES],
                    SB[:, kc * TC + l0:kc * TC + l0 + 512],
                    start=(kc == 0), stop=(kc == NKC - 1)) for kc in range(NKC)],
                    reads=list(rSB) + [rCN], writes=[rPB["PG"]])
                bld.emit("v", lambda e, l0=l0: e.tensor_tensor(
                    SF[0:NES, F_LG + l0:F_LG + l0 + 512].rearrange("p (g n) -> p g n", n=GRP),
                    PG[0:NES, 0:512].rearrange("p (g n) -> p g n", n=GRP),
                    CN["tie"][:, :].rearrange("p n -> p () n").broadcast_to([NES, 512 // GRP, GRP]),
                    ALU.add),
                    reads=[rPB["PG"], rCN], writes=[rlg])
                bld.emit("v", lambda e, l0=l0: e.tensor_reduce(
                    SF[0:NES, F_GM + l0 // GRP:F_GM + l0 // GRP + 32],
                    SF[0:NES, F_LG + l0:F_LG + l0 + 512].rearrange("p (g n) -> p g n", n=GRP),
                    AX.X, ALU.max), reads=[rlg], writes=[rlg])
                bld.emit("v", lambda e, l0=l0: e.tensor_tensor(
                    SC[0:NES, S_PERM + l0:S_PERM + l0 + 512].rearrange("p (g n) -> p g n", n=GRP),
                    SF[0:NES, F_LG + l0:F_LG + l0 + 512].rearrange("p (g n) -> p g n", n=GRP),
                    SF[0:NES, F_GM + l0 // GRP:F_GM + l0 // GRP + 32]
                    .rearrange("p g -> p g ()").broadcast_to([NES, 32, GRP]),
                    ALU.is_equal), reads=[rlg], writes=[rpm])
            dbg("perm", SC[0:NES, S_PERM:S_PERM + TC], [rpm.wtok], regs=[rpm])
            rpt = reg(rSC, "PERMT")
            for tt in range(NTT):
                bld.emit("t", lambda e, tt=tt: e.transpose(
                    PTB[:, 0:64], SC[0:NES, S_PERM + tt * 128:S_PERM + (tt + 1) * 128],
                    CN["ident"][0:NES, 0:NES]), reads=[rpm, rCN], writes=[rPB["PTB"]])
                bld.emit("v", lambda e, tt=tt: e.tensor_copy(
                    SC[:, S_PERMT + tt * 64:S_PERMT + (tt + 1) * 64], PTB[:, 0:64]),
                    reads=[rPB["PTB"]], writes=[rpt] if tt == NTT - 1 else [])

            pc_ = bld.barrier(extra=dbg_toks)
            for blk in range(NEB):
                buf = blk % 2
                e0 = (blk * EB) // ESL
                pre2 = pc_ if blk < 2 else [bld.tok("t")]
                rf1 = load_w("f1_%d" % buf, lambda c, e0=e0: dW["f1"][c * 128:(c + 1) * 128, e0:e0 + 4, :, :]
                             .rearrange("p e s f -> p (e s f)"), NKC, EB * ESZ, OFF["f1_%d" % buf],
                             extra=pre2)
                rf2 = reg(rWA, "f2_%d" % buf)
                bld.emit("g", lambda e, e0=e0, buf=buf: e.dma_start(
                    out=WA[:, OFF["f2_%d" % buf]:OFF["f2_%d" % buf] + 8192],
                    in_=dW["f2"][e0:e0 + 4, :, :, :].rearrange("e s f d -> (e s) f d")
                    .rearrange("(j two) f d -> (two f) j d", two=2)),
                    writes=[rf2], extra=pre2)

                rsel = reg(rWA, "sel")
                rP = reg(rSC, "P")
                rPT2 = reg(rSC, "PT2")
                sel_fin = []
                for tt in range(NTT):
                    bld.emit("v", [lambda e, tt=tt, blk=blk, parity=parity: e.tensor_tensor(
                        SC[:, S_P + parity * 64:S_P + parity * 64 + 64].rearrange("p (g j) -> p g j", j=8),
                        SC[:, S_PERMT + tt * 64 + blk * 16:S_PERMT + tt * 64 + (blk + 1) * 16]
                        .rearrange("p (j two) -> p j two", two=2)[:, :, parity]
                        .rearrange("p j -> p () j").broadcast_to([128, 8, 8]),
                        CN["bdP"][:, :].rearrange("p g -> p g ()").broadcast_to([128, 8, 8]),
                        ALU.mult) for parity in range(2)],
                        reads=[rpt, rCN], writes=[rP])
                    for half in range(2):
                        pt_ = PA if half == 0 else PB
                        slg = rPB["PA" if half == 0 else "PB"]
                        bld.emit("t", [lambda e, tt=tt, q=q, half=half, pt_=pt_: e.matmul(
                            pt_[:, q * 128:(q + 1) * 128],
                            YT[:, tt * 1024 + (half * 4 + q) * 128:tt * 1024 + (half * 4 + q + 1) * 128],
                            SC[:, S_P:S_P + 128], start=True, stop=True) for q in range(4)],
                            reads=[rSBtm[tt], rP], writes=[slg])
                        eng = "v" if half == 0 else "s"
                        selw = WA[:, OFF["sel"]:OFF["sel"] + 8192].rearrange("p (d n) -> p d n", n=1024)
                        if eng == "v":
                            tk = bld.emit("v", lambda e, tt=tt, half=half, pt_=pt_, selw=selw: e.tensor_copy(
                                selw[:, half * 4:half * 4 + 4, tt * 128:(tt + 1) * 128],
                                pt_[:, :].rearrange("p (d n) -> p d n", n=128)),
                                reads=[slg], extra=rsel.rtoks + [rsel.wtok])
                        else:
                            tk = bld.emit("s", lambda e, tt=tt, half=half, pt_=pt_, selw=selw: e.copy(
                                selw[:, half * 4:half * 4 + 4, tt * 128:(tt + 1) * 128],
                                pt_[:, :].rearrange("p (d n) -> p d n", n=128)),
                                reads=[slg], extra=rsel.rtoks + [rsel.wtok])
                        sel_fin.append(tk)
                rsel.wtok = sel_fin[-1]
                rsel.rtoks = []
                if blk == 0:
                    dbg("sel", WA[:, OFF["sel"]:OFF["sel"] + 8192], sel_fin, regs=[rsel])

                rinter = reg(rWA, "inter")
                selv = WA[:, OFF["sel"]:OFF["sel"] + 8192].rearrange(
                    "p (d t pr g j2) -> p d t pr g j2", d=8, t=8, pr=2, g=8)
                inter_fin = []

                def emit_f1(j):
                    es = blk * EB + j
                    parity, jj = es % 2, j // 2
                    pcb = PC if j % 2 == 0 else PA
                    slf = rPB["PC" if j % 2 == 0 else "PA"]
                    pbase = parity * 64
                    bld.emit("t", [lambda e, kc=kc, j=j, buf=buf, parity=parity, jj=jj, pbase=pbase, pcb=pcb: e.matmul(
                        pcb[pbase:pbase + ESZ, (j % 4) * 128:(j % 4) * 128 + 64],
                        WA[:, OFF["f1_%d" % buf] + kc * 1024 + j * ESZ:
                           OFF["f1_%d" % buf] + kc * 1024 + (j + 1) * ESZ],
                        selv[:, kc, :, parity, :, jj],
                        start=(kc == 0), stop=(kc == NKC - 1)) for kc in range(NKC)],
                        reads=[rsel, rf1], writes=[slf], extra=sel_fin)
                    rin_ = reg(rSC, "INN%d" % (j % 2))
                    bld.emit("s", lambda e, j=j, es=es, pbase=pbase, pcb=pcb: e.activation(
                        SC[pbase:pbase + ESZ, S_INN + (j % 2) * 64:S_INN + (j % 2) * 64 + 64],
                        pcb[pbase:pbase + ESZ, (j % 4) * 128:(j % 4) * 128 + 64],
                        AF.Relu, bias=FBIA[:, es:es + 1]), reads=[slf, rCN], writes=[rin_])

                emit_f1(0)
                for j in range(EB):
                    es = blk * EB + j
                    parity, jj = es % 2, j // 2
                    pbase = parity * 64
                    if j + 1 < EB:
                        emit_f1(j + 1)
                    rin_ = reg(rSC, "INN%d" % (j % 2))
                    for half in range(2):
                        pt_ = PD if half == 0 else PG
                        sli = rPB["PD"] if half == 0 else rPB["PG"]
                        bld.emit("t", lambda e, j=j, buf=buf, parity=parity, jj=jj, half=half, pt_=pt_, pbase=pbase: e.matmul(
                            pt_[0:64, 0:512],
                            SC[pbase:pbase + ESZ, S_INN + (j % 2) * 64:S_INN + (j % 2) * 64 + 64],
                            WA[p64(parity), OFF["f2_%d" % buf] + jj * 1024 + half * 512:
                               OFF["f2_%d" % buf] + jj * 1024 + (half + 1) * 512],
                            start=True, stop=True),
                            reads=[rin_, rf2], writes=[sli])
                        tk = bld.emit("v", lambda e, parity=parity, jj=jj, half=half, pt_=pt_: e.tensor_copy(
                            WA[p64(parity), OFF["inter"] + jj * 1024 + half * 512:
                               OFF["inter"] + jj * 1024 + (half + 1) * 512],
                            pt_[0:64, 0:512]),
                            reads=[sli], extra=rinter.rtoks + [rinter.wtok])
                        inter_fin.append(tk)
                rinter.wtok = inter_fin[-1]
                rinter.rtoks = []
                if blk == 0:
                    dbg("inter", WA[:, OFF["inter"]:OFF["inter"] + 8192], inter_fin, regs=[rinter])

                stg_emitted = [False] * NTT

                def emit_stage(tt2):
                    if tt2 >= NTT or stg_emitted[tt2]:
                        return
                    stg_emitted[tt2] = True
                    sp2 = S_STG if tt2 % 2 == 0 else 1024
                    bld.emit("y", [lambda e, tt2=tt2, parity=parity, sp2=sp2: e.dma_start(
                        out=SC[p64(parity), sp2:sp2 + 1024],
                        in_=WA[parity * 64 + tt2 * 8:parity * 64 + (tt2 + 1) * 8,
                               OFF["inter"]:OFF["inter"] + 8192]
                        .rearrange("p (j d) -> p j d", d=1024)) for parity in range(2)],
                        reads=[rinter], writes=[reg(rSC, "STG%d" % (tt2 % 2))], extra=inter_fin)

                emit_stage(0)
                emit_stage(1)

                def scat_prep(tt2):
                    if tt2 >= NTT:
                        return
                    par2 = tt2 % 2
                    pdst = S_P if par2 == 0 else 0
                    tdst = S_PT2 if par2 == 0 else 128
                    tb2 = PTB if par2 == 0 else PGB
                    tb2n = "PTB" if par2 == 0 else "PG"
                    rP_ = reg(rSC, "P" if par2 == 0 else "P1")
                    bld.emit("v", [lambda e, tt2=tt2, blk=blk, parity=parity, pdst=pdst: e.tensor_tensor(
                        SC[:, pdst + parity * 64:pdst + parity * 64 + 64].rearrange("p (g j) -> p g j", j=8),
                        SC[:, S_PERMT + tt2 * 64 + blk * 16:S_PERMT + tt2 * 64 + (blk + 1) * 16]
                        .rearrange("p (j two) -> p j two", two=2)[:, :, parity]
                        .rearrange("p j -> p () j").broadcast_to([128, 8, 8]),
                        CN["bdP"][:, :].rearrange("p g -> p g ()").broadcast_to([128, 8, 8]),
                        ALU.mult) for parity in range(2)],
                        reads=[rpt, rCN], writes=[rP_], extra=pc_)
                    bld.emit("t", lambda e, pdst=pdst, tb2=tb2: e.transpose(
                        tb2[:, 0:128], SC[:, pdst:pdst + 128], CN["ident"][:, :]),
                        reads=[rP_, rCN], writes=[rPB[tb2n]])
                    bld.emit("v", lambda e, tdst=tdst, tb2=tb2: e.tensor_copy(
                        SC[:, tdst:tdst + 128], tb2[:, 0:128]),
                        reads=[rPB[tb2n]],
                        writes=[reg(rSC, "PT2" if par2 == 0 else "PT21")])

                scat_prep(0)
                scat_prep(1)
                for tt in range(NTT):
                    rstg = reg(rSC, "STG%d" % (tt % 2))
                    stoff = S_STG if tt % 2 == 0 else 1024
                    tdst = S_PT2 if tt % 2 == 0 else 128
                    rPT2_ = reg(rSC, "PT2" if tt % 2 == 0 else "PT21")
                    for half in range(2):
                        pt_ = PR if half == 0 else PS_
                        slx = rPB["PR"] if half == 0 else rPB["PS"]
                        bld.emit("t", lambda e, half=half, pt_=pt_, stoff=stoff, tdst=tdst: e.matmul(
                            pt_[:, 0:512], SC[:, tdst:tdst + 128],
                            SC[:, stoff + half * 512:stoff + (half + 1) * 512],
                            start=True, stop=True),
                            reads=[rPT2_, rstg], writes=[slx])
                        if blk == 0:
                            bld.emit("v", lambda e, tt=tt, half=half, pt_=pt_: e.tensor_tensor(
                                SB[:, tt * 1024 + half * 512:tt * 1024 + (half + 1) * 512],
                                YT[:, tt * 1024 + half * 512:tt * 1024 + (half + 1) * 512],
                                pt_[:, 0:512], ALU.add),
                                reads=[slx, rSBtm[tt]], writes=[rSB[tt]])
                        else:
                            bld.emit("v", lambda e, tt=tt, half=half, pt_=pt_: e.tensor_tensor(
                                SB[:, tt * 1024 + half * 512:tt * 1024 + (half + 1) * 512],
                                SB[:, tt * 1024 + half * 512:tt * 1024 + (half + 1) * 512],
                                pt_[:, 0:512], ALU.add),
                                reads=[slx], writes=[rSB[tt]])
                    emit_stage(tt + 2)
                    scat_prep(tt + 2)
            dbg("moeacc", SB[:, 0:NKC * TC], [r.wtok for r in rSB], regs=list(rSB))

            for tt in range(NTT):
                racc = rSB[tt]
                rsq2 = reg(rSC, "SQL2")
                rlr = reg(rROWS, "l2r")
                bld.emit("v", [
                    lambda e, tt=tt: e.reduce_sum(ROWS[:, 0:1], SB[:, tt * 1024:(tt + 1) * 1024], AX.X),
                    lambda e, tt=tt: e.tensor_tensor(SC[:, 0:1024], SB[:, tt * 1024:(tt + 1) * 1024],
                                                     SB[:, tt * 1024:(tt + 1) * 1024], ALU.mult),
                ], reads=[racc], writes=[rsq2, rlr])
                bld.emit("v", lambda e: e.reduce_sum(ROWS[:, 1:2], SC[:, 0:1024], AX.X),
                         reads=[rsq2], writes=[rlr])
                bld.emit("v", lambda e: e.tensor_scalar_mul(ROWS[:, 2:3], ROWS[:, 0:1], 1.0 / D),
                         reads=[rlr], writes=[rlr])
                bld.emit("v", [
                    lambda e: e.tensor_tensor(ROWS[:, 3:4], ROWS[:, 2:3],
                                              ROWS[:, 2:3], ALU.mult),
                    lambda e: e.tensor_scalar_mul(ROWS[:, 4:5], ROWS[:, 1:2], 1.0 / D),
                ], reads=[rlr], writes=[rlr])
                bld.emit("v", lambda e: e.tensor_sub(ROWS[:, 4:5], ROWS[:, 4:5],
                                                     ROWS[:, 3:4]),
                         reads=[rlr], writes=[rlr])
                bld.emit("s", lambda e: e.activation(ROWS[:, 5:6], ROWS[:, 4:5],
                                                     AF.Ln, bias=CN["eps"][:, 1:2]), reads=[rlr], writes=[rlr])
                bld.emit("s", lambda e: e.activation(ROWS[:, 6:7], ROWS[:, 5:6],
                                                     AF.Exp, scale=-0.5), reads=[rlr], writes=[rlr])
                ryo = reg(rSF, "YO%d" % (tt % 2))
                bld.emit("v", lambda e, tt=tt: e.tensor_scalar(
                    SF[:, F_YO + (tt % 2) * 1024:F_YO + (tt % 2) * 1024 + 1024],
                    SB[:, tt * 1024:(tt + 1) * 1024],
                    ROWS[:, 2:3], ROWS[:, 6:7], ALU.subtract, ALU.mult),
                    reads=[racc, rlr], writes=[ryo])
                ot = bld.emit("y", lambda e, tt=tt, seq=seq, t0g=t0g: e.dma_start(
                    out=dy[seq, t0g + tt * 128:t0g + (tt + 1) * 128, :],
                    in_=SF[:, F_YO + (tt % 2) * 1024:F_YO + (tt % 2) * 1024 + 1024]),
                    reads=[ryo])
                out_toks.append(ot)

    bld.waits("y", out_toks + dbg_toks)

    with nc.Block() as block:
        for eng, meth in (("t", block.tensor), ("v", block.vector), ("s", block.scalar),
                          ("g", block.gpsimd), ("y", block.sync)):
            def runner(e, eng=eng):
                for fn in bld.ops[eng]:
                    fn(e)
            meth(runner)
    stack.close()
    return nc


def _host_consts():
    bf = ml_dtypes.bfloat16
    ident = np.eye(128, dtype=np.float32)
    maskT = np.zeros((128, 128), dtype=np.float32)
    for s_ in range(128):
        maskT[s_, s_:] = 1.0
    hsel = np.zeros((128, NKC * 16), dtype=np.float32)
    hrep = np.zeros((16, NKC * 128), dtype=np.float32)
    for hp in range(NKC):
        for p in range(128):
            h = 2 * hp + (1 if p >= 64 else 0)
            hsel[p, hp * 16 + h] = 1.0
            hrep[h, hp * 128 + p] = 1.0
    ones1 = np.ones((128, 1), dtype=np.float32)
    row1 = np.ones((1, 128), dtype=np.float32)
    bdP = np.zeros((128, 8), dtype=np.float32)
    for t_ in range(128):
        bdP[t_, t_ // 16] = 1.0
    maskT2 = np.zeros((128, 64), dtype=np.float32)
    for p_ in range(128):
        maskT2[p_, p_ % 64:] = 1.0
    seye = np.concatenate([np.eye(64, dtype=np.float32)] * 2, axis=1)
    bdm = np.zeros((128, 128), dtype=np.float32)
    bdm[0:64, 0:64] = 1.0
    bdm[64:128, 64:128] = 1.0
    tie = np.zeros((NES, GRP), dtype=np.float32)
    tie[:, :] = np.linspace(0.0, 1e-6, GRP, dtype=np.float32)[None, :]
    c = {"c_ident": ident, "c_maskT": maskT, "c_hsel": hsel, "c_hrep": hrep,
         "c_ones1": ones1, "c_row1": row1, "c_bdP": bdP, "c_maskT2": maskT2,
         "c_seye": seye, "c_bdm": bdm}
    c = {k: v.astype(bf) for k, v in c.items()}
    c["c_tie"] = tie
    eps = np.zeros((128, 3), dtype=np.float32)
    eps[:, 0] = EPS_RMS
    eps[:, 1] = EPS_LN
    eps[:, 2] = 1.0
    c["c_eps"] = eps
    return c


def kernel(L0_attn_x, L0_attn_Wq, L0_attn_Wk, L0_attn_Wv, L0_attn_Wg, L0_attn_Wo,
           L0_attn_w1, L0_attn_w2, L0_attn_w3, L0_attn_g_attn, L0_attn_g_ffn,
           L0_ffn_controller, L0_ffn_f1, L0_ffn_f2, L0_ffn_bias):
    nc = build_nc()
    consts = _host_consts()
    x = np.ascontiguousarray(np.asarray(L0_attn_x, dtype=np.float32))
    w = {"Wq": L0_attn_Wq, "Wk": L0_attn_Wk, "Wv": L0_attn_Wv, "Wg": L0_attn_Wg,
         "Wo": L0_attn_Wo, "w1": L0_attn_w1, "w2": L0_attn_w2, "w3": L0_attn_w3,
         "ga": L0_attn_g_attn, "gf": L0_attn_g_ffn, "ctl": L0_ffn_controller,
         "f1": L0_ffn_f1, "f2": L0_ffn_f2, "fb": L0_ffn_bias}
    w = {k: np.ascontiguousarray(np.asarray(v, dtype=np.float32)) for k, v in w.items()}
    in_maps = []
    for cid in range(NCORE):
        m = {"x": x[cid * SPC:(cid + 1) * SPC]}
        m.update(w)
        m.update(consts)
        in_maps.append(m)
    res = run_bass_kernel_spmd(nc, in_maps, core_ids=list(range(NCORE)))
    return np.concatenate([r["y"] for r in res.results], axis=0)


# version 18
# speedup vs baseline: 1.0466x; 1.0261x over previous
import contextlib
import numpy as np
import ml_dtypes
import concourse.bass as bass
import concourse.mybir as mybir
from concourse.bass_utils import run_bass_kernel_spmd

F32 = mybir.dt.float32
BF16 = mybir.dt.bfloat16
AF = mybir.ActivationFunctionType
ALU = mybir.AluOpType
AX = mybir.AxisListType

B, T, D = 16, 2048, 1024
H, DK, DV = 16, 64, 64
DFF = 2736
NE, ESL, ESZ = 16, 4, 64
NES = NE * ESL
GRP = 16
EPS_RMS, EPS_LN = 1e-6, 1e-5

NCORE = 8
SPC = B // NCORE
TC = 1024
NCH = T // TC
PT = 256
NPT = TC // PT
C = 64
NC_T = PT // C
NKC = D // 128
W3KC = (DFF + 127) // 128
EB = 16
NEB = NES // EB
NTT = TC // 128
NLN = TC // 512

DBG = {"name": None, "shape": None}


class Reg:
    __slots__ = ("wtok", "rtoks")

    def __init__(self):
        self.wtok = None
        self.rtoks = []


class Bld:
    ENGS = ("t", "v", "s", "g", "y")
    NLANES = {"g": 20, "y": 8}

    def __init__(self, nc, stack):
        self.nc = nc
        self.ops = {e: [] for e in self.ENGS}
        self.cnt = {e: 0 for e in self.ENGS}
        self.waited = {e: {} for e in self.ENGS}
        self.sems = {e: stack.enter_context(nc.semaphore("sem_" + e)) for e in ("t", "v", "s", "p")}
        self.cnt["p"] = 0
        self.rr = {"g": 0, "y": 0}
        for e in ("g", "y"):
            for i in range(self.NLANES[e]):
                k = e + str(i)
                self.sems[k] = stack.enter_context(nc.semaphore("sem_" + k))
                self.cnt[k] = 0
                for c in self.waited.values():
                    pass

    def _wait(self, eng, pe, v):
        if self.waited[eng].get(pe, 0) >= v:
            return
        self.waited[eng][pe] = v
        sem = self.sems[pe]
        self.ops[eng].append(lambda e, sem=sem, v=v: e.wait_ge(sem, v))

    def _dma_lane(self, eng):
        i = self.rr[eng] % self.NLANES[eng]
        self.rr[eng] += 1
        return eng + str(i)

    def waits(self, eng, toks):
        best = {}
        for t in toks:
            if t is None:
                continue
            pe, v = t
            if best.get(pe, 0) < v:
                best[pe] = v
        for pe, v in best.items():
            self._wait(eng, pe, v)

    def emit(self, eng, fns, reads=(), writes=(), extra=()):
        if not isinstance(fns, (list, tuple)):
            fns = [fns]
        def _add(n, t):
            if isinstance(t, list):
                n.extend(t)
            elif t is not None:
                n.append(t)
        need = []
        for t_ in extra:
            _add(need, t_)
        for r in reads:
            _add(need, r.wtok)
        for w in writes:
            _add(need, w.wtok)
            need.extend(w.rtoks)
        if eng == "p":
            self.waits("g", need)
            sem = self.sems["p"]
            for i, fn in enumerate(fns):
                if i == len(fns) - 1:
                    self.ops["g"].append(lambda e, fn=fn, sem=sem: fn(e).then_inc(sem, 1))
                else:
                    self.ops["g"].append(fn)
            self.cnt["p"] += 1
            tok = ("p", self.cnt["p"])
            for r in reads:
                r.rtoks.append(tok)
            for w in writes:
                w.wtok = tok
                w.rtoks = []
            return tok
        self.waits(eng, need)
        if eng in ("g", "y"):
            lane = self._dma_lane(eng)
            sem = self.sems[lane]
            self._wait(eng, lane, self.cnt[lane])
            for fn in fns:
                self.ops[eng].append(lambda e, fn=fn, sem=sem: fn(e).then_inc(sem, 16))
            self.cnt[lane] += 16 * len(fns)
            tok = (lane, self.cnt[lane])
        else:
            sem = self.sems[eng]
            for i, fn in enumerate(fns):
                if i == len(fns) - 1:
                    self.ops[eng].append(lambda e, fn=fn, sem=sem: fn(e).then_inc(sem, 1))
                else:
                    self.ops[eng].append(fn)
            self.cnt[eng] += 1
            tok = (eng, self.cnt[eng])
        for r in reads:
            r.rtoks.append(tok)
        for w in writes:
            w.wtok = tok
            w.rtoks = []
        return tok

    def tok(self, eng):
        return (eng, self.cnt[eng])

    def barrier(self, extra=()):
        pb = [self.tok("t"), self.tok("v"), self.tok("s"), self.tok("p")] + list(extra)
        for e in ("t", "v", "s"):
            self.waits(e, pb)
        return pb


def build_nc():
    nc = bass.Bass("TRN2", target_bir_lowering=False)
    stack = contextlib.ExitStack()

    dx = nc.dram_tensor("x", [SPC, T, D], F32, kind="ExternalInput").ap()
    dy = nc.dram_tensor("y", [SPC, T, D], F32, kind="ExternalOutput").ap()
    dW = {}
    for nm, shp in [("Wq", [D, D]), ("Wk", [D, D]), ("Wv", [D, D]), ("Wg", [D, D]),
                    ("Wo", [D, D]), ("w1", [D, DFF]), ("w2", [D, DFF]), ("w3", [DFF, D]),
                    ("ga", [D]), ("gf", [D]), ("ctl", [D, NE, ESL]),
                    ("f1", [D, NE, ESL, ESZ]), ("f2", [NE, ESL, ESZ, D]),
                    ("fb", [NE, ESL, ESZ])]:
        dW[nm] = nc.dram_tensor(nm, shp, F32, kind="ExternalInput").ap()
    dC = {}
    for nm, shp in [("ident", [128, 128]), ("maskT", [128, 128]), ("hsel", [128, NKC * 16]),
                    ("hrep", [16, NKC * 128]), ("ones1", [128, 1]), ("row1", [1, 128]),
                    ("bdP", [128, 8]), ("maskT2", [128, 64]), ("seye", [64, 128]),
                    ("bdm", [128, 128])]:
        dC[nm] = nc.dram_tensor("c_" + nm, shp, BF16, kind="ExternalInput").ap()
    dC["tie"] = nc.dram_tensor("c_tie", [NES, GRP], F32, kind="ExternalInput").ap()
    dC["eps"] = nc.dram_tensor("c_eps", [128, 3], F32, kind="ExternalInput").ap()
    ddbg = None
    if DBG["name"]:
        ddbg = nc.dram_tensor("dbg", DBG["shape"], DBG.get("dtype", F32), kind="ExternalOutput").ap()

    sbt = lambda name, shape, dt: stack.enter_context(nc.sbuf_tensor(name, shape, dt))
    WA = sbt("WA", [128, 66304], BF16)
    SB = sbt("SBs", [128, NKC * TC], BF16)
    YT = sbt("YT", [128, NTT * 1024], BF16)
    SC = sbt("SCb", [128, 4992], BF16)
    SF = sbt("SFf", [128, 3334], F32)
    STT_ = sbt("STATE", [128, NKC * 128], BF16)
    CN = {k: sbt("k" + k, list(dC[k].shape), dC[k].dtype) for k in dC}
    GA = sbt("gacol", [128, NKC], F32)
    GF = sbt("gfcol", [128, NKC], F32)
    FBIA = sbt("fbias", [ESZ, NES], F32)
    CTL = sbt("ctlw", [128, NKC * NES], BF16)
    ROWS = sbt("rows", [128, 1536], F32)
    RB = sbt("rowsb", [128, 1024], BF16)
    XS = [sbt("XS0", [128, 2 * D], BF16), sbt("XS1", [128, 2 * D], BF16)]

    pst = lambda name, shape, dt: stack.enter_context(nc.psum_tensor(name, shape, dt))
    PA = pst("PA", [128, 512], F32)
    PB = pst("PB", [128, 512], F32)
    PC = pst("PC", [128, 512], F32)
    PDB = pst("PDB", [128, 1024], BF16)
    PD = PDB.bitcast(F32)
    PS_ = pst("PS", [128, 512], F32)
    PR = pst("PR", [128, 512], F32)
    PRB = PR.bitcast(BF16)
    PG = pst("PG", [128, 512], F32)
    PGB = PG.bitcast(BF16)
    PTB = pst("PTB", [128, 1024], BF16)

    bld = Bld(nc, stack)

    S_SQ, S_IR, S_TMP, S_SCM, S_VH, S_EQ, S_EK = 0, 1024, 1280, 1536, 2048, 2304, 2816
    S_KTS = 1792
    S_VTS = 2048
    S_SMM = 3328
    S_KBD, S_VBD = 3456, 3968
    S_MR, S_VR = 1024, 1536
    S_PERM, S_PERMT = 2048, 3072
    S_P, S_PT2, S_INN, S_STG = 3584, 3712, 3840, 3968
    F_SP, F_SCUM, F_COLS = 0, 512, 1024
    F_LG, F_GM = 0, 1024
    F_YO = 1216
    F_PUT = 3270

    rWA = {}
    rXS = [Reg(), Reg()]
    rCN = Reg()
    rSB = [Reg() for _ in range(NKC)]
    rSBtm = [Reg() for _ in range(NTT)]
    rYT, rSC, rSF, rROWS, rRB = {}, {}, {}, {}, {}
    rST = Reg()
    rPB = {k: Reg() for k in ("PA", "PB", "PC", "PD", "PS", "PR", "PG", "PTB")}

    def reg(d, key):
        if key not in d:
            d[key] = Reg()
        return d[key]

    for k in dC:
        bld.emit("g", lambda e, a=CN[k], b=dC[k]: e.dma_start(out=a[:, :], in_=b), writes=[rCN])
    bld.emit("g", lambda e: e.dma_start(out=GA[:, :], in_=dW["ga"].rearrange("(c p) -> p c", p=128), allow_slow_non_contiguous=True), writes=[rCN])
    bld.emit("g", lambda e: e.dma_start(out=GF[:, :], in_=dW["gf"].rearrange("(c p) -> p c", p=128), allow_slow_non_contiguous=True), writes=[rCN])
    bld.emit("g", lambda e: e.dma_start(out=FBIA[:, :], in_=dW["fb"].rearrange("e s f -> f (e s)"), allow_slow_non_contiguous=True), writes=[rCN])
    bld.emit("g", lambda e: e.dma_start(out=CTL[:, :], in_=dW["ctl"].rearrange("(c p) e s -> p c (e s)", p=128)), writes=[rCN])

    OFF = {"Wq": 0, "Wk": 8192, "Wv": 16384, "Wg": 24576, "Wo": 32768,
           "w2": 0, "w3": 21888, "w1": 44416,
           "f1_0": 0, "f1_1": 8192, "f2_0": 16384, "f2_1": 24576,
           "sel": 32768, "inter": 49152}
    WCOLS = {"Wq": 1024, "Wk": 1024, "Wv": 1024, "Wg": 1024, "Wo": 1024,
             "w1": DFF, "w2": DFF, "w3": 1024}

    def load_w(name, src_fn, nchunks, ncols, off, extra=(), prows_fn=None):
        r = reg(rWA, name)
        war = (r.wtok if isinstance(r.wtok, list) else [r.wtok]) + r.rtoks
        toks = []
        for c in range(nchunks):
            pr = prows_fn(c) if prows_fn else 128
            toks.append(bld.emit("g", lambda e, c=c, pr=pr: e.dma_start(
                out=WA[0:pr, off + c * ncols: off + (c + 1) * ncols], in_=src_fn(c)),
                extra=(list(extra) + war) if c == 0 else []))
        r.wtok = toks
        r.rtoks = []
        return r

    def wsl(name, kc, col0, ncols, prows=128):
        o = OFF[name] + kc * WCOLS[name] + col0
        return WA[0:prows, o:o + ncols]

    def p64(hh):
        return slice(hh * 64, (hh + 1) * 64)

    dbg_toks = []

    def dbg(name, ap, deps, regs=()):
        if DBG["name"] != name or ddbg is None:
            return
        dbg_toks.append(bld.emit("g", lambda e: e.dma_start(out=ddbg, in_=ap),
                                 reads=list(regs), extra=deps))

    out_toks = []
    first_phase = True

    for seq in range(SPC):
        for ch in range(NCH):
            t0g = ch * TC
            pb = [] if first_phase else bld.barrier(extra=dbg_toks)
            first_phase = False
            if ch == 0:
                bld.emit("v", lambda e: e.memset(STT_[:, :], 0),
                         writes=[rST] + [reg(rYT, "ST%d" % hp) for hp in range(NKC)])
            bld.emit("v", lambda e: e.memset(SC[:, S_KBD:S_KBD + 512], 0),
                     writes=[reg(rSC, "KBD")], extra=pb)

            x2_done = []
            xp_emitted = [False] * NPT

            def emit_xload(ti2, seq2, base2):
                if ti2 >= NPT or xp_emitted[ti2]:
                    return
                xp_emitted[ti2] = True
                par2 = ti2 % 2
                t02 = base2 + ti2 * PT
                xwar = [rXS[par2].wtok] + rXS[par2].rtoks
                xtoks = []
                for a2 in range(2):
                    xtoks.append(bld.emit("g", lambda e, par2=par2, seq2=seq2, t02=t02, a2=a2: e.dma_start(
                        out=XS[par2][:, a2 * D:(a2 + 1) * D],
                        in_=dx[seq2, t02 + a2 * 128:t02 + (a2 + 1) * 128, :]),
                        extra=xwar if a2 == 0 else []))
                rXS[par2].wtok = xtoks
                rXS[par2].rtoks = []

            emit_xload(0, seq, t0g)
            emit_xload(1, seq, t0g)
            for wn in ("Wq", "Wk", "Wv", "Wg", "Wo"):
                load_w(wn, lambda c, wn=wn: dW[wn][c * 128:(c + 1) * 128, :], NKC, 1024,
                       OFF[wn], extra=pb)
            for ti in range(NPT):
                par = ti % 2
                t0 = t0g + ti * PT
                tl = ti * PT
                rh = reg(rYT, "A_H")
                rqh, rkh = reg(rYT, "A_QH"), reg(rYT, "A_KH")
                rvf, ro = reg(rYT, "A_VF"), reg(rYT, "A_H")
                A_H, A_QH, A_KH, A_O, A_VF = 0, 2048, 4096, 0, 6144
                for a in range(2):
                    bld.emit("t", [lambda e, a=a, kc=kc, par=par: e.transpose(
                        PTB[:, kc * 128:(kc + 1) * 128],
                        XS[par][:, a * D + kc * 128: a * D + (kc + 1) * 128],
                        CN["ident"][:, :]) for kc in range(NKC)],
                        reads=[rXS[par], rCN], writes=[rPB["PTB"]], extra=pb)
                    bld.emit("v", lambda e, a=a, tl=tl: e.tensor_copy(
                        SB[:, :].rearrange("p (c m) -> p c m", m=TC)[:, :, tl + a * 128:tl + (a + 1) * 128],
                        PTB[:, :].rearrange("p (c n) -> p c n", n=128)),
                        reads=[rPB["PTB"]],
                        writes=[rSB[kc2] for kc2 in range(NKC)] if a == 1 else [], extra=pb)
                emit_xload(ti + 2, seq, t0g)
                for kc in range(NKC):
                    rsq = reg(rSC, "SQ%d" % (kc % 2))
                    bld.emit("v", lambda e, kc=kc, tl=tl: e.tensor_tensor(
                        SC[:, S_SQ + (kc % 2) * 256:S_SQ + (kc % 2) * 256 + 256],
                        SB[:, kc * TC + tl:kc * TC + tl + 256],
                        SB[:, kc * TC + tl:kc * TC + tl + 256], ALU.mult),
                        reads=[rSB[kc]], writes=[rsq])
                    bld.emit("t", lambda e, kc=kc: e.matmul(
                        PS_[0:1, 0:256], CN["ones1"][:, :],
                        SC[:, S_SQ + (kc % 2) * 256:S_SQ + (kc % 2) * 256 + 256],
                        start=(kc == 0), stop=(kc == NKC - 1)),
                        reads=[rsq, rCN], writes=[rPB["PS"]] if kc in (0, NKC - 1) else [])
                rr1 = reg(rROWS, "r1")
                bld.emit("s", lambda e: e.activation(ROWS[0:1, 0:256], PS_[0:1, 0:256], AF.Ln,
                                                     bias=CN["eps"][0:1, 0:1], scale=1.0 / D),
                         reads=[rPB["PS"]], writes=[rr1])
                rb1 = reg(rRB, "r1")
                bld.emit("s", lambda e: e.activation(RB[0:1, 0:256], ROWS[0:1, 0:256], AF.Exp, scale=-0.5),
                         reads=[rr1], writes=[rb1])
                bld.emit("t", lambda e: e.matmul(PR[:, 0:256], CN["row1"][:, :], RB[0:1, 0:256],
                                                 start=True, stop=True),
                         reads=[rb1, rCN], writes=[rPB["PR"]])
                bld.emit("v", lambda e: e.tensor_copy(SC[:, S_IR:S_IR + 256], PR[:, 0:256]),
                         reads=[rPB["PR"]], writes=[reg(rSC, "IR")])
                bld.emit("v", [lambda e, kc=kc, tl=tl: e.scalar_tensor_tensor(
                    YT[:, A_H + kc * 256:A_H + (kc + 1) * 256],
                    SB[:, kc * TC + tl:kc * TC + tl + 256],
                    GA[:, kc:kc + 1], SC[:, S_IR:S_IR + 256], ALU.mult, ALU.mult)
                    for kc in range(NKC)],
                    reads=list(rSB) + [reg(rSC, "IR"), rCN], writes=[rh])
                dbg("h", YT[:, A_H:A_H + 2048], [rh.wtok], regs=[rh])

                accs = [("PA0", PA, 0), ("PA1", PA, 256), ("PB0", PB, 0), ("PB1", PB, 256)]
                na = [0]

                def proj_mm(wn, mc):
                    nm, pt_, c0 = accs[na[0] % 4]
                    na[0] += 1
                    sl = rPB[nm[:2]]
                    bld.emit("t", [lambda e, kc=kc, mc=mc, wn=wn, pt_=pt_, c0=c0: e.matmul(
                        pt_[:, c0:c0 + 256], wsl(wn, kc, mc * 128, 128),
                        YT[:, A_H + kc * 256:A_H + (kc + 1) * 256],
                        start=(kc == 0), stop=(kc == NKC - 1)) for kc in range(NKC)],
                        reads=[rh, reg(rWA, wn)], writes=[sl])
                    return sl, pt_, c0

                sq_toks = {}
                for wn, dst, rdst in (("Wq", A_QH, rqh), ("Wk", A_KH, rkh)):
                    for mc in range(NKC):
                        sl, pt_, c0 = proj_mm(wn, mc)
                        ev = bld.emit("s", lambda e, pt_=pt_, c0=c0, dst=dst, mc=mc: e.copy(
                            YT[:, dst + mc * 256:dst + (mc + 1) * 256], pt_[:, c0:c0 + 256]),
                            reads=[sl], writes=[rdst])
                for mc in range(NKC):
                    sl, pt_, c0 = proj_mm("Wv", mc)
                    bld.emit("s", lambda e, pt_=pt_, c0=c0, mc=mc: e.copy(
                        YT[:, A_VF + mc * 256:A_VF + (mc + 1) * 256], pt_[:, c0:c0 + 256]),
                        reads=[sl], writes=[rvf])
                sstat = rPB["PS"]
                for wn, dst, sc0, rdst in (("Wq", A_QH, 0, rqh), ("Wk", A_KH, 256, rkh)):
                    for mc in range(NKC):
                        off2 = S_TMP if mc % 2 == 0 else S_VH
                        rsqt = reg(rSC, "TMP%d" % (mc % 2))
                        bld.emit("s", lambda e, dst=dst, mc=mc, off2=off2: e.activation(
                            SC[:, off2:off2 + 256],
                            YT[:, dst + mc * 256:dst + (mc + 1) * 256], AF.Square),
                            reads=[rdst], writes=[rsqt])
                        bld.emit("t", lambda e, mc=mc, sc0=sc0, off2=off2: e.matmul(
                            PS_[0:16, sc0:sc0 + 256], CN["hsel"][:, mc * 16:(mc + 1) * 16],
                            SC[:, off2:off2 + 256],
                            start=(mc == 0), stop=(mc == NKC - 1)),
                            reads=[rsqt, rCN],
                            writes=[sstat] if mc in (0, NKC - 1) else [])

                for which, sc0, ro0 in (("q", 0, 256), ("k", 256, 768)):
                    rmx = reg(rROWS, "mx" + which)
                    bld.emit("v", lambda e, sc0=sc0, ro0=ro0: e.tensor_scalar_max(
                        ROWS[0:16, ro0:ro0 + 256], PS_[0:16, sc0:sc0 + 256], 1e-24),
                        reads=[rPB["PS"]], writes=[rmx])
                    bld.emit("s", lambda e, ro0=ro0: e.activation(
                        ROWS[0:16, ro0 + 256:ro0 + 512], ROWS[0:16, ro0:ro0 + 256], AF.Ln),
                        reads=[rmx], writes=[rmx])
                    rrb = reg(rRB, "rq" + which)
                    bld.emit("s", lambda e, ro0=ro0, rb0=(256 if ro0 == 256 else 512): e.activation(
                        RB[0:16, rb0:rb0 + 256], ROWS[0:16, ro0 + 256:ro0 + 512], AF.Exp, scale=-0.5),
                        reads=[rmx], writes=[rrb])

                gcols = []

                def gate_tail(mc):
                    rot = mc % 2
                    rbank = PR if rot == 0 else PS_
                    rslot = rPB["PR" if rot == 0 else "PS"]
                    bld.emit("t", lambda e, mc=mc, rbank=rbank: e.matmul(
                        rbank[:, 0:512], CN["hrep"][:, mc * 128:(mc + 1) * 128],
                        RB[0:16, 256:768], start=True, stop=True),
                        reads=[reg(rRB, "rqq"), reg(rRB, "rqk"), rCN], writes=[rslot])
                    for which, bc0, dst, rdst, eoff, toff, tnm in (
                            ("q", 0, A_QH, rqh, S_EQ, S_TMP, "TMP0"),
                            ("k", 256, A_KH, rkh, S_EK, S_VH, "TMP1")):
                        rexp = reg(rSC, ("EQ%d" if which == "q" else "EK%d") % rot)
                        bld.emit("v", lambda e, rot=rot, eoff=eoff, rbank=rbank, toff=toff, bc0=bc0: e.tensor_tensor(
                            SC[:, toff:toff + 256], rbank[:, bc0:bc0 + 256],
                            SC[:, eoff + rot * 256:eoff + rot * 256 + 256], ALU.mult),
                            reads=[rslot, rexp], writes=[reg(rSC, tnm)])
                        bld.emit("p", lambda e, mc=mc, dst=dst, toff=toff: e.tensor_tensor(
                            YT[:, dst + mc * 256:dst + (mc + 1) * 256],
                            YT[:, dst + mc * 256:dst + (mc + 1) * 256],
                            SC[:, toff:toff + 256], ALU.mult),
                            reads=[reg(rSC, tnm)], writes=[rdst], extra=pb)

                def st_spevac(mc, sl, pt_, c0):
                    rot = mc % 2
                    bld.emit("v", lambda e, pt_=pt_, c0=c0, rot=rot: e.tensor_scalar(
                        SF[:, F_SP + rot * 256:F_SP + rot * 256 + 256], pt_[:, c0:c0 + 256],
                        -1.0, 40.0, ALU.mult, ALU.min),
                        reads=[sl], writes=[reg(rSF, "SP%d" % rot)])

                def st_exp(mc):
                    rot = mc % 2
                    bld.emit("s", lambda e, rot=rot: e.activation(
                        SF[:, F_SCUM + rot * 256:F_SCUM + rot * 256 + 256],
                        SF[:, F_SP + rot * 256:F_SP + rot * 256 + 256], AF.Exp),
                        reads=[reg(rSF, "SP%d" % rot)], writes=[reg(rSF, "SCUM%d" % rot)])

                def st_ln(mc):
                    rot = mc % 2
                    bld.emit("s", lambda e, rot=rot: e.activation(
                        SF[:, F_SP + rot * 256:F_SP + rot * 256 + 256],
                        SF[:, F_SCUM + rot * 256:F_SCUM + rot * 256 + 256], AF.Ln,
                        bias=CN["eps"][0:128, 2:3]),
                        reads=[reg(rSF, "SCUM%d" % rot)], writes=[reg(rSF, "SP%d" % rot)])

                def st_scan(mc):
                    rot = mc % 2
                    bld.emit("v", lambda e, rot=rot: e.tensor_tensor_scan(
                        SF[:, F_SCUM + rot * 256:F_SCUM + rot * 256 + 256],
                        SF[:, F_SP + rot * 256:F_SP + rot * 256 + 256],
                        SF[:, F_SP + rot * 256:F_SP + rot * 256 + 256],
                        0.0, ALU.add, ALU.bypass),
                        reads=[reg(rSF, "SP%d" % rot)], writes=[reg(rSF, "SCUM%d" % rot)])

                def st_cols(mc):
                    rot = mc % 2
                    base = F_COLS + mc * 24
                    rcol = reg(rSF, "COL%d" % mc)
                    bld.emit("v", [
                        lambda e, rot=rot, base=base: e.tensor_copy(
                            SF[:, base:base + NC_T],
                            SF[:, F_SCUM + rot * 256:F_SCUM + rot * 256 + 256]
                            .rearrange("p (c n) -> p c n", n=C)[:, :, C // 2 - 1]),
                        lambda e, rot=rot, base=base: e.tensor_copy(
                            SF[:, base + 4:base + 8],
                            SF[:, F_SCUM + rot * 256:F_SCUM + rot * 256 + 256]
                            .rearrange("p (c n) -> p c n", n=C)[:, :, C - 1]),
                    ], reads=[reg(rSF, "SCUM%d" % rot)], writes=[rcol])
                    bld.emit("v", [
                        lambda e, base=base: e.tensor_scalar_mul(
                            SF[:, base + 8:base + 9], SF[:, base + 4:base + 5], -1.0),
                        lambda e, base=base: e.tensor_sub(
                            SF[:, base + 9:base + 12], SF[:, base + 4:base + 7], SF[:, base + 5:base + 8]),
                    ], reads=[rcol], writes=[rcol])

                def st_acexp(mc):
                    base = F_COLS + mc * 24
                    rcol = reg(rSF, "COL%d" % mc)
                    bld.emit("s", [
                        lambda e, base=base: e.activation(SF[:, base + 12:base + 16],
                                                          SF[:, base + 8:base + 12], AF.Exp),
                        lambda e, base=base: e.activation(SF[:, base + 16:base + 20],
                                                          SF[:, base:base + 4], AF.Exp, scale=-1.0),
                    ], reads=[rcol], writes=[rcol])

                def st_fksub(mc):
                    base = F_COLS + mc * 24
                    rcol = reg(rSF, "COL%d" % mc)
                    bld.emit("v", lambda e, base=base: e.tensor_sub(
                        SF[:, base + 8:base + 12], SF[:, base:base + 4], SF[:, base + 4:base + 8]),
                        reads=[rcol], writes=[rcol])

                def st_fkexp(mc):
                    base = F_COLS + mc * 24
                    rcol = reg(rSF, "COL%d" % mc)
                    bld.emit("s", lambda e, base=base: e.activation(
                        SF[:, base + 16:base + 20], SF[:, base + 8:base + 12], AF.Exp),
                        reads=[rcol], writes=[rcol])

                def st_gkprep(mc):
                    base = F_COLS + mc * 24
                    rcol = reg(rSF, "COL%d" % mc)
                    bld.emit("v", [
                        lambda e, base=base: e.tensor_scalar_mul(
                            SF[:, base + 8:base + 9], SF[:, base:base + 1], -1.0),
                        lambda e, base=base: e.tensor_sub(
                            SF[:, base + 9:base + 12], SF[:, base + 4:base + 7], SF[:, base + 1:base + 4]),
                    ], reads=[rcol], writes=[rcol])

                def st_gkexp(mc):
                    base = F_COLS + mc * 24
                    rcol = reg(rSF, "COL%d" % mc)
                    bld.emit("s", lambda e, base=base: e.activation(
                        SF[:, base + 20:base + 24], SF[:, base + 8:base + 12], AF.Exp),
                        reads=[rcol], writes=[rcol])

                def st_bsub(mc):
                    rot = mc % 2
                    base = F_COLS + mc * 24
                    rcol = reg(rSF, "COL%d" % mc)
                    bld.emit("v", lambda e, rot=rot, base=base: e.tensor_tensor(
                        SF[:, F_SP + rot * 256:F_SP + rot * 256 + 256].rearrange("p (c n) -> p c n", n=C),
                        SF[:, F_SCUM + rot * 256:F_SCUM + rot * 256 + 256].rearrange("p (c n) -> p c n", n=C),
                        SF[:, base:base + NC_T].rearrange("p c -> p c ()").broadcast_to([128, NC_T, C]),
                        ALU.subtract),
                        reads=[reg(rSF, "SCUM%d" % rot), rcol], writes=[reg(rSF, "SP%d" % rot)])

                def st_eqek(mc):
                    rot = mc % 2
                    rsp = reg(rSF, "SP%d" % rot)
                    bld.emit("s", lambda e, rot=rot: e.activation(
                        SC[:, S_EQ + rot * 256:S_EQ + rot * 256 + 256],
                        SF[:, F_SP + rot * 256:F_SP + rot * 256 + 256], AF.Exp, scale=-1.0),
                        reads=[rsp], writes=[reg(rSC, "EQ%d" % rot)])
                    bld.emit("s", lambda e, rot=rot: e.activation(
                        SC[:, S_EK + rot * 256:S_EK + rot * 256 + 256],
                        SF[:, F_SP + rot * 256:F_SP + rot * 256 + 256], AF.Exp),
                        reads=[rsp], writes=[reg(rSC, "EK%d" % rot)])

                for pr in range(NKC // 2):
                    m0, m1 = 2 * pr, 2 * pr + 1
                    prsl = {}
                    for mc in (m0, m1):
                        prsl[mc] = proj_mm("Wg", mc)
                        gcols.append((F_COLS + mc * 24, reg(rSF, "COL%d" % mc)))
                    for mc in (m0, m1):
                        st_spevac(mc, *prsl[mc])
                    for st in (st_exp, st_ln, st_scan, st_cols, st_acexp, st_fksub,
                               st_fkexp, st_gkprep, st_gkexp, st_bsub, st_eqek):
                        for mc in (m0, m1):
                            st(mc)
                    for mc in (m0, m1):
                        gate_tail(mc)
                dbg("qt", YT[:, A_QH:A_QH + 2048], [rqh.wtok], regs=[rqh])
                dbg("kt", YT[:, A_KH:A_KH + 2048], [rkh.wtok], regs=[rkh])

                GP = [dict(kk=4096, kts=S_KTS, vts=S_VTS, vbd=S_VBD, scm=S_SCM,
                           kkn="KK0", ktsn="KTS", vtsn="VTS", vbdn="VBD", scmn="SCM0",
                           tb=PTB, tbn="PTB", scb=PG, scbn="PG", smm=S_SMM, smmn="SMM",
                           pob=PC, pobn="PC", pub=PD, pubn="PD"),
                      dict(kk=4160, kts=4224, vts=4352, vbd=4480, scm=4608,
                           kkn="KK1", ktsn="KTS1", vtsn="VTS1", vbdn="VBD1", scmn="SCM1",
                           tb=PRB, tbn="PR", scb=PS_, scbn="PS", smm=4672, smmn="SMM1",
                           pob=PA, pobn="PA", pub=PB, pubn="PB")]
                for hp in range(NKC):
                    base, rcol = gcols[hp]
                    rkbd = reg(rSC, "KBD")
                    bld.emit("p", [lambda e, hp=hp, hh=hh: e.tensor_copy(
                        SC[p64(hh), S_KBD:S_KBD + 512].rearrange("p (c n) -> p c n", n=128)[:, :, hh * 64:(hh + 1) * 64],
                        YT[p64(hh), A_KH + hp * 256:A_KH + (hp + 1) * 256].rearrange("p (c n) -> p c n", n=64))
                        for hh in range(2)],
                        reads=[rkh], writes=[rkbd], extra=pb)

                    def gla_tr(cc):
                        g = GP[cc % 2]
                        rtb = rPB[g["tbn"]]
                        bld.emit("t", [
                            lambda e, hp=hp, cc=cc, g=g: e.transpose(
                                g["tb"][0:C, 0:128],
                                YT[:, A_KH + hp * 256 + cc * C:A_KH + hp * 256 + (cc + 1) * C],
                                CN["ident"][:, :]),
                            lambda e, hp=hp, cc=cc, g=g: e.transpose(
                                g["tb"][0:C, 128:256],
                                YT[:, A_VF + hp * 256 + cc * C:A_VF + hp * 256 + (cc + 1) * C],
                                CN["ident"][:, :]),
                        ], reads=[rkh, rvf, rCN], writes=[rtb])
                        rkts = reg(rSC, g["ktsn"])
                        rvts = reg(rSC, g["vtsn"])
                        bld.emit("s", [
                            lambda e, g=g: e.copy(SC[0:C, g["kts"]:g["kts"] + 128], g["tb"][0:C, 0:128]),
                            lambda e, g=g: e.copy(SC[0:C, g["vts"]:g["vts"] + 128], g["tb"][0:C, 128:256]),
                        ], reads=[rtb], writes=[rkts, rvts])
                        return (rkts, rvts)

                    def gla_prep(cc, tr_):
                        g = GP[cc % 2]
                        rkts, rvts = tr_
                        slc = rPB[g["scbn"]]
                        bld.emit("t", lambda e, g=g: e.matmul(
                            g["scb"][:, 256:384], CN["seye"][:, :], SC[0:C, g["vts"]:g["vts"] + 128],
                            start=True, stop=True),
                            reads=[rvts, rCN], writes=[slc])
                        rvbd = reg(rSC, g["vbdn"])
                        bld.emit("v", lambda e, g=g: e.tensor_tensor(
                            SC[:, g["vbd"]:g["vbd"] + 128], g["scb"][:, 256:384], CN["bdm"][:, :], ALU.mult),
                            reads=[slc, rCN], writes=[rvbd])
                        bld.emit("t", lambda e, hp=hp, cc=cc, g=g: e.matmul(
                            g["scb"][:, 0:C],
                            SC[:, S_KBD + cc * 128:S_KBD + (cc + 1) * 128],
                            YT[:, A_QH + hp * 256 + cc * C:A_QH + hp * 256 + (cc + 1) * C],
                            start=True, stop=True),
                            reads=[rkbd, rqh], writes=[slc])
                        rscm = reg(rSC, g["scmn"])
                        bld.emit("v", lambda e, g=g: e.tensor_tensor(
                            SC[:, g["scm"]:g["scm"] + C], g["scb"][:, 0:C],
                            CN["maskT2"][:, :], ALU.mult),
                            reads=[slc, rCN], writes=[rscm])
                        rqg = reg(rSC, g["smmn"])
                        bld.emit("v", lambda e, hp=hp, cc=cc, base=base, g=g: e.tensor_scalar(
                            SC[:, g["smm"]:g["smm"] + C],
                            YT[:, A_QH + hp * 256 + cc * C:A_QH + hp * 256 + (cc + 1) * C],
                            SF[:, base + 20 + cc:base + 21 + cc], None, ALU.mult),
                            reads=[rqh, rcol], writes=[rqg])
                        return (rkts, rvts, rvbd, rscm, rqg)

                    trs = {0: gla_tr(0), 1: gla_tr(1)}
                    pregs = {0: gla_prep(0, trs[0])}
                    for cc in range(NC_T):
                        if cc + 2 < NC_T:
                            trs[cc + 2] = gla_tr(cc + 2)
                        if cc + 1 < NC_T:
                            pregs[cc + 1] = gla_prep(cc + 1, trs[cc + 1])
                        g = GP[cc % 2]
                        rkts, rvts, rvbd, rscm, rqg = pregs[cc]
                        slo = rPB[g["pobn"]]
                        poc0 = (cc & 7) * 64
                        bld.emit("t", [
                            lambda e, g=g, poc0=poc0: e.matmul(
                                g["pob"][:, poc0:poc0 + C], SC[:, g["vbd"]:g["vbd"] + 128],
                                SC[:, g["scm"]:g["scm"] + C], start=True, stop=False),
                            lambda e, hp=hp, g=g, poc0=poc0: e.matmul(
                                g["pob"][:, poc0:poc0 + C], STT_[:, hp * 128:(hp + 1) * 128],
                                SC[:, g["smm"]:g["smm"] + C],
                                start=False, stop=True),
                        ], reads=[rvbd, rscm, rqg, rST], writes=[slo])
                        slu = rPB[g["pubn"]]
                        puc0 = (cc & 7) * 64
                        bld.emit("t", [lambda e, hh=hh, g=g, puc0=puc0: e.matmul(
                            g["pub"][p64(hh), puc0:puc0 + 64],
                            SC[0:C, g["kts"] + hh * 64:g["kts"] + (hh + 1) * 64],
                            SC[0:C, g["vts"] + hh * 64:g["vts"] + (hh + 1) * 64],
                            start=True, stop=True) for hh in range(2)],
                            reads=[rkts, rvts], writes=[slu])
                        bld.emit("s", lambda e, hp=hp, cc=cc, g=g, poc0=poc0: e.copy(
                            YT[:, A_O + hp * 256 + cc * C:A_O + hp * 256 + (cc + 1) * C],
                            g["pob"][:, poc0:poc0 + C]),
                            reads=[slo], writes=[ro])
                        rput = reg(rSF, "PUT")
                        bld.emit("v", lambda e, base=base, cc=cc, g=g, puc0=puc0: e.tensor_scalar(
                            SF[:, F_PUT:F_PUT + 64], g["pub"][:, puc0:puc0 + 64],
                            SF[:, base + 16 + cc:base + 17 + cc], None, ALU.mult),
                            reads=[slu, rcol], writes=[rput])
                        bld.emit("v", [lambda e, hp=hp, cc=cc, base=base, hh=hh:
                                       e.scalar_tensor_tensor(
                            STT_[p64(hh), hp * 128 + hh * 64:hp * 128 + (hh + 1) * 64],
                            STT_[p64(hh), hp * 128 + hh * 64:hp * 128 + (hh + 1) * 64],
                            SF[p64(hh), base + 12 + cc:base + 13 + cc],
                            SF[p64(hh), F_PUT:F_PUT + 64],
                            ALU.mult, ALU.add) for hh in range(2)],
                            reads=[rput, rcol, rST], writes=[rST])
                dbg("o", YT[:, A_O:A_O + 2048], [ro.wtok], regs=[ro])

                for mc in range(NKC):
                    nm, pt_, c0 = accs[na[0] % 4]
                    na[0] += 1
                    sl = rPB[nm[:2]]
                    bld.emit("t", [lambda e, kc=kc, mc=mc, pt_=pt_, c0=c0: e.matmul(
                        pt_[:, c0:c0 + 256], wsl("Wo", kc, mc * 128, 128),
                        YT[:, A_O + kc * 256:A_O + (kc + 1) * 256],
                        start=(kc == 0), stop=(kc == NKC - 1)) for kc in range(NKC)],
                        reads=[ro, reg(rWA, "Wo")], writes=[sl])
                    tkx = bld.emit("v", lambda e, pt_=pt_, c0=c0, mc=mc, tl=tl: e.tensor_tensor(
                        SB[:, mc * TC + tl:mc * TC + tl + PT],
                        SB[:, mc * TC + tl:mc * TC + tl + PT],
                        pt_[:, c0:c0 + 256], ALU.add),
                        reads=[sl], writes=[rSB[mc]])
                    x2_done.append(tkx)
            dbg("x2", SB[:, 0:NKC * TC], x2_done, regs=list(rSB))

            pb = bld.barrier(extra=dbg_toks)
            z_done = []
            xp_emitted = [False] * NPT
            emit_xload(0, seq, t0g)
            emit_xload(1, seq, t0g)
            load_w("w1", lambda c: dW["w1"][c * 128:(c + 1) * 128, :], NKC, DFF, OFF["w1"], extra=pb)
            load_w("w2", lambda c: dW["w2"][c * 128:(c + 1) * 128, :], NKC, DFF, OFF["w2"], extra=pb)
            load_w("w3", lambda c: dW["w3"][c * 128:min(DFF, (c + 1) * 128), :], W3KC, 1024, OFF["w3"],
                   extra=pb, prows_fn=lambda c: min(128, DFF - c * 128))
            for ti in range(NPT):
                par = ti % 2
                t0 = t0g + ti * PT
                tl = ti * PT
                rh2, ru = reg(rYT, "B_H2"), reg(rYT, "B_U")
                B_H2, B_U = 0, 2048
                for kc in range(NKC):
                    rsq = reg(rSC, "SQ%d" % (kc % 2))
                    bld.emit("v", lambda e, kc=kc, tl=tl: e.tensor_tensor(
                        SC[:, S_SQ + (kc % 2) * 256:S_SQ + (kc % 2) * 256 + 256],
                        SB[:, kc * TC + tl:kc * TC + tl + 256],
                        SB[:, kc * TC + tl:kc * TC + tl + 256], ALU.mult),
                        reads=[rSB[kc]], writes=[rsq], extra=pb)
                    bld.emit("t", lambda e, kc=kc: e.matmul(
                        PS_[0:1, 0:256], CN["ones1"][:, :],
                        SC[:, S_SQ + (kc % 2) * 256:S_SQ + (kc % 2) * 256 + 256],
                        start=(kc == 0), stop=(kc == NKC - 1)),
                        reads=[rsq, rCN], writes=[rPB["PS"]] if kc in (0, NKC - 1) else [])
                rr1 = reg(rROWS, "r1")
                bld.emit("s", lambda e: e.activation(ROWS[0:1, 0:256], PS_[0:1, 0:256], AF.Ln,
                                                     bias=CN["eps"][0:1, 0:1], scale=1.0 / D),
                         reads=[rPB["PS"]], writes=[rr1])
                rb1 = reg(rRB, "r1")
                bld.emit("s", lambda e: e.activation(RB[0:1, 0:256], ROWS[0:1, 0:256], AF.Exp, scale=-0.5),
                         reads=[rr1], writes=[rb1])
                bld.emit("t", lambda e: e.matmul(PR[:, 0:256], CN["row1"][:, :], RB[0:1, 0:256],
                                                 start=True, stop=True),
                         reads=[rb1, rCN], writes=[rPB["PR"]])
                bld.emit("v", lambda e: e.tensor_copy(SC[:, S_IR:S_IR + 256], PR[:, 0:256]),
                         reads=[rPB["PR"]], writes=[reg(rSC, "IR")])
                bld.emit("v", [lambda e, kc=kc, tl=tl: e.scalar_tensor_tensor(
                    YT[:, B_H2 + kc * 256:B_H2 + (kc + 1) * 256],
                    SB[:, kc * TC + tl:kc * TC + tl + 256],
                    GF[:, kc:kc + 1], SC[:, S_IR:S_IR + 256], ALU.mult, ALU.mult)
                    for kc in range(NKC)],
                    reads=list(rSB) + [reg(rSC, "IR"), rCN], writes=[rh2])
                for a in range(2):
                    bld.emit("t", [lambda e, a=a, kc=kc, par=par: e.transpose(
                        PTB[:, kc * 128:(kc + 1) * 128],
                        XS[par][:, a * D + kc * 128: a * D + (kc + 1) * 128],
                        CN["ident"][:, :]) for kc in range(NKC)],
                        reads=[rXS[par], rCN, rh2], writes=[rPB["PTB"]], extra=pb)
                    bld.emit("v", lambda e, a=a, tl=tl: e.tensor_tensor(
                        SB[:, :].rearrange("p (c m) -> p c m", m=TC)[:, :, tl + a * 128:tl + (a + 1) * 128],
                        SB[:, :].rearrange("p (c m) -> p c m", m=TC)[:, :, tl + a * 128:tl + (a + 1) * 128],
                        PTB[:, :].rearrange("p (c n) -> p c n", n=128), ALU.add),
                        reads=[rPB["PTB"]], writes=[rSB[kc2] for kc2 in range(NKC)])
                emit_xload(ti + 2, seq, t0g)

                accsB = [("PA0", PA, 0), ("PA1", PA, 256), ("PB0", PB, 0), ("PB1", PB, 256),
                         ("PC0", PC, 0), ("PC1", PC, 256), ("PD0", PD, 0), ("PD1", PD, 256)]
                nb = [0]
                for mc in range(W3KC):
                    rows = min(128, DFF - mc * 128)
                    nm, pt_, c0 = accsB[nb[0] % 8]
                    nb[0] += 1
                    sl = rPB[nm[:2]]
                    bld.emit("t", [lambda e, kc=kc, mc=mc, pt_=pt_, c0=c0, rows=rows: e.matmul(
                        pt_[0:rows, c0:c0 + 256], wsl("w1", kc, mc * 128, rows),
                        YT[:, B_H2 + kc * 256:B_H2 + (kc + 1) * 256],
                        start=(kc == 0), stop=(kc == NKC - 1)) for kc in range(NKC)],
                        reads=[rh2, reg(rWA, "w1")], writes=[sl])
                    bld.emit("s", lambda e, pt_=pt_, c0=c0, mc=mc, rows=rows: e.activation(
                        YT[0:rows, B_U + mc * 256:B_U + (mc + 1) * 256],
                        pt_[0:rows, c0:c0 + 256], AF.Silu),
                        reads=[sl], writes=[ru] if mc == W3KC - 1 else [])
                for mc in range(W3KC):
                    rows = min(128, DFF - mc * 128)
                    nm, pt_, c0 = accsB[nb[0] % 8]
                    nb[0] += 1
                    sl = rPB[nm[:2]]
                    bld.emit("t", [lambda e, kc=kc, mc=mc, pt_=pt_, c0=c0, rows=rows: e.matmul(
                        pt_[0:rows, c0:c0 + 256], wsl("w2", kc, mc * 128, rows),
                        YT[:, B_H2 + kc * 256:B_H2 + (kc + 1) * 256],
                        start=(kc == 0), stop=(kc == NKC - 1)) for kc in range(NKC)],
                        reads=[rh2, reg(rWA, "w2")], writes=[sl])
                    bld.emit("v", lambda e, pt_=pt_, c0=c0, mc=mc, rows=rows: e.tensor_tensor(
                        YT[0:rows, B_U + mc * 256:B_U + (mc + 1) * 256],
                        YT[0:rows, B_U + mc * 256:B_U + (mc + 1) * 256],
                        pt_[0:rows, c0:c0 + 256], ALU.mult),
                        reads=[sl, ru], writes=[ru] if mc == W3KC - 1 else [])
                for mc in range(NKC):
                    nm, pt_, c0 = accsB[nb[0] % 8]
                    nb[0] += 1
                    sl = rPB[nm[:2]]
                    bld.emit("t", [lambda e, kc=kc, mc=mc, pt_=pt_, c0=c0,
                                   rows=min(128, DFF - kc * 128): e.matmul(
                        pt_[:, c0:c0 + 256], wsl("w3", kc, mc * 128, 128, prows=rows),
                        YT[0:rows, B_U + kc * 256:B_U + (kc + 1) * 256],
                        start=(kc == 0), stop=(kc == W3KC - 1)) for kc in range(W3KC)],
                        reads=[ru, reg(rWA, "w3")], writes=[sl])
                    tkz = bld.emit("v", lambda e, pt_=pt_, c0=c0, mc=mc, tl=tl: e.tensor_tensor(
                        SB[:, mc * TC + tl:mc * TC + tl + 256],
                        SB[:, mc * TC + tl:mc * TC + tl + 256], pt_[:, c0:c0 + 256], ALU.add),
                        reads=[sl], writes=[rSB[mc]])
                    z_done.append(tkz)
            dbg("z", SB[:, 0:NKC * TC], z_done, regs=list(rSB))

            pb = bld.barrier(extra=dbg_toks)
            for lam in range(NLN):
                l0 = lam * 512
                bld.emit("v", lambda e, l0=l0: e.tensor_tensor(
                    SC[:, 0:4096].rearrange("p (c n) -> p c n", n=512),
                    SB[:, :].rearrange("p (c m) -> p c m", m=TC)[:, :, l0:l0 + 512],
                    SB[:, :].rearrange("p (c m) -> p c m", m=TC)[:, :, l0:l0 + 512], ALU.mult),
                    reads=list(rSB), writes=[reg(rSC, "SQ4")], extra=pb)
                bld.emit("t", [lambda e, kc=kc, l0=l0: e.matmul(
                    PS_[0:1, 0:512], CN["ones1"][:, :], SB[:, kc * TC + l0:kc * TC + l0 + 512],
                    start=(kc == 0), stop=(kc == NKC - 1)) for kc in range(NKC)],
                    reads=list(rSB) + [rCN], writes=[rPB["PS"]], extra=pb)
                bld.emit("t", [lambda e, kc=kc: e.matmul(
                    PR[0:1, 0:512], CN["ones1"][:, :], SC[:, kc * 512:(kc + 1) * 512],
                    start=(kc == 0), stop=(kc == NKC - 1)) for kc in range(NKC)],
                    reads=[reg(rSC, "SQ4"), rCN], writes=[rPB["PR"]])
                rln = reg(rROWS, "ln")
                bld.emit("v", lambda e: e.tensor_scalar_mul(ROWS[0:1, 0:512], PS_[0:1, 0:512], 1.0 / D),
                         reads=[rPB["PS"]], writes=[rln])
                bld.emit("v", [
                    lambda e: e.tensor_tensor(ROWS[0:1, 512:1024], ROWS[0:1, 0:512],
                                              ROWS[0:1, 0:512], ALU.mult),
                    lambda e: e.tensor_scalar_mul(ROWS[0:1, 1024:1536], PR[0:1, 0:512], 1.0 / D),
                ], reads=[rln, rPB["PR"]], writes=[rln])
                bld.emit("v", lambda e: e.tensor_sub(ROWS[0:1, 1024:1536], ROWS[0:1, 1024:1536],
                                                     ROWS[0:1, 512:1024]),
                         reads=[rln], writes=[rln])
                bld.emit("s", lambda e: e.activation(ROWS[0:1, 512:1024], ROWS[0:1, 1024:1536],
                                                     AF.Ln, bias=CN["eps"][0:1, 1:2]), reads=[rln], writes=[rln])
                rlb = reg(rRB, "ln")
                bld.emit("s", [
                    lambda e: e.activation(RB[0:1, 0:512], ROWS[0:1, 512:1024], AF.Exp, scale=-0.5),
                    lambda e: e.copy(RB[0:1, 512:1024], ROWS[0:1, 0:512]),
                ], reads=[rln], writes=[rlb])
                bld.emit("t", lambda e: e.matmul(PR[:, 0:512], CN["row1"][:, :], RB[0:1, 512:1024],
                                                 start=True, stop=True),
                         reads=[rlb, rCN], writes=[rPB["PR"]])
                bld.emit("v", lambda e: e.tensor_copy(SC[:, S_MR:S_MR + 512], PR[:, 0:512]),
                         reads=[rPB["PR"]], writes=[reg(rSC, "MR")])
                bld.emit("t", lambda e: e.matmul(PS_[:, 0:512], CN["row1"][:, :], RB[0:1, 0:512],
                                                 start=True, stop=True),
                         reads=[rlb, rCN], writes=[rPB["PS"]])
                bld.emit("v", lambda e: e.tensor_copy(SC[:, S_VR:S_VR + 512], PS_[:, 0:512]),
                         reads=[rPB["PS"]], writes=[reg(rSC, "VR")])
                for kc in range(NKC):
                    bld.emit("v", lambda e, kc=kc, l0=l0: e.tensor_tensor(
                        SB[:, kc * TC + l0:kc * TC + l0 + 512],
                        SB[:, kc * TC + l0:kc * TC + l0 + 512], SC[:, S_MR:S_MR + 512], ALU.subtract),
                        reads=[reg(rSC, "MR")], writes=[rSB[kc]])
                    bld.emit("v", lambda e, kc=kc, l0=l0: e.tensor_tensor(
                        SB[:, kc * TC + l0:kc * TC + l0 + 512],
                        SB[:, kc * TC + l0:kc * TC + l0 + 512], SC[:, S_VR:S_VR + 512], ALU.mult),
                        reads=[reg(rSC, "VR")], writes=[rSB[kc]])
            dbg("y1", SB[:, 0:NKC * TC], [r.wtok for r in rSB], regs=list(rSB))

            for tt in range(NTT):
                bld.emit("t", [lambda e, tt=tt, kc=kc: e.transpose(
                    PTB[:, kc * 128:(kc + 1) * 128],
                    SB[:, kc * TC + tt * 128:kc * TC + (tt + 1) * 128],
                    CN["ident"][:, :]) for kc in range(NKC)],
                    reads=list(rSB) + [rCN], writes=[rPB["PTB"]])
                if tt % 2 == 0:
                    bld.emit("v", lambda e, tt=tt: e.tensor_copy(
                        YT[:, tt * 1024:(tt + 1) * 1024], PTB[:, :]),
                        reads=[rPB["PTB"]], writes=[rSBtm[tt]])
                else:
                    bld.emit("s", lambda e, tt=tt: e.copy(
                        YT[:, tt * 1024:(tt + 1) * 1024], PTB[:, :]),
                        reads=[rPB["PTB"]], writes=[rSBtm[tt]])
            rlg = reg(rSF, "LG")
            rpm = reg(rSC, "PERM")
            for lam in range(NLN):
                l0 = lam * 512
                bld.emit("t", [lambda e, kc=kc, l0=l0: e.matmul(
                    PG[0:NES, 0:512], CTL[:, kc * NES:(kc + 1) * NES],
                    SB[:, kc * TC + l0:kc * TC + l0 + 512],
                    start=(kc == 0), stop=(kc == NKC - 1)) for kc in range(NKC)],
                    reads=list(rSB) + [rCN], writes=[rPB["PG"]])
                bld.emit("v", lambda e, l0=l0: e.tensor_tensor(
                    SF[0:NES, F_LG + l0:F_LG + l0 + 512].rearrange("p (g n) -> p g n", n=GRP),
                    PG[0:NES, 0:512].rearrange("p (g n) -> p g n", n=GRP),
                    CN["tie"][:, :].rearrange("p n -> p () n").broadcast_to([NES, 512 // GRP, GRP]),
                    ALU.add),
                    reads=[rPB["PG"], rCN], writes=[rlg])
                bld.emit("v", lambda e, l0=l0: e.tensor_reduce(
                    SF[0:NES, F_GM + l0 // GRP:F_GM + l0 // GRP + 32],
                    SF[0:NES, F_LG + l0:F_LG + l0 + 512].rearrange("p (g n) -> p g n", n=GRP),
                    AX.X, ALU.max), reads=[rlg], writes=[rlg])
                bld.emit("v", lambda e, l0=l0: e.tensor_tensor(
                    SC[0:NES, S_PERM + l0:S_PERM + l0 + 512].rearrange("p (g n) -> p g n", n=GRP),
                    SF[0:NES, F_LG + l0:F_LG + l0 + 512].rearrange("p (g n) -> p g n", n=GRP),
                    SF[0:NES, F_GM + l0 // GRP:F_GM + l0 // GRP + 32]
                    .rearrange("p g -> p g ()").broadcast_to([NES, 32, GRP]),
                    ALU.is_equal), reads=[rlg], writes=[rpm])
            dbg("perm", SC[0:NES, S_PERM:S_PERM + TC], [rpm.wtok], regs=[rpm])
            rpt = reg(rSC, "PERMT")
            for tt in range(NTT):
                bld.emit("t", lambda e, tt=tt: e.transpose(
                    PTB[:, 0:64], SC[0:NES, S_PERM + tt * 128:S_PERM + (tt + 1) * 128],
                    CN["ident"][0:NES, 0:NES]), reads=[rpm, rCN], writes=[rPB["PTB"]])
                bld.emit("v", lambda e, tt=tt: e.tensor_copy(
                    SC[:, S_PERMT + tt * 64:S_PERMT + (tt + 1) * 64], PTB[:, 0:64]),
                    reads=[rPB["PTB"]], writes=[rpt] if tt == NTT - 1 else [])

            pc_ = bld.barrier(extra=dbg_toks)
            for blk in range(NEB):
                buf = blk % 2
                e0 = (blk * EB) // ESL
                pre2 = pc_ if blk < 2 else [bld.tok("t")]
                rf1 = load_w("f1_%d" % buf, lambda c, e0=e0: dW["f1"][c * 128:(c + 1) * 128, e0:e0 + 4, :, :]
                             .rearrange("p e s f -> p (e s f)"), NKC, EB * ESZ, OFF["f1_%d" % buf],
                             extra=pre2)
                rf2 = reg(rWA, "f2_%d" % buf)
                bld.emit("g", lambda e, e0=e0, buf=buf: e.dma_start(
                    out=WA[:, OFF["f2_%d" % buf]:OFF["f2_%d" % buf] + 8192],
                    in_=dW["f2"][e0:e0 + 4, :, :, :].rearrange("e s f d -> (e s) f d")
                    .rearrange("(j two) f d -> (two f) j d", two=2)),
                    writes=[rf2], extra=pre2)

                rsel = reg(rWA, "sel")
                rP = reg(rSC, "P")
                rPT2 = reg(rSC, "PT2")
                sel_fin = []

                def gath_p(tt2):
                    if tt2 >= NTT:
                        return
                    gdst = S_P if tt2 % 2 == 0 else 0
                    rP_ = reg(rSC, "P" if tt2 % 2 == 0 else "P1")
                    bld.emit("v", [lambda e, tt2=tt2, blk=blk, parity=parity, gdst=gdst: e.tensor_tensor(
                        SC[:, gdst + parity * 64:gdst + parity * 64 + 64].rearrange("p (g j) -> p g j", j=8),
                        SC[:, S_PERMT + tt2 * 64 + blk * 16:S_PERMT + tt2 * 64 + (blk + 1) * 16]
                        .rearrange("p (j two) -> p j two", two=2)[:, :, parity]
                        .rearrange("p j -> p () j").broadcast_to([128, 8, 8]),
                        CN["bdP"][:, :].rearrange("p g -> p g ()").broadcast_to([128, 8, 8]),
                        ALU.mult) for parity in range(2)],
                        reads=[rpt, rCN], writes=[rP_])

                gath_p(0)
                gath_p(1)
                for tt in range(NTT):
                    gdst = S_P if tt % 2 == 0 else 0
                    rP_ = reg(rSC, "P" if tt % 2 == 0 else "P1")
                    for half in range(2):
                        pt_ = PA if half == 0 else PB
                        slg = rPB["PA" if half == 0 else "PB"]
                        bld.emit("t", [lambda e, tt=tt, q=q, half=half, pt_=pt_, gdst=gdst: e.matmul(
                            pt_[:, q * 128:(q + 1) * 128],
                            YT[:, tt * 1024 + (half * 4 + q) * 128:tt * 1024 + (half * 4 + q + 1) * 128],
                            SC[:, gdst:gdst + 128], start=True, stop=True) for q in range(4)],
                            reads=[rSBtm[tt], rP_], writes=[slg])
                        if half == 1:
                            gath_p(tt + 2)
                        eng = "v" if half == 0 else "s"
                        selw = WA[:, OFF["sel"]:OFF["sel"] + 8192].rearrange("p (d n) -> p d n", n=1024)
                        if eng == "v":
                            tk = bld.emit("v", lambda e, tt=tt, half=half, pt_=pt_, selw=selw: e.tensor_copy(
                                selw[:, half * 4:half * 4 + 4, tt * 128:(tt + 1) * 128],
                                pt_[:, :].rearrange("p (d n) -> p d n", n=128)),
                                reads=[slg], extra=rsel.rtoks + [rsel.wtok])
                        else:
                            tk = bld.emit("s", lambda e, tt=tt, half=half, pt_=pt_, selw=selw: e.copy(
                                selw[:, half * 4:half * 4 + 4, tt * 128:(tt + 1) * 128],
                                pt_[:, :].rearrange("p (d n) -> p d n", n=128)),
                                reads=[slg], extra=rsel.rtoks + [rsel.wtok])
                        sel_fin.append(tk)
                rsel.wtok = sel_fin[-1]
                rsel.rtoks = []
                if blk == 0:
                    dbg("sel", WA[:, OFF["sel"]:OFF["sel"] + 8192], sel_fin, regs=[rsel])

                rinter = reg(rWA, "inter")
                selv = WA[:, OFF["sel"]:OFF["sel"] + 8192].rearrange(
                    "p (d t pr g j2) -> p d t pr g j2", d=8, t=8, pr=2, g=8)
                inter_fin = []

                def emit_f1(j):
                    es = blk * EB + j
                    parity, jj = es % 2, j // 2
                    pcb = PC if j % 2 == 0 else PA
                    slf = rPB["PC" if j % 2 == 0 else "PA"]
                    pbase = parity * 64
                    bld.emit("t", [lambda e, kc=kc, j=j, buf=buf, parity=parity, jj=jj, pbase=pbase, pcb=pcb: e.matmul(
                        pcb[pbase:pbase + ESZ, (j % 4) * 128:(j % 4) * 128 + 64],
                        WA[:, OFF["f1_%d" % buf] + kc * 1024 + j * ESZ:
                           OFF["f1_%d" % buf] + kc * 1024 + (j + 1) * ESZ],
                        selv[:, kc, :, parity, :, jj],
                        start=(kc == 0), stop=(kc == NKC - 1)) for kc in range(NKC)],
                        reads=[rsel, rf1], writes=[slf], extra=sel_fin)
                    rin_ = reg(rSC, "INN%d" % (j % 2))
                    bld.emit("s", lambda e, j=j, es=es, pbase=pbase, pcb=pcb: e.activation(
                        SC[pbase:pbase + ESZ, S_INN + (j % 2) * 64:S_INN + (j % 2) * 64 + 64],
                        pcb[pbase:pbase + ESZ, (j % 4) * 128:(j % 4) * 128 + 64],
                        AF.Relu, bias=FBIA[:, es:es + 1]), reads=[slf, rCN], writes=[rin_])

                emit_f1(0)
                for j in range(EB):
                    es = blk * EB + j
                    parity, jj = es % 2, j // 2
                    pbase = parity * 64
                    if j + 1 < EB:
                        emit_f1(j + 1)
                    rin_ = reg(rSC, "INN%d" % (j % 2))
                    for half in range(2):
                        pt_ = PD if half == 0 else PG
                        sli = rPB["PD"] if half == 0 else rPB["PG"]
                        bld.emit("t", lambda e, j=j, buf=buf, parity=parity, jj=jj, half=half, pt_=pt_, pbase=pbase: e.matmul(
                            pt_[0:64, 0:512],
                            SC[pbase:pbase + ESZ, S_INN + (j % 2) * 64:S_INN + (j % 2) * 64 + 64],
                            WA[p64(parity), OFF["f2_%d" % buf] + jj * 1024 + half * 512:
                               OFF["f2_%d" % buf] + jj * 1024 + (half + 1) * 512],
                            start=True, stop=True),
                            reads=[rin_, rf2], writes=[sli])
                        tk = bld.emit("v", lambda e, parity=parity, jj=jj, half=half, pt_=pt_: e.tensor_copy(
                            WA[p64(parity), OFF["inter"] + jj * 1024 + half * 512:
                               OFF["inter"] + jj * 1024 + (half + 1) * 512],
                            pt_[0:64, 0:512]),
                            reads=[sli], extra=rinter.rtoks + [rinter.wtok])
                        inter_fin.append(tk)
                rinter.wtok = inter_fin[-1]
                rinter.rtoks = []
                if blk == 0:
                    dbg("inter", WA[:, OFF["inter"]:OFF["inter"] + 8192], inter_fin, regs=[rinter])

                stg_emitted = [False] * NTT

                def emit_stage(tt2):
                    if tt2 >= NTT or stg_emitted[tt2]:
                        return
                    stg_emitted[tt2] = True
                    sp2 = S_STG if tt2 % 2 == 0 else 1024
                    bld.emit("y", [lambda e, tt2=tt2, parity=parity, sp2=sp2: e.dma_start(
                        out=SC[p64(parity), sp2:sp2 + 1024],
                        in_=WA[parity * 64 + tt2 * 8:parity * 64 + (tt2 + 1) * 8,
                               OFF["inter"]:OFF["inter"] + 8192]
                        .rearrange("p (j d) -> p j d", d=1024)) for parity in range(2)],
                        reads=[rinter], writes=[reg(rSC, "STG%d" % (tt2 % 2))], extra=inter_fin)

                emit_stage(0)
                emit_stage(1)

                def scat_prep(tt2):
                    if tt2 >= NTT:
                        return
                    par2 = tt2 % 2
                    pdst = S_P if par2 == 0 else 0
                    tdst = S_PT2 if par2 == 0 else 128
                    tb2 = PTB if par2 == 0 else PGB
                    tb2n = "PTB" if par2 == 0 else "PG"
                    rP_ = reg(rSC, "P" if par2 == 0 else "P1")
                    bld.emit("v", [lambda e, tt2=tt2, blk=blk, parity=parity, pdst=pdst: e.tensor_tensor(
                        SC[:, pdst + parity * 64:pdst + parity * 64 + 64].rearrange("p (g j) -> p g j", j=8),
                        SC[:, S_PERMT + tt2 * 64 + blk * 16:S_PERMT + tt2 * 64 + (blk + 1) * 16]
                        .rearrange("p (j two) -> p j two", two=2)[:, :, parity]
                        .rearrange("p j -> p () j").broadcast_to([128, 8, 8]),
                        CN["bdP"][:, :].rearrange("p g -> p g ()").broadcast_to([128, 8, 8]),
                        ALU.mult) for parity in range(2)],
                        reads=[rpt, rCN], writes=[rP_], extra=pc_)
                    bld.emit("t", lambda e, pdst=pdst, tb2=tb2: e.transpose(
                        tb2[:, 0:128], SC[:, pdst:pdst + 128], CN["ident"][:, :]),
                        reads=[rP_, rCN], writes=[rPB[tb2n]])
                    bld.emit("v", lambda e, tdst=tdst, tb2=tb2: e.tensor_copy(
                        SC[:, tdst:tdst + 128], tb2[:, 0:128]),
                        reads=[rPB[tb2n]],
                        writes=[reg(rSC, "PT2" if par2 == 0 else "PT21")])

                scat_prep(0)
                scat_prep(1)
                for tt in range(NTT):
                    rstg = reg(rSC, "STG%d" % (tt % 2))
                    stoff = S_STG if tt % 2 == 0 else 1024
                    tdst = S_PT2 if tt % 2 == 0 else 128
                    rPT2_ = reg(rSC, "PT2" if tt % 2 == 0 else "PT21")
                    for half in range(2):
                        pt_ = PR if half == 0 else PS_
                        slx = rPB["PR"] if half == 0 else rPB["PS"]
                        bld.emit("t", lambda e, half=half, pt_=pt_, stoff=stoff, tdst=tdst: e.matmul(
                            pt_[:, 0:512], SC[:, tdst:tdst + 128],
                            SC[:, stoff + half * 512:stoff + (half + 1) * 512],
                            start=True, stop=True),
                            reads=[rPT2_, rstg], writes=[slx])
                        if blk == 0:
                            bld.emit("v", lambda e, tt=tt, half=half, pt_=pt_: e.tensor_tensor(
                                SB[:, tt * 1024 + half * 512:tt * 1024 + (half + 1) * 512],
                                YT[:, tt * 1024 + half * 512:tt * 1024 + (half + 1) * 512],
                                pt_[:, 0:512], ALU.add),
                                reads=[slx, rSBtm[tt]], writes=[rSB[tt]])
                        else:
                            bld.emit("v", lambda e, tt=tt, half=half, pt_=pt_: e.tensor_tensor(
                                SB[:, tt * 1024 + half * 512:tt * 1024 + (half + 1) * 512],
                                SB[:, tt * 1024 + half * 512:tt * 1024 + (half + 1) * 512],
                                pt_[:, 0:512], ALU.add),
                                reads=[slx], writes=[rSB[tt]])
                    emit_stage(tt + 2)
                    scat_prep(tt + 2)
            dbg("moeacc", SB[:, 0:NKC * TC], [r.wtok for r in rSB], regs=list(rSB))

            for tt in range(NTT):
                racc = rSB[tt]
                rsq2 = reg(rSC, "SQL2")
                rlr = reg(rROWS, "l2r")
                bld.emit("v", [
                    lambda e, tt=tt: e.reduce_sum(ROWS[:, 0:1], SB[:, tt * 1024:(tt + 1) * 1024], AX.X),
                    lambda e, tt=tt: e.tensor_tensor(SC[:, 0:1024], SB[:, tt * 1024:(tt + 1) * 1024],
                                                     SB[:, tt * 1024:(tt + 1) * 1024], ALU.mult),
                ], reads=[racc], writes=[rsq2, rlr])
                bld.emit("v", lambda e: e.reduce_sum(ROWS[:, 1:2], SC[:, 0:1024], AX.X),
                         reads=[rsq2], writes=[rlr])
                bld.emit("v", lambda e: e.tensor_scalar_mul(ROWS[:, 2:3], ROWS[:, 0:1], 1.0 / D),
                         reads=[rlr], writes=[rlr])
                bld.emit("v", [
                    lambda e: e.tensor_tensor(ROWS[:, 3:4], ROWS[:, 2:3],
                                              ROWS[:, 2:3], ALU.mult),
                    lambda e: e.tensor_scalar_mul(ROWS[:, 4:5], ROWS[:, 1:2], 1.0 / D),
                ], reads=[rlr], writes=[rlr])
                bld.emit("v", lambda e: e.tensor_sub(ROWS[:, 4:5], ROWS[:, 4:5],
                                                     ROWS[:, 3:4]),
                         reads=[rlr], writes=[rlr])
                bld.emit("s", lambda e: e.activation(ROWS[:, 5:6], ROWS[:, 4:5],
                                                     AF.Ln, bias=CN["eps"][:, 1:2]), reads=[rlr], writes=[rlr])
                bld.emit("s", lambda e: e.activation(ROWS[:, 6:7], ROWS[:, 5:6],
                                                     AF.Exp, scale=-0.5), reads=[rlr], writes=[rlr])
                ryo = reg(rSF, "YO%d" % (tt % 2))
                bld.emit("v", lambda e, tt=tt: e.tensor_scalar(
                    SF[:, F_YO + (tt % 2) * 1024:F_YO + (tt % 2) * 1024 + 1024],
                    SB[:, tt * 1024:(tt + 1) * 1024],
                    ROWS[:, 2:3], ROWS[:, 6:7], ALU.subtract, ALU.mult),
                    reads=[racc, rlr], writes=[ryo])
                ot = bld.emit("y", lambda e, tt=tt, seq=seq, t0g=t0g: e.dma_start(
                    out=dy[seq, t0g + tt * 128:t0g + (tt + 1) * 128, :],
                    in_=SF[:, F_YO + (tt % 2) * 1024:F_YO + (tt % 2) * 1024 + 1024]),
                    reads=[ryo])
                out_toks.append(ot)

    bld.waits("y", out_toks + dbg_toks)

    with nc.Block() as block:
        for eng, meth in (("t", block.tensor), ("v", block.vector), ("s", block.scalar),
                          ("g", block.gpsimd), ("y", block.sync)):
            def runner(e, eng=eng):
                for fn in bld.ops[eng]:
                    fn(e)
            meth(runner)
    stack.close()
    return nc


def _host_consts():
    bf = ml_dtypes.bfloat16
    ident = np.eye(128, dtype=np.float32)
    maskT = np.zeros((128, 128), dtype=np.float32)
    for s_ in range(128):
        maskT[s_, s_:] = 1.0
    hsel = np.zeros((128, NKC * 16), dtype=np.float32)
    hrep = np.zeros((16, NKC * 128), dtype=np.float32)
    for hp in range(NKC):
        for p in range(128):
            h = 2 * hp + (1 if p >= 64 else 0)
            hsel[p, hp * 16 + h] = 1.0
            hrep[h, hp * 128 + p] = 1.0
    ones1 = np.ones((128, 1), dtype=np.float32)
    row1 = np.ones((1, 128), dtype=np.float32)
    bdP = np.zeros((128, 8), dtype=np.float32)
    for t_ in range(128):
        bdP[t_, t_ // 16] = 1.0
    maskT2 = np.zeros((128, 64), dtype=np.float32)
    for p_ in range(128):
        maskT2[p_, p_ % 64:] = 1.0
    seye = np.concatenate([np.eye(64, dtype=np.float32)] * 2, axis=1)
    bdm = np.zeros((128, 128), dtype=np.float32)
    bdm[0:64, 0:64] = 1.0
    bdm[64:128, 64:128] = 1.0
    tie = np.zeros((NES, GRP), dtype=np.float32)
    tie[:, :] = np.linspace(0.0, 1e-6, GRP, dtype=np.float32)[None, :]
    c = {"c_ident": ident, "c_maskT": maskT, "c_hsel": hsel, "c_hrep": hrep,
         "c_ones1": ones1, "c_row1": row1, "c_bdP": bdP, "c_maskT2": maskT2,
         "c_seye": seye, "c_bdm": bdm}
    c = {k: v.astype(bf) for k, v in c.items()}
    c["c_tie"] = tie
    eps = np.zeros((128, 3), dtype=np.float32)
    eps[:, 0] = EPS_RMS
    eps[:, 1] = EPS_LN
    eps[:, 2] = 1.0
    c["c_eps"] = eps
    return c


def kernel(L0_attn_x, L0_attn_Wq, L0_attn_Wk, L0_attn_Wv, L0_attn_Wg, L0_attn_Wo,
           L0_attn_w1, L0_attn_w2, L0_attn_w3, L0_attn_g_attn, L0_attn_g_ffn,
           L0_ffn_controller, L0_ffn_f1, L0_ffn_f2, L0_ffn_bias):
    nc = build_nc()
    consts = _host_consts()
    x = np.ascontiguousarray(np.asarray(L0_attn_x, dtype=np.float32))
    w = {"Wq": L0_attn_Wq, "Wk": L0_attn_Wk, "Wv": L0_attn_Wv, "Wg": L0_attn_Wg,
         "Wo": L0_attn_Wo, "w1": L0_attn_w1, "w2": L0_attn_w2, "w3": L0_attn_w3,
         "ga": L0_attn_g_attn, "gf": L0_attn_g_ffn, "ctl": L0_ffn_controller,
         "f1": L0_ffn_f1, "f2": L0_ffn_f2, "fb": L0_ffn_bias}
    w = {k: np.ascontiguousarray(np.asarray(v, dtype=np.float32)) for k, v in w.items()}
    in_maps = []
    for cid in range(NCORE):
        m = {"x": x[cid * SPC:(cid + 1) * SPC]}
        m.update(w)
        m.update(consts)
        in_maps.append(m)
    res = run_bass_kernel_spmd(nc, in_maps, core_ids=list(range(NCORE)))
    return np.concatenate([r["y"] for r in res.results], axis=0)


# version 19
# speedup vs baseline: 1.0517x; 1.0049x over previous
import contextlib
import numpy as np
import ml_dtypes
import concourse.bass as bass
import concourse.mybir as mybir
from concourse.bass_utils import run_bass_kernel_spmd

F32 = mybir.dt.float32
BF16 = mybir.dt.bfloat16
AF = mybir.ActivationFunctionType
ALU = mybir.AluOpType
AX = mybir.AxisListType

B, T, D = 16, 2048, 1024
H, DK, DV = 16, 64, 64
DFF = 2736
NE, ESL, ESZ = 16, 4, 64
NES = NE * ESL
GRP = 16
EPS_RMS, EPS_LN = 1e-6, 1e-5

NCORE = 8
SPC = B // NCORE
TC = 1024
NCH = T // TC
PT = 256
NPT = TC // PT
C = 64
NC_T = PT // C
NKC = D // 128
W3KC = (DFF + 127) // 128
EB = 16
NEB = NES // EB
NTT = TC // 128
NLN = TC // 512

DBG = {"name": None, "shape": None}


class Reg:
    __slots__ = ("wtok", "rtoks")

    def __init__(self):
        self.wtok = None
        self.rtoks = []


class Bld:
    ENGS = ("t", "v", "s", "g", "y")
    NLANES = {"g": 20, "y": 8}

    def __init__(self, nc, stack):
        self.nc = nc
        self.ops = {e: [] for e in self.ENGS}
        self.cnt = {e: 0 for e in self.ENGS}
        self.waited = {e: {} for e in self.ENGS}
        self.sems = {e: stack.enter_context(nc.semaphore("sem_" + e)) for e in ("t", "v", "s", "p")}
        self.cnt["p"] = 0
        self.rr = {"g": 0, "y": 0}
        for e in ("g", "y"):
            for i in range(self.NLANES[e]):
                k = e + str(i)
                self.sems[k] = stack.enter_context(nc.semaphore("sem_" + k))
                self.cnt[k] = 0
                for c in self.waited.values():
                    pass

    def _wait(self, eng, pe, v):
        if self.waited[eng].get(pe, 0) >= v:
            return
        self.waited[eng][pe] = v
        sem = self.sems[pe]
        self.ops[eng].append(lambda e, sem=sem, v=v: e.wait_ge(sem, v))

    def _dma_lane(self, eng):
        i = self.rr[eng] % self.NLANES[eng]
        self.rr[eng] += 1
        return eng + str(i)

    def waits(self, eng, toks):
        best = {}
        for t in toks:
            if t is None:
                continue
            pe, v = t
            if best.get(pe, 0) < v:
                best[pe] = v
        for pe, v in best.items():
            self._wait(eng, pe, v)

    def emit(self, eng, fns, reads=(), writes=(), extra=()):
        if not isinstance(fns, (list, tuple)):
            fns = [fns]
        def _add(n, t):
            if isinstance(t, list):
                n.extend(t)
            elif t is not None:
                n.append(t)
        need = []
        for t_ in extra:
            _add(need, t_)
        for r in reads:
            _add(need, r.wtok)
        for w in writes:
            _add(need, w.wtok)
            need.extend(w.rtoks)
        if eng == "p":
            self.waits("g", need)
            sem = self.sems["p"]
            for i, fn in enumerate(fns):
                if i == len(fns) - 1:
                    self.ops["g"].append(lambda e, fn=fn, sem=sem: fn(e).then_inc(sem, 1))
                else:
                    self.ops["g"].append(fn)
            self.cnt["p"] += 1
            tok = ("p", self.cnt["p"])
            for r in reads:
                r.rtoks.append(tok)
            for w in writes:
                w.wtok = tok
                w.rtoks = []
            return tok
        self.waits(eng, need)
        if eng in ("g", "y"):
            lane = self._dma_lane(eng)
            sem = self.sems[lane]
            self._wait(eng, lane, self.cnt[lane])
            for fn in fns:
                self.ops[eng].append(lambda e, fn=fn, sem=sem: fn(e).then_inc(sem, 16))
            self.cnt[lane] += 16 * len(fns)
            tok = (lane, self.cnt[lane])
        else:
            sem = self.sems[eng]
            for i, fn in enumerate(fns):
                if i == len(fns) - 1:
                    self.ops[eng].append(lambda e, fn=fn, sem=sem: fn(e).then_inc(sem, 1))
                else:
                    self.ops[eng].append(fn)
            self.cnt[eng] += 1
            tok = (eng, self.cnt[eng])
        for r in reads:
            r.rtoks.append(tok)
        for w in writes:
            w.wtok = tok
            w.rtoks = []
        return tok

    def tok(self, eng):
        return (eng, self.cnt[eng])

    def barrier(self, extra=()):
        pb = [self.tok("t"), self.tok("v"), self.tok("s"), self.tok("p")] + list(extra)
        for e in ("t", "v", "s"):
            self.waits(e, pb)
        return pb


def build_nc():
    nc = bass.Bass("TRN2", target_bir_lowering=False)
    stack = contextlib.ExitStack()

    dx = nc.dram_tensor("x", [SPC, T, D], F32, kind="ExternalInput").ap()
    dy = nc.dram_tensor("y", [SPC, T, D], F32, kind="ExternalOutput").ap()
    dW = {}
    for nm, shp in [("Wq", [D, D]), ("Wk", [D, D]), ("Wv", [D, D]), ("Wg", [D, D]),
                    ("Wo", [D, D]), ("w1", [D, DFF]), ("w2", [D, DFF]), ("w3", [DFF, D]),
                    ("ga", [D]), ("gf", [D]), ("ctl", [D, NE, ESL]),
                    ("f1", [D, NE, ESL, ESZ]), ("f2", [NE, ESL, ESZ, D]),
                    ("fb", [NE, ESL, ESZ])]:
        dW[nm] = nc.dram_tensor(nm, shp, F32, kind="ExternalInput").ap()
    dC = {}
    for nm, shp in [("ident", [128, 128]), ("maskT", [128, 128]), ("hsel", [128, NKC * 16]),
                    ("hrep", [16, NKC * 128]), ("ones1", [128, 1]), ("row1", [1, 128]),
                    ("bdP", [128, 8]), ("maskT2", [128, 64]), ("seye", [64, 128]),
                    ("bdm", [128, 128])]:
        dC[nm] = nc.dram_tensor("c_" + nm, shp, BF16, kind="ExternalInput").ap()
    dC["tie"] = nc.dram_tensor("c_tie", [NES, GRP], F32, kind="ExternalInput").ap()
    dC["eps"] = nc.dram_tensor("c_eps", [128, 3], F32, kind="ExternalInput").ap()
    ddbg = None
    if DBG["name"]:
        ddbg = nc.dram_tensor("dbg", DBG["shape"], DBG.get("dtype", F32), kind="ExternalOutput").ap()

    sbt = lambda name, shape, dt: stack.enter_context(nc.sbuf_tensor(name, shape, dt))
    WA = sbt("WA", [128, 66304], BF16)
    SB = sbt("SBs", [128, NKC * TC], BF16)
    YT = sbt("YT", [128, NTT * 1024], BF16)
    SC = sbt("SCb", [128, 4992], BF16)
    SF = sbt("SFf", [128, 3334], F32)
    STT_ = sbt("STATE", [128, NKC * 128 + 128], BF16)
    SPARE_ST = NKC * 128
    CN = {k: sbt("k" + k, list(dC[k].shape), dC[k].dtype) for k in dC}
    GA = sbt("gacol", [128, NKC], F32)
    GF = sbt("gfcol", [128, NKC], F32)
    FBIA = sbt("fbias", [ESZ, NES], F32)
    CTL = sbt("ctlw", [128, NKC * NES], BF16)
    ROWS = sbt("rows", [128, 1536], F32)
    RB = sbt("rowsb", [128, 1024], BF16)
    XS = [sbt("XS0", [128, 2 * D], BF16), sbt("XS1", [128, 2 * D], BF16)]

    pst = lambda name, shape, dt: stack.enter_context(nc.psum_tensor(name, shape, dt))
    PA = pst("PA", [128, 512], F32)
    PB = pst("PB", [128, 512], F32)
    PC = pst("PC", [128, 512], F32)
    PDB = pst("PDB", [128, 1024], BF16)
    PD = PDB.bitcast(F32)
    PS_ = pst("PS", [128, 512], F32)
    PR = pst("PR", [128, 512], F32)
    PRB = PR.bitcast(BF16)
    PG = pst("PG", [128, 512], F32)
    PGB = PG.bitcast(BF16)
    PTB = pst("PTB", [128, 1024], BF16)

    bld = Bld(nc, stack)

    S_SQ, S_IR, S_TMP, S_SCM, S_VH, S_EQ, S_EK = 0, 1024, 1280, 1536, 2048, 2304, 2816
    S_KTS = 1792
    S_VTS = 2048
    S_SMM = 3328
    S_KBD, S_VBD = 3456, 3968
    S_MR, S_VR = 1024, 1536
    S_PERM, S_PERMT = 2048, 3072
    S_P, S_PT2, S_INN, S_STG = 3584, 3712, 3840, 3968
    F_SP, F_SCUM, F_COLS = 0, 512, 1024
    F_LG, F_GM = 0, 1024
    F_YO = 1216
    F_PUT = 3270

    rWA = {}
    rXS = [Reg(), Reg()]
    rCN = Reg()
    rSB = [Reg() for _ in range(NKC)]
    rSBtm = [Reg() for _ in range(NTT)]
    rYT, rSC, rSF, rROWS, rRB = {}, {}, {}, {}, {}
    rST = Reg()
    rSTb = [Reg(), Reg()]
    rPB = {k: Reg() for k in ("PA", "PB", "PC", "PD", "PS", "PR", "PG", "PTB")}

    def reg(d, key):
        if key not in d:
            d[key] = Reg()
        return d[key]

    for k in dC:
        bld.emit("g", lambda e, a=CN[k], b=dC[k]: e.dma_start(out=a[:, :], in_=b), writes=[rCN])
    bld.emit("g", lambda e: e.dma_start(out=GA[:, :], in_=dW["ga"].rearrange("(c p) -> p c", p=128), allow_slow_non_contiguous=True), writes=[rCN])
    bld.emit("g", lambda e: e.dma_start(out=GF[:, :], in_=dW["gf"].rearrange("(c p) -> p c", p=128), allow_slow_non_contiguous=True), writes=[rCN])
    bld.emit("g", lambda e: e.dma_start(out=FBIA[:, :], in_=dW["fb"].rearrange("e s f -> f (e s)"), allow_slow_non_contiguous=True), writes=[rCN])
    bld.emit("g", lambda e: e.dma_start(out=CTL[:, :], in_=dW["ctl"].rearrange("(c p) e s -> p c (e s)", p=128)), writes=[rCN])

    OFF = {"Wq": 0, "Wk": 8192, "Wv": 16384, "Wg": 24576, "Wo": 32768,
           "w2": 0, "w3": 21888, "w1": 44416,
           "f1_0": 0, "f1_1": 8192, "f2_0": 16384, "f2_1": 24576,
           "sel": 32768, "inter": 49152}
    WCOLS = {"Wq": 1024, "Wk": 1024, "Wv": 1024, "Wg": 1024, "Wo": 1024,
             "w1": DFF, "w2": DFF, "w3": 1024}

    def load_w(name, src_fn, nchunks, ncols, off, extra=(), prows_fn=None):
        r = reg(rWA, name)
        war = (r.wtok if isinstance(r.wtok, list) else [r.wtok]) + r.rtoks
        toks = []
        for c in range(nchunks):
            pr = prows_fn(c) if prows_fn else 128
            toks.append(bld.emit("g", lambda e, c=c, pr=pr: e.dma_start(
                out=WA[0:pr, off + c * ncols: off + (c + 1) * ncols], in_=src_fn(c)),
                extra=(list(extra) + war) if c == 0 else []))
        r.wtok = toks
        r.rtoks = []
        return r

    def wsl(name, kc, col0, ncols, prows=128):
        o = OFF[name] + kc * WCOLS[name] + col0
        return WA[0:prows, o:o + ncols]

    def p64(hh):
        return slice(hh * 64, (hh + 1) * 64)

    dbg_toks = []

    def dbg(name, ap, deps, regs=()):
        if DBG["name"] != name or ddbg is None:
            return
        dbg_toks.append(bld.emit("g", lambda e: e.dma_start(out=ddbg, in_=ap),
                                 reads=list(regs), extra=deps))

    out_toks = []
    first_phase = True

    for seq in range(SPC):
        for ch in range(NCH):
            t0g = ch * TC
            pb = [] if first_phase else bld.barrier(extra=dbg_toks)
            first_phase = False
            if ch == 0:
                bld.emit("v", lambda e: e.memset(STT_[:, :], 0),
                         writes=[rSTb[0], rSTb[1]] + [reg(rYT, "ST%d" % hp) for hp in range(NKC)])
            bld.emit("v", lambda e: e.memset(SC[:, S_KBD:S_KBD + 512], 0),
                     writes=[reg(rSC, "KBD")], extra=pb)

            x2_done = []
            xp_emitted = [False] * NPT

            def emit_xload(ti2, seq2, base2):
                if ti2 >= NPT or xp_emitted[ti2]:
                    return
                xp_emitted[ti2] = True
                par2 = ti2 % 2
                t02 = base2 + ti2 * PT
                xwar = [rXS[par2].wtok] + rXS[par2].rtoks
                xtoks = []
                for a2 in range(2):
                    xtoks.append(bld.emit("g", lambda e, par2=par2, seq2=seq2, t02=t02, a2=a2: e.dma_start(
                        out=XS[par2][:, a2 * D:(a2 + 1) * D],
                        in_=dx[seq2, t02 + a2 * 128:t02 + (a2 + 1) * 128, :]),
                        extra=xwar if a2 == 0 else []))
                rXS[par2].wtok = xtoks
                rXS[par2].rtoks = []

            emit_xload(0, seq, t0g)
            emit_xload(1, seq, t0g)
            for wn in ("Wq", "Wk", "Wv", "Wg", "Wo"):
                load_w(wn, lambda c, wn=wn: dW[wn][c * 128:(c + 1) * 128, :], NKC, 1024,
                       OFF[wn], extra=pb)
            for ti in range(NPT):
                par = ti % 2
                t0 = t0g + ti * PT
                tl = ti * PT
                rh = reg(rYT, "A_H")
                rqh, rkh = reg(rYT, "A_QH"), reg(rYT, "A_KH")
                rvf, ro = reg(rYT, "A_VF"), reg(rYT, "A_H")
                A_H, A_QH, A_KH, A_O, A_VF = 0, 2048, 4096, 0, 6144
                for a in range(2):
                    bld.emit("t", [lambda e, a=a, kc=kc, par=par: e.transpose(
                        PTB[:, kc * 128:(kc + 1) * 128],
                        XS[par][:, a * D + kc * 128: a * D + (kc + 1) * 128],
                        CN["ident"][:, :]) for kc in range(NKC)],
                        reads=[rXS[par], rCN], writes=[rPB["PTB"]], extra=pb)
                    bld.emit("v", lambda e, a=a, tl=tl: e.tensor_copy(
                        SB[:, :].rearrange("p (c m) -> p c m", m=TC)[:, :, tl + a * 128:tl + (a + 1) * 128],
                        PTB[:, :].rearrange("p (c n) -> p c n", n=128)),
                        reads=[rPB["PTB"]],
                        writes=[rSB[kc2] for kc2 in range(NKC)] if a == 1 else [], extra=pb)
                emit_xload(ti + 2, seq, t0g)
                for kc in range(NKC):
                    rsq = reg(rSC, "SQ%d" % (kc % 2))
                    bld.emit("v", lambda e, kc=kc, tl=tl: e.tensor_tensor(
                        SC[:, S_SQ + (kc % 2) * 256:S_SQ + (kc % 2) * 256 + 256],
                        SB[:, kc * TC + tl:kc * TC + tl + 256],
                        SB[:, kc * TC + tl:kc * TC + tl + 256], ALU.mult),
                        reads=[rSB[kc]], writes=[rsq])
                    bld.emit("t", lambda e, kc=kc: e.matmul(
                        PS_[0:1, 0:256], CN["ones1"][:, :],
                        SC[:, S_SQ + (kc % 2) * 256:S_SQ + (kc % 2) * 256 + 256],
                        start=(kc == 0), stop=(kc == NKC - 1)),
                        reads=[rsq, rCN], writes=[rPB["PS"]] if kc in (0, NKC - 1) else [])
                rr1 = reg(rROWS, "r1")
                bld.emit("s", lambda e: e.activation(ROWS[0:1, 0:256], PS_[0:1, 0:256], AF.Ln,
                                                     bias=CN["eps"][0:1, 0:1], scale=1.0 / D),
                         reads=[rPB["PS"]], writes=[rr1])
                rb1 = reg(rRB, "r1")
                bld.emit("s", lambda e: e.activation(RB[0:1, 0:256], ROWS[0:1, 0:256], AF.Exp, scale=-0.5),
                         reads=[rr1], writes=[rb1])
                bld.emit("t", lambda e: e.matmul(PR[:, 0:256], CN["row1"][:, :], RB[0:1, 0:256],
                                                 start=True, stop=True),
                         reads=[rb1, rCN], writes=[rPB["PR"]])
                bld.emit("v", lambda e: e.tensor_copy(SC[:, S_IR:S_IR + 256], PR[:, 0:256]),
                         reads=[rPB["PR"]], writes=[reg(rSC, "IR")])
                bld.emit("v", [lambda e, kc=kc, tl=tl: e.scalar_tensor_tensor(
                    YT[:, A_H + kc * 256:A_H + (kc + 1) * 256],
                    SB[:, kc * TC + tl:kc * TC + tl + 256],
                    GA[:, kc:kc + 1], SC[:, S_IR:S_IR + 256], ALU.mult, ALU.mult)
                    for kc in range(NKC)],
                    reads=list(rSB) + [reg(rSC, "IR"), rCN], writes=[rh])
                dbg("h", YT[:, A_H:A_H + 2048], [rh.wtok], regs=[rh])

                accs = [("PA0", PA, 0), ("PA1", PA, 256), ("PB0", PB, 0), ("PB1", PB, 256)]
                na = [0]

                def proj_mm(wn, mc):
                    nm, pt_, c0 = accs[na[0] % 4]
                    na[0] += 1
                    sl = rPB[nm[:2]]
                    bld.emit("t", [lambda e, kc=kc, mc=mc, wn=wn, pt_=pt_, c0=c0: e.matmul(
                        pt_[:, c0:c0 + 256], wsl(wn, kc, mc * 128, 128),
                        YT[:, A_H + kc * 256:A_H + (kc + 1) * 256],
                        start=(kc == 0), stop=(kc == NKC - 1)) for kc in range(NKC)],
                        reads=[rh, reg(rWA, wn)], writes=[sl])
                    return sl, pt_, c0

                sq_toks = {}
                for wn, dst, rdst in (("Wq", A_QH, rqh), ("Wk", A_KH, rkh)):
                    for mc in range(NKC):
                        sl, pt_, c0 = proj_mm(wn, mc)
                        ev = bld.emit("s", lambda e, pt_=pt_, c0=c0, dst=dst, mc=mc: e.copy(
                            YT[:, dst + mc * 256:dst + (mc + 1) * 256], pt_[:, c0:c0 + 256]),
                            reads=[sl], writes=[rdst])
                for mc in range(NKC):
                    sl, pt_, c0 = proj_mm("Wv", mc)
                    bld.emit("s", lambda e, pt_=pt_, c0=c0, mc=mc: e.copy(
                        YT[:, A_VF + mc * 256:A_VF + (mc + 1) * 256], pt_[:, c0:c0 + 256]),
                        reads=[sl], writes=[rvf])
                sstat = rPB["PS"]
                for wn, dst, sc0, rdst in (("Wq", A_QH, 0, rqh), ("Wk", A_KH, 256, rkh)):
                    for mc in range(NKC):
                        off2 = S_TMP if mc % 2 == 0 else S_VH
                        rsqt = reg(rSC, "TMP%d" % (mc % 2))
                        bld.emit("s", lambda e, dst=dst, mc=mc, off2=off2: e.activation(
                            SC[:, off2:off2 + 256],
                            YT[:, dst + mc * 256:dst + (mc + 1) * 256], AF.Square),
                            reads=[rdst], writes=[rsqt])
                        bld.emit("t", lambda e, mc=mc, sc0=sc0, off2=off2: e.matmul(
                            PS_[0:16, sc0:sc0 + 256], CN["hsel"][:, mc * 16:(mc + 1) * 16],
                            SC[:, off2:off2 + 256],
                            start=(mc == 0), stop=(mc == NKC - 1)),
                            reads=[rsqt, rCN],
                            writes=[sstat] if mc in (0, NKC - 1) else [])

                for which, sc0, ro0 in (("q", 0, 256), ("k", 256, 768)):
                    rmx = reg(rROWS, "mx" + which)
                    bld.emit("v", lambda e, sc0=sc0, ro0=ro0: e.tensor_scalar_max(
                        ROWS[0:16, ro0:ro0 + 256], PS_[0:16, sc0:sc0 + 256], 1e-24),
                        reads=[rPB["PS"]], writes=[rmx])
                    bld.emit("s", lambda e, ro0=ro0: e.activation(
                        ROWS[0:16, ro0 + 256:ro0 + 512], ROWS[0:16, ro0:ro0 + 256], AF.Ln),
                        reads=[rmx], writes=[rmx])
                    rrb = reg(rRB, "rq" + which)
                    bld.emit("s", lambda e, ro0=ro0, rb0=(256 if ro0 == 256 else 512): e.activation(
                        RB[0:16, rb0:rb0 + 256], ROWS[0:16, ro0 + 256:ro0 + 512], AF.Exp, scale=-0.5),
                        reads=[rmx], writes=[rrb])

                gcols = []

                def gate_tail(mc):
                    rot = mc % 2
                    rbank = PR if rot == 0 else PS_
                    rslot = rPB["PR" if rot == 0 else "PS"]
                    bld.emit("t", lambda e, mc=mc, rbank=rbank: e.matmul(
                        rbank[:, 0:512], CN["hrep"][:, mc * 128:(mc + 1) * 128],
                        RB[0:16, 256:768], start=True, stop=True),
                        reads=[reg(rRB, "rqq"), reg(rRB, "rqk"), rCN], writes=[rslot])
                    for which, bc0, dst, rdst, eoff, toff, tnm in (
                            ("q", 0, A_QH, rqh, S_EQ, S_TMP, "TMP0"),
                            ("k", 256, A_KH, rkh, S_EK, S_VH, "TMP1")):
                        rexp = reg(rSC, ("EQ%d" if which == "q" else "EK%d") % rot)
                        bld.emit("v", lambda e, rot=rot, eoff=eoff, rbank=rbank, toff=toff, bc0=bc0: e.tensor_tensor(
                            SC[:, toff:toff + 256], rbank[:, bc0:bc0 + 256],
                            SC[:, eoff + rot * 256:eoff + rot * 256 + 256], ALU.mult),
                            reads=[rslot, rexp], writes=[reg(rSC, tnm)])
                        bld.emit("p", lambda e, mc=mc, dst=dst, toff=toff: e.tensor_tensor(
                            YT[:, dst + mc * 256:dst + (mc + 1) * 256],
                            YT[:, dst + mc * 256:dst + (mc + 1) * 256],
                            SC[:, toff:toff + 256], ALU.mult),
                            reads=[reg(rSC, tnm)], writes=[rdst], extra=pb)

                def st_spevac(mc, sl, pt_, c0):
                    rot = mc % 2
                    bld.emit("v", lambda e, pt_=pt_, c0=c0, rot=rot: e.tensor_scalar(
                        SF[:, F_SP + rot * 256:F_SP + rot * 256 + 256], pt_[:, c0:c0 + 256],
                        -1.0, 40.0, ALU.mult, ALU.min),
                        reads=[sl], writes=[reg(rSF, "SP%d" % rot)])

                def st_exp(mc):
                    rot = mc % 2
                    bld.emit("s", lambda e, rot=rot: e.activation(
                        SF[:, F_SCUM + rot * 256:F_SCUM + rot * 256 + 256],
                        SF[:, F_SP + rot * 256:F_SP + rot * 256 + 256], AF.Exp),
                        reads=[reg(rSF, "SP%d" % rot)], writes=[reg(rSF, "SCUM%d" % rot)])

                def st_ln(mc):
                    rot = mc % 2
                    bld.emit("s", lambda e, rot=rot: e.activation(
                        SF[:, F_SP + rot * 256:F_SP + rot * 256 + 256],
                        SF[:, F_SCUM + rot * 256:F_SCUM + rot * 256 + 256], AF.Ln,
                        bias=CN["eps"][0:128, 2:3]),
                        reads=[reg(rSF, "SCUM%d" % rot)], writes=[reg(rSF, "SP%d" % rot)])

                def st_scan(mc):
                    rot = mc % 2
                    bld.emit("v", lambda e, rot=rot: e.tensor_tensor_scan(
                        SF[:, F_SCUM + rot * 256:F_SCUM + rot * 256 + 256],
                        SF[:, F_SP + rot * 256:F_SP + rot * 256 + 256],
                        SF[:, F_SP + rot * 256:F_SP + rot * 256 + 256],
                        0.0, ALU.add, ALU.bypass),
                        reads=[reg(rSF, "SP%d" % rot)], writes=[reg(rSF, "SCUM%d" % rot)])

                def st_cols(mc):
                    rot = mc % 2
                    base = F_COLS + mc * 24
                    rcol = reg(rSF, "COL%d" % mc)
                    bld.emit("v", [
                        lambda e, rot=rot, base=base: e.tensor_copy(
                            SF[:, base:base + NC_T],
                            SF[:, F_SCUM + rot * 256:F_SCUM + rot * 256 + 256]
                            .rearrange("p (c n) -> p c n", n=C)[:, :, C // 2 - 1]),
                        lambda e, rot=rot, base=base: e.tensor_copy(
                            SF[:, base + 4:base + 8],
                            SF[:, F_SCUM + rot * 256:F_SCUM + rot * 256 + 256]
                            .rearrange("p (c n) -> p c n", n=C)[:, :, C - 1]),
                    ], reads=[reg(rSF, "SCUM%d" % rot)], writes=[rcol])
                    bld.emit("v", [
                        lambda e, base=base: e.tensor_scalar_mul(
                            SF[:, base + 8:base + 9], SF[:, base + 4:base + 5], -1.0),
                        lambda e, base=base: e.tensor_sub(
                            SF[:, base + 9:base + 12], SF[:, base + 4:base + 7], SF[:, base + 5:base + 8]),
                    ], reads=[rcol], writes=[rcol])

                def st_acexp(mc):
                    base = F_COLS + mc * 24
                    rcol = reg(rSF, "COL%d" % mc)
                    bld.emit("s", [
                        lambda e, base=base: e.activation(SF[:, base + 12:base + 16],
                                                          SF[:, base + 8:base + 12], AF.Exp),
                        lambda e, base=base: e.activation(SF[:, base + 16:base + 20],
                                                          SF[:, base:base + 4], AF.Exp, scale=-1.0),
                    ], reads=[rcol], writes=[rcol])

                def st_fksub(mc):
                    base = F_COLS + mc * 24
                    rcol = reg(rSF, "COL%d" % mc)
                    bld.emit("v", lambda e, base=base: e.tensor_sub(
                        SF[:, base + 8:base + 12], SF[:, base:base + 4], SF[:, base + 4:base + 8]),
                        reads=[rcol], writes=[rcol])

                def st_fkexp(mc):
                    base = F_COLS + mc * 24
                    rcol = reg(rSF, "COL%d" % mc)
                    bld.emit("s", lambda e, base=base: e.activation(
                        SF[:, base + 16:base + 20], SF[:, base + 8:base + 12], AF.Exp),
                        reads=[rcol], writes=[rcol])

                def st_gkprep(mc):
                    base = F_COLS + mc * 24
                    rcol = reg(rSF, "COL%d" % mc)
                    bld.emit("v", [
                        lambda e, base=base: e.tensor_scalar_mul(
                            SF[:, base + 8:base + 9], SF[:, base:base + 1], -1.0),
                        lambda e, base=base: e.tensor_sub(
                            SF[:, base + 9:base + 12], SF[:, base + 4:base + 7], SF[:, base + 1:base + 4]),
                    ], reads=[rcol], writes=[rcol])

                def st_gkexp(mc):
                    base = F_COLS + mc * 24
                    rcol = reg(rSF, "COL%d" % mc)
                    bld.emit("s", lambda e, base=base: e.activation(
                        SF[:, base + 20:base + 24], SF[:, base + 8:base + 12], AF.Exp),
                        reads=[rcol], writes=[rcol])

                def st_bsub(mc):
                    rot = mc % 2
                    base = F_COLS + mc * 24
                    rcol = reg(rSF, "COL%d" % mc)
                    bld.emit("v", lambda e, rot=rot, base=base: e.tensor_tensor(
                        SF[:, F_SP + rot * 256:F_SP + rot * 256 + 256].rearrange("p (c n) -> p c n", n=C),
                        SF[:, F_SCUM + rot * 256:F_SCUM + rot * 256 + 256].rearrange("p (c n) -> p c n", n=C),
                        SF[:, base:base + NC_T].rearrange("p c -> p c ()").broadcast_to([128, NC_T, C]),
                        ALU.subtract),
                        reads=[reg(rSF, "SCUM%d" % rot), rcol], writes=[reg(rSF, "SP%d" % rot)])

                def st_eqek(mc):
                    rot = mc % 2
                    rsp = reg(rSF, "SP%d" % rot)
                    bld.emit("s", lambda e, rot=rot: e.activation(
                        SC[:, S_EQ + rot * 256:S_EQ + rot * 256 + 256],
                        SF[:, F_SP + rot * 256:F_SP + rot * 256 + 256], AF.Exp, scale=-1.0),
                        reads=[rsp], writes=[reg(rSC, "EQ%d" % rot)])
                    bld.emit("s", lambda e, rot=rot: e.activation(
                        SC[:, S_EK + rot * 256:S_EK + rot * 256 + 256],
                        SF[:, F_SP + rot * 256:F_SP + rot * 256 + 256], AF.Exp),
                        reads=[rsp], writes=[reg(rSC, "EK%d" % rot)])

                for pr in range(NKC // 2):
                    m0, m1 = 2 * pr, 2 * pr + 1
                    prsl = {}
                    for mc in (m0, m1):
                        prsl[mc] = proj_mm("Wg", mc)
                        gcols.append((F_COLS + mc * 24, reg(rSF, "COL%d" % mc)))
                    for mc in (m0, m1):
                        st_spevac(mc, *prsl[mc])
                    for st in (st_exp, st_ln, st_scan, st_cols, st_acexp, st_fksub,
                               st_fkexp, st_gkprep, st_gkexp, st_bsub, st_eqek):
                        for mc in (m0, m1):
                            st(mc)
                    for mc in (m0, m1):
                        gate_tail(mc)
                dbg("qt", YT[:, A_QH:A_QH + 2048], [rqh.wtok], regs=[rqh])
                dbg("kt", YT[:, A_KH:A_KH + 2048], [rkh.wtok], regs=[rkh])

                GP = [dict(kk=4096, kts=S_KTS, vts=S_VTS, vbd=S_VBD, scm=S_SCM,
                           kkn="KK0", ktsn="KTS", vtsn="VTS", vbdn="VBD", scmn="SCM0",
                           tb=PTB, tbn="PTB", scb=PG, scbn="PG", smm=S_SMM, smmn="SMM",
                           pob=PC, pobn="PC", pub=PD, pubn="PD"),
                      dict(kk=4160, kts=4224, vts=4352, vbd=4480, scm=4608,
                           kkn="KK1", ktsn="KTS1", vtsn="VTS1", vbdn="VBD1", scmn="SCM1",
                           tb=PRB, tbn="PR", scb=PS_, scbn="PS", smm=4672, smmn="SMM1",
                           pob=PA, pobn="PA", pub=PB, pubn="PB")]
                for hp in range(NKC):
                    base, rcol = gcols[hp]
                    rkbd = reg(rSC, "KBD")
                    bld.emit("p", [lambda e, hp=hp, hh=hh: e.tensor_copy(
                        SC[p64(hh), S_KBD:S_KBD + 512].rearrange("p (c n) -> p c n", n=128)[:, :, hh * 64:(hh + 1) * 64],
                        YT[p64(hh), A_KH + hp * 256:A_KH + (hp + 1) * 256].rearrange("p (c n) -> p c n", n=64))
                        for hh in range(2)],
                        reads=[rkh], writes=[rkbd], extra=pb)

                    def gla_tr(cc):
                        g = GP[cc % 2]
                        rtb = rPB[g["tbn"]]
                        bld.emit("t", [
                            lambda e, hp=hp, cc=cc, g=g: e.transpose(
                                g["tb"][0:C, 0:128],
                                YT[:, A_KH + hp * 256 + cc * C:A_KH + hp * 256 + (cc + 1) * C],
                                CN["ident"][:, :]),
                            lambda e, hp=hp, cc=cc, g=g: e.transpose(
                                g["tb"][0:C, 128:256],
                                YT[:, A_VF + hp * 256 + cc * C:A_VF + hp * 256 + (cc + 1) * C],
                                CN["ident"][:, :]),
                        ], reads=[rkh, rvf, rCN], writes=[rtb])
                        rkts = reg(rSC, g["ktsn"])
                        rvts = reg(rSC, g["vtsn"])
                        bld.emit("s", [
                            lambda e, g=g: e.copy(SC[0:C, g["kts"]:g["kts"] + 128], g["tb"][0:C, 0:128]),
                            lambda e, g=g: e.copy(SC[0:C, g["vts"]:g["vts"] + 128], g["tb"][0:C, 128:256]),
                        ], reads=[rtb], writes=[rkts, rvts])
                        return (rkts, rvts)

                    def gla_prep(cc, tr_):
                        g = GP[cc % 2]
                        rkts, rvts = tr_
                        slc = rPB[g["scbn"]]
                        bld.emit("t", lambda e, g=g: e.matmul(
                            g["scb"][:, 256:384], CN["seye"][:, :], SC[0:C, g["vts"]:g["vts"] + 128],
                            start=True, stop=True),
                            reads=[rvts, rCN], writes=[slc])
                        rvbd = reg(rSC, g["vbdn"])
                        bld.emit("v", lambda e, g=g: e.tensor_tensor(
                            SC[:, g["vbd"]:g["vbd"] + 128], g["scb"][:, 256:384], CN["bdm"][:, :], ALU.mult),
                            reads=[slc, rCN], writes=[rvbd])
                        bld.emit("t", lambda e, hp=hp, cc=cc, g=g: e.matmul(
                            g["scb"][:, 0:C],
                            SC[:, S_KBD + cc * 128:S_KBD + (cc + 1) * 128],
                            YT[:, A_QH + hp * 256 + cc * C:A_QH + hp * 256 + (cc + 1) * C],
                            start=True, stop=True),
                            reads=[rkbd, rqh], writes=[slc])
                        rscm = reg(rSC, g["scmn"])
                        bld.emit("v", lambda e, g=g: e.tensor_tensor(
                            SC[:, g["scm"]:g["scm"] + C], g["scb"][:, 0:C],
                            CN["maskT2"][:, :], ALU.mult),
                            reads=[slc, rCN], writes=[rscm])
                        rqg = reg(rSC, g["smmn"])
                        bld.emit("v", lambda e, hp=hp, cc=cc, base=base, g=g: e.tensor_scalar(
                            SC[:, g["smm"]:g["smm"] + C],
                            YT[:, A_QH + hp * 256 + cc * C:A_QH + hp * 256 + (cc + 1) * C],
                            SF[:, base + 20 + cc:base + 21 + cc], None, ALU.mult),
                            reads=[rqh, rcol], writes=[rqg])
                        return (rkts, rvts, rvbd, rscm, rqg)

                    trs = {0: gla_tr(0), 1: gla_tr(1)}
                    pregs = {0: gla_prep(0, trs[0])}
                    for cc in range(NC_T):
                        if cc + 2 < NC_T:
                            trs[cc + 2] = gla_tr(cc + 2)
                        if cc + 1 < NC_T:
                            pregs[cc + 1] = gla_prep(cc + 1, trs[cc + 1])
                        g = GP[cc % 2]
                        rkts, rvts, rvbd, rscm, rqg = pregs[cc]
                        soff = hp * 128 if cc % 2 == 0 else SPARE_ST
                        doff = SPARE_ST if cc % 2 == 0 else hp * 128
                        rsS, rdS = rSTb[cc % 2], rSTb[(cc + 1) % 2]
                        slu = rPB[g["pubn"]]
                        puc0 = (cc & 7) * 64
                        bld.emit("t", [lambda e, hh=hh, g=g, puc0=puc0: e.matmul(
                            g["pub"][p64(hh), puc0:puc0 + 64],
                            SC[0:C, g["kts"] + hh * 64:g["kts"] + (hh + 1) * 64],
                            SC[0:C, g["vts"] + hh * 64:g["vts"] + (hh + 1) * 64],
                            start=True, stop=True) for hh in range(2)],
                            reads=[rkts, rvts], writes=[slu])
                        rput = reg(rSF, "PUT")
                        bld.emit("v", lambda e, base=base, cc=cc, g=g, puc0=puc0: e.tensor_scalar(
                            SF[:, F_PUT:F_PUT + 64], g["pub"][:, puc0:puc0 + 64],
                            SF[:, base + 16 + cc:base + 17 + cc], None, ALU.mult),
                            reads=[slu, rcol], writes=[rput])
                        bld.emit("v", [lambda e, cc=cc, base=base, hh=hh, soff=soff, doff=doff:
                                       e.scalar_tensor_tensor(
                            STT_[p64(hh), doff + hh * 64:doff + (hh + 1) * 64],
                            STT_[p64(hh), soff + hh * 64:soff + (hh + 1) * 64],
                            SF[p64(hh), base + 12 + cc:base + 13 + cc],
                            SF[p64(hh), F_PUT:F_PUT + 64],
                            ALU.mult, ALU.add) for hh in range(2)],
                            reads=[rput, rcol, rsS], writes=[rdS])
                        slo = rPB[g["pobn"]]
                        poc0 = (cc & 7) * 64
                        bld.emit("t", [
                            lambda e, g=g, poc0=poc0: e.matmul(
                                g["pob"][:, poc0:poc0 + C], SC[:, g["vbd"]:g["vbd"] + 128],
                                SC[:, g["scm"]:g["scm"] + C], start=True, stop=False),
                            lambda e, g=g, poc0=poc0, soff=soff: e.matmul(
                                g["pob"][:, poc0:poc0 + C], STT_[:, soff:soff + 128],
                                SC[:, g["smm"]:g["smm"] + C],
                                start=False, stop=True),
                        ], reads=[rvbd, rscm, rqg, rsS], writes=[slo])
                        bld.emit("s", lambda e, hp=hp, cc=cc, g=g, poc0=poc0: e.copy(
                            YT[:, A_O + hp * 256 + cc * C:A_O + hp * 256 + (cc + 1) * C],
                            g["pob"][:, poc0:poc0 + C]),
                            reads=[slo], writes=[ro])
                dbg("o", YT[:, A_O:A_O + 2048], [ro.wtok], regs=[ro])

                for mc in range(NKC):
                    nm, pt_, c0 = accs[na[0] % 4]
                    na[0] += 1
                    sl = rPB[nm[:2]]
                    bld.emit("t", [lambda e, kc=kc, mc=mc, pt_=pt_, c0=c0: e.matmul(
                        pt_[:, c0:c0 + 256], wsl("Wo", kc, mc * 128, 128),
                        YT[:, A_O + kc * 256:A_O + (kc + 1) * 256],
                        start=(kc == 0), stop=(kc == NKC - 1)) for kc in range(NKC)],
                        reads=[ro, reg(rWA, "Wo")], writes=[sl])
                    tkx = bld.emit("v", lambda e, pt_=pt_, c0=c0, mc=mc, tl=tl: e.tensor_tensor(
                        SB[:, mc * TC + tl:mc * TC + tl + PT],
                        SB[:, mc * TC + tl:mc * TC + tl + PT],
                        pt_[:, c0:c0 + 256], ALU.add),
                        reads=[sl], writes=[rSB[mc]])
                    x2_done.append(tkx)
            dbg("x2", SB[:, 0:NKC * TC], x2_done, regs=list(rSB))

            pb = bld.barrier(extra=dbg_toks)
            z_done = []
            xp_emitted = [False] * NPT
            emit_xload(0, seq, t0g)
            emit_xload(1, seq, t0g)
            load_w("w1", lambda c: dW["w1"][c * 128:(c + 1) * 128, :], NKC, DFF, OFF["w1"], extra=pb)
            load_w("w2", lambda c: dW["w2"][c * 128:(c + 1) * 128, :], NKC, DFF, OFF["w2"], extra=pb)
            load_w("w3", lambda c: dW["w3"][c * 128:min(DFF, (c + 1) * 128), :], W3KC, 1024, OFF["w3"],
                   extra=pb, prows_fn=lambda c: min(128, DFF - c * 128))
            for ti in range(NPT):
                par = ti % 2
                t0 = t0g + ti * PT
                tl = ti * PT
                rh2, ru = reg(rYT, "B_H2"), reg(rYT, "B_U")
                B_H2, B_U = 0, 2048
                for kc in range(NKC):
                    rsq = reg(rSC, "SQ%d" % (kc % 2))
                    bld.emit("v", lambda e, kc=kc, tl=tl: e.tensor_tensor(
                        SC[:, S_SQ + (kc % 2) * 256:S_SQ + (kc % 2) * 256 + 256],
                        SB[:, kc * TC + tl:kc * TC + tl + 256],
                        SB[:, kc * TC + tl:kc * TC + tl + 256], ALU.mult),
                        reads=[rSB[kc]], writes=[rsq], extra=pb)
                    bld.emit("t", lambda e, kc=kc: e.matmul(
                        PS_[0:1, 0:256], CN["ones1"][:, :],
                        SC[:, S_SQ + (kc % 2) * 256:S_SQ + (kc % 2) * 256 + 256],
                        start=(kc == 0), stop=(kc == NKC - 1)),
                        reads=[rsq, rCN], writes=[rPB["PS"]] if kc in (0, NKC - 1) else [])
                rr1 = reg(rROWS, "r1")
                bld.emit("s", lambda e: e.activation(ROWS[0:1, 0:256], PS_[0:1, 0:256], AF.Ln,
                                                     bias=CN["eps"][0:1, 0:1], scale=1.0 / D),
                         reads=[rPB["PS"]], writes=[rr1])
                rb1 = reg(rRB, "r1")
                bld.emit("s", lambda e: e.activation(RB[0:1, 0:256], ROWS[0:1, 0:256], AF.Exp, scale=-0.5),
                         reads=[rr1], writes=[rb1])
                bld.emit("t", lambda e: e.matmul(PR[:, 0:256], CN["row1"][:, :], RB[0:1, 0:256],
                                                 start=True, stop=True),
                         reads=[rb1, rCN], writes=[rPB["PR"]])
                bld.emit("v", lambda e: e.tensor_copy(SC[:, S_IR:S_IR + 256], PR[:, 0:256]),
                         reads=[rPB["PR"]], writes=[reg(rSC, "IR")])
                bld.emit("v", [lambda e, kc=kc, tl=tl: e.scalar_tensor_tensor(
                    YT[:, B_H2 + kc * 256:B_H2 + (kc + 1) * 256],
                    SB[:, kc * TC + tl:kc * TC + tl + 256],
                    GF[:, kc:kc + 1], SC[:, S_IR:S_IR + 256], ALU.mult, ALU.mult)
                    for kc in range(NKC)],
                    reads=list(rSB) + [reg(rSC, "IR"), rCN], writes=[rh2])
                for a in range(2):
                    bld.emit("t", [lambda e, a=a, kc=kc, par=par: e.transpose(
                        PTB[:, kc * 128:(kc + 1) * 128],
                        XS[par][:, a * D + kc * 128: a * D + (kc + 1) * 128],
                        CN["ident"][:, :]) for kc in range(NKC)],
                        reads=[rXS[par], rCN, rh2], writes=[rPB["PTB"]], extra=pb)
                    bld.emit("v", lambda e, a=a, tl=tl: e.tensor_tensor(
                        SB[:, :].rearrange("p (c m) -> p c m", m=TC)[:, :, tl + a * 128:tl + (a + 1) * 128],
                        SB[:, :].rearrange("p (c m) -> p c m", m=TC)[:, :, tl + a * 128:tl + (a + 1) * 128],
                        PTB[:, :].rearrange("p (c n) -> p c n", n=128), ALU.add),
                        reads=[rPB["PTB"]], writes=[rSB[kc2] for kc2 in range(NKC)])
                emit_xload(ti + 2, seq, t0g)

                accsB = [("PA0", PA, 0), ("PA1", PA, 256), ("PB0", PB, 0), ("PB1", PB, 256),
                         ("PC0", PC, 0), ("PC1", PC, 256), ("PD0", PD, 0), ("PD1", PD, 256)]
                nb = [0]
                for mc in range(W3KC):
                    rows = min(128, DFF - mc * 128)
                    nm, pt_, c0 = accsB[nb[0] % 8]
                    nb[0] += 1
                    sl = rPB[nm[:2]]
                    bld.emit("t", [lambda e, kc=kc, mc=mc, pt_=pt_, c0=c0, rows=rows: e.matmul(
                        pt_[0:rows, c0:c0 + 256], wsl("w1", kc, mc * 128, rows),
                        YT[:, B_H2 + kc * 256:B_H2 + (kc + 1) * 256],
                        start=(kc == 0), stop=(kc == NKC - 1)) for kc in range(NKC)],
                        reads=[rh2, reg(rWA, "w1")], writes=[sl])
                    bld.emit("s", lambda e, pt_=pt_, c0=c0, mc=mc, rows=rows: e.activation(
                        YT[0:rows, B_U + mc * 256:B_U + (mc + 1) * 256],
                        pt_[0:rows, c0:c0 + 256], AF.Silu),
                        reads=[sl], writes=[ru] if mc == W3KC - 1 else [])
                for mc in range(W3KC):
                    rows = min(128, DFF - mc * 128)
                    nm, pt_, c0 = accsB[nb[0] % 8]
                    nb[0] += 1
                    sl = rPB[nm[:2]]
                    bld.emit("t", [lambda e, kc=kc, mc=mc, pt_=pt_, c0=c0, rows=rows: e.matmul(
                        pt_[0:rows, c0:c0 + 256], wsl("w2", kc, mc * 128, rows),
                        YT[:, B_H2 + kc * 256:B_H2 + (kc + 1) * 256],
                        start=(kc == 0), stop=(kc == NKC - 1)) for kc in range(NKC)],
                        reads=[rh2, reg(rWA, "w2")], writes=[sl])
                    bld.emit("v", lambda e, pt_=pt_, c0=c0, mc=mc, rows=rows: e.tensor_tensor(
                        YT[0:rows, B_U + mc * 256:B_U + (mc + 1) * 256],
                        YT[0:rows, B_U + mc * 256:B_U + (mc + 1) * 256],
                        pt_[0:rows, c0:c0 + 256], ALU.mult),
                        reads=[sl, ru], writes=[ru] if mc == W3KC - 1 else [])
                for mc in range(NKC):
                    nm, pt_, c0 = accsB[nb[0] % 8]
                    nb[0] += 1
                    sl = rPB[nm[:2]]
                    bld.emit("t", [lambda e, kc=kc, mc=mc, pt_=pt_, c0=c0,
                                   rows=min(128, DFF - kc * 128): e.matmul(
                        pt_[:, c0:c0 + 256], wsl("w3", kc, mc * 128, 128, prows=rows),
                        YT[0:rows, B_U + kc * 256:B_U + (kc + 1) * 256],
                        start=(kc == 0), stop=(kc == W3KC - 1)) for kc in range(W3KC)],
                        reads=[ru, reg(rWA, "w3")], writes=[sl])
                    tkz = bld.emit("v", lambda e, pt_=pt_, c0=c0, mc=mc, tl=tl: e.tensor_tensor(
                        SB[:, mc * TC + tl:mc * TC + tl + 256],
                        SB[:, mc * TC + tl:mc * TC + tl + 256], pt_[:, c0:c0 + 256], ALU.add),
                        reads=[sl], writes=[rSB[mc]])
                    z_done.append(tkz)
            dbg("z", SB[:, 0:NKC * TC], z_done, regs=list(rSB))

            pb = bld.barrier(extra=dbg_toks)
            for lam in range(NLN):
                l0 = lam * 512
                bld.emit("v", lambda e, l0=l0: e.tensor_tensor(
                    SC[:, 0:4096].rearrange("p (c n) -> p c n", n=512),
                    SB[:, :].rearrange("p (c m) -> p c m", m=TC)[:, :, l0:l0 + 512],
                    SB[:, :].rearrange("p (c m) -> p c m", m=TC)[:, :, l0:l0 + 512], ALU.mult),
                    reads=list(rSB), writes=[reg(rSC, "SQ4")], extra=pb)
                bld.emit("t", [lambda e, kc=kc, l0=l0: e.matmul(
                    PS_[0:1, 0:512], CN["ones1"][:, :], SB[:, kc * TC + l0:kc * TC + l0 + 512],
                    start=(kc == 0), stop=(kc == NKC - 1)) for kc in range(NKC)],
                    reads=list(rSB) + [rCN], writes=[rPB["PS"]], extra=pb)
                bld.emit("t", [lambda e, kc=kc: e.matmul(
                    PR[0:1, 0:512], CN["ones1"][:, :], SC[:, kc * 512:(kc + 1) * 512],
                    start=(kc == 0), stop=(kc == NKC - 1)) for kc in range(NKC)],
                    reads=[reg(rSC, "SQ4"), rCN], writes=[rPB["PR"]])
                rln = reg(rROWS, "ln")
                bld.emit("v", lambda e: e.tensor_scalar_mul(ROWS[0:1, 0:512], PS_[0:1, 0:512], 1.0 / D),
                         reads=[rPB["PS"]], writes=[rln])
                bld.emit("v", [
                    lambda e: e.tensor_tensor(ROWS[0:1, 512:1024], ROWS[0:1, 0:512],
                                              ROWS[0:1, 0:512], ALU.mult),
                    lambda e: e.tensor_scalar_mul(ROWS[0:1, 1024:1536], PR[0:1, 0:512], 1.0 / D),
                ], reads=[rln, rPB["PR"]], writes=[rln])
                bld.emit("v", lambda e: e.tensor_sub(ROWS[0:1, 1024:1536], ROWS[0:1, 1024:1536],
                                                     ROWS[0:1, 512:1024]),
                         reads=[rln], writes=[rln])
                bld.emit("s", lambda e: e.activation(ROWS[0:1, 512:1024], ROWS[0:1, 1024:1536],
                                                     AF.Ln, bias=CN["eps"][0:1, 1:2]), reads=[rln], writes=[rln])
                rlb = reg(rRB, "ln")
                bld.emit("s", [
                    lambda e: e.activation(RB[0:1, 0:512], ROWS[0:1, 512:1024], AF.Exp, scale=-0.5),
                    lambda e: e.copy(RB[0:1, 512:1024], ROWS[0:1, 0:512]),
                ], reads=[rln], writes=[rlb])
                bld.emit("t", lambda e: e.matmul(PR[:, 0:512], CN["row1"][:, :], RB[0:1, 512:1024],
                                                 start=True, stop=True),
                         reads=[rlb, rCN], writes=[rPB["PR"]])
                bld.emit("v", lambda e: e.tensor_copy(SC[:, S_MR:S_MR + 512], PR[:, 0:512]),
                         reads=[rPB["PR"]], writes=[reg(rSC, "MR")])
                bld.emit("t", lambda e: e.matmul(PS_[:, 0:512], CN["row1"][:, :], RB[0:1, 0:512],
                                                 start=True, stop=True),
                         reads=[rlb, rCN], writes=[rPB["PS"]])
                bld.emit("v", lambda e: e.tensor_copy(SC[:, S_VR:S_VR + 512], PS_[:, 0:512]),
                         reads=[rPB["PS"]], writes=[reg(rSC, "VR")])
                for kc in range(NKC):
                    bld.emit("v", lambda e, kc=kc, l0=l0: e.tensor_tensor(
                        SB[:, kc * TC + l0:kc * TC + l0 + 512],
                        SB[:, kc * TC + l0:kc * TC + l0 + 512], SC[:, S_MR:S_MR + 512], ALU.subtract),
                        reads=[reg(rSC, "MR")], writes=[rSB[kc]])
                    bld.emit("v", lambda e, kc=kc, l0=l0: e.tensor_tensor(
                        SB[:, kc * TC + l0:kc * TC + l0 + 512],
                        SB[:, kc * TC + l0:kc * TC + l0 + 512], SC[:, S_VR:S_VR + 512], ALU.mult),
                        reads=[reg(rSC, "VR")], writes=[rSB[kc]])
            dbg("y1", SB[:, 0:NKC * TC], [r.wtok for r in rSB], regs=list(rSB))

            for tt in range(NTT):
                bld.emit("t", [lambda e, tt=tt, kc=kc: e.transpose(
                    PTB[:, kc * 128:(kc + 1) * 128],
                    SB[:, kc * TC + tt * 128:kc * TC + (tt + 1) * 128],
                    CN["ident"][:, :]) for kc in range(NKC)],
                    reads=list(rSB) + [rCN], writes=[rPB["PTB"]])
                if tt % 2 == 0:
                    bld.emit("v", lambda e, tt=tt: e.tensor_copy(
                        YT[:, tt * 1024:(tt + 1) * 1024], PTB[:, :]),
                        reads=[rPB["PTB"]], writes=[rSBtm[tt]])
                else:
                    bld.emit("s", lambda e, tt=tt: e.copy(
                        YT[:, tt * 1024:(tt + 1) * 1024], PTB[:, :]),
                        reads=[rPB["PTB"]], writes=[rSBtm[tt]])
            rlg = reg(rSF, "LG")
            rpm = reg(rSC, "PERM")
            for lam in range(NLN):
                l0 = lam * 512
                bld.emit("t", [lambda e, kc=kc, l0=l0: e.matmul(
                    PG[0:NES, 0:512], CTL[:, kc * NES:(kc + 1) * NES],
                    SB[:, kc * TC + l0:kc * TC + l0 + 512],
                    start=(kc == 0), stop=(kc == NKC - 1)) for kc in range(NKC)],
                    reads=list(rSB) + [rCN], writes=[rPB["PG"]])
                bld.emit("v", lambda e, l0=l0: e.tensor_tensor(
                    SF[0:NES, F_LG + l0:F_LG + l0 + 512].rearrange("p (g n) -> p g n", n=GRP),
                    PG[0:NES, 0:512].rearrange("p (g n) -> p g n", n=GRP),
                    CN["tie"][:, :].rearrange("p n -> p () n").broadcast_to([NES, 512 // GRP, GRP]),
                    ALU.add),
                    reads=[rPB["PG"], rCN], writes=[rlg])
                bld.emit("v", lambda e, l0=l0: e.tensor_reduce(
                    SF[0:NES, F_GM + l0 // GRP:F_GM + l0 // GRP + 32],
                    SF[0:NES, F_LG + l0:F_LG + l0 + 512].rearrange("p (g n) -> p g n", n=GRP),
                    AX.X, ALU.max), reads=[rlg], writes=[rlg])
                bld.emit("v", lambda e, l0=l0: e.tensor_tensor(
                    SC[0:NES, S_PERM + l0:S_PERM + l0 + 512].rearrange("p (g n) -> p g n", n=GRP),
                    SF[0:NES, F_LG + l0:F_LG + l0 + 512].rearrange("p (g n) -> p g n", n=GRP),
                    SF[0:NES, F_GM + l0 // GRP:F_GM + l0 // GRP + 32]
                    .rearrange("p g -> p g ()").broadcast_to([NES, 32, GRP]),
                    ALU.is_equal), reads=[rlg], writes=[rpm])
            dbg("perm", SC[0:NES, S_PERM:S_PERM + TC], [rpm.wtok], regs=[rpm])
            rpt = reg(rSC, "PERMT")
            for tt in range(NTT):
                bld.emit("t", lambda e, tt=tt: e.transpose(
                    PTB[:, 0:64], SC[0:NES, S_PERM + tt * 128:S_PERM + (tt + 1) * 128],
                    CN["ident"][0:NES, 0:NES]), reads=[rpm, rCN], writes=[rPB["PTB"]])
                bld.emit("v", lambda e, tt=tt: e.tensor_copy(
                    SC[:, S_PERMT + tt * 64:S_PERMT + (tt + 1) * 64], PTB[:, 0:64]),
                    reads=[rPB["PTB"]], writes=[rpt] if tt == NTT - 1 else [])

            pc_ = bld.barrier(extra=dbg_toks)
            for blk in range(NEB):
                buf = blk % 2
                e0 = (blk * EB) // ESL
                pre2 = pc_ if blk < 2 else [bld.tok("t")]
                rf1 = load_w("f1_%d" % buf, lambda c, e0=e0: dW["f1"][c * 128:(c + 1) * 128, e0:e0 + 4, :, :]
                             .rearrange("p e s f -> p (e s f)"), NKC, EB * ESZ, OFF["f1_%d" % buf],
                             extra=pre2)
                rf2 = reg(rWA, "f2_%d" % buf)
                bld.emit("g", lambda e, e0=e0, buf=buf: e.dma_start(
                    out=WA[:, OFF["f2_%d" % buf]:OFF["f2_%d" % buf] + 8192],
                    in_=dW["f2"][e0:e0 + 4, :, :, :].rearrange("e s f d -> (e s) f d")
                    .rearrange("(j two) f d -> (two f) j d", two=2)),
                    writes=[rf2], extra=pre2)

                rsel = reg(rWA, "sel")
                rP = reg(rSC, "P")
                rPT2 = reg(rSC, "PT2")
                sel_fin = []

                def gath_p(tt2):
                    if tt2 >= NTT:
                        return
                    gdst = S_P if tt2 % 2 == 0 else 0
                    rP_ = reg(rSC, "P" if tt2 % 2 == 0 else "P1")
                    bld.emit("v", [lambda e, tt2=tt2, blk=blk, parity=parity, gdst=gdst: e.tensor_tensor(
                        SC[:, gdst + parity * 64:gdst + parity * 64 + 64].rearrange("p (g j) -> p g j", j=8),
                        SC[:, S_PERMT + tt2 * 64 + blk * 16:S_PERMT + tt2 * 64 + (blk + 1) * 16]
                        .rearrange("p (j two) -> p j two", two=2)[:, :, parity]
                        .rearrange("p j -> p () j").broadcast_to([128, 8, 8]),
                        CN["bdP"][:, :].rearrange("p g -> p g ()").broadcast_to([128, 8, 8]),
                        ALU.mult) for parity in range(2)],
                        reads=[rpt, rCN], writes=[rP_])

                gath_p(0)
                gath_p(1)
                for tt in range(NTT):
                    gdst = S_P if tt % 2 == 0 else 0
                    rP_ = reg(rSC, "P" if tt % 2 == 0 else "P1")
                    for half in range(2):
                        pt_ = PA if half == 0 else PB
                        slg = rPB["PA" if half == 0 else "PB"]
                        bld.emit("t", [lambda e, tt=tt, q=q, half=half, pt_=pt_, gdst=gdst: e.matmul(
                            pt_[:, q * 128:(q + 1) * 128],
                            YT[:, tt * 1024 + (half * 4 + q) * 128:tt * 1024 + (half * 4 + q + 1) * 128],
                            SC[:, gdst:gdst + 128], start=True, stop=True) for q in range(4)],
                            reads=[rSBtm[tt], rP_], writes=[slg])
                        if half == 1:
                            gath_p(tt + 2)
                        eng = "v" if half == 0 else "s"
                        selw = WA[:, OFF["sel"]:OFF["sel"] + 8192].rearrange("p (d n) -> p d n", n=1024)
                        if eng == "v":
                            tk = bld.emit("v", lambda e, tt=tt, half=half, pt_=pt_, selw=selw: e.tensor_copy(
                                selw[:, half * 4:half * 4 + 4, tt * 128:(tt + 1) * 128],
                                pt_[:, :].rearrange("p (d n) -> p d n", n=128)),
                                reads=[slg], extra=rsel.rtoks + [rsel.wtok])
                        else:
                            tk = bld.emit("s", lambda e, tt=tt, half=half, pt_=pt_, selw=selw: e.copy(
                                selw[:, half * 4:half * 4 + 4, tt * 128:(tt + 1) * 128],
                                pt_[:, :].rearrange("p (d n) -> p d n", n=128)),
                                reads=[slg], extra=rsel.rtoks + [rsel.wtok])
                        sel_fin.append(tk)
                rsel.wtok = sel_fin[-1]
                rsel.rtoks = []
                if blk == 0:
                    dbg("sel", WA[:, OFF["sel"]:OFF["sel"] + 8192], sel_fin, regs=[rsel])

                rinter = reg(rWA, "inter")
                selv = WA[:, OFF["sel"]:OFF["sel"] + 8192].rearrange(
                    "p (d t pr g j2) -> p d t pr g j2", d=8, t=8, pr=2, g=8)
                inter_fin = []

                def emit_f1(j):
                    es = blk * EB + j
                    parity, jj = es % 2, j // 2
                    pcb = PC if j % 2 == 0 else PA
                    slf = rPB["PC" if j % 2 == 0 else "PA"]
                    pbase = parity * 64
                    bld.emit("t", [lambda e, kc=kc, j=j, buf=buf, parity=parity, jj=jj, pbase=pbase, pcb=pcb: e.matmul(
                        pcb[pbase:pbase + ESZ, (j % 4) * 128:(j % 4) * 128 + 64],
                        WA[:, OFF["f1_%d" % buf] + kc * 1024 + j * ESZ:
                           OFF["f1_%d" % buf] + kc * 1024 + (j + 1) * ESZ],
                        selv[:, kc, :, parity, :, jj],
                        start=(kc == 0), stop=(kc == NKC - 1)) for kc in range(NKC)],
                        reads=[rsel, rf1], writes=[slf], extra=sel_fin)
                    rin_ = reg(rSC, "INN%d" % (j % 2))
                    bld.emit("s", lambda e, j=j, es=es, pbase=pbase, pcb=pcb: e.activation(
                        SC[pbase:pbase + ESZ, S_INN + (j % 2) * 64:S_INN + (j % 2) * 64 + 64],
                        pcb[pbase:pbase + ESZ, (j % 4) * 128:(j % 4) * 128 + 64],
                        AF.Relu, bias=FBIA[:, es:es + 1]), reads=[slf, rCN], writes=[rin_])

                emit_f1(0)
                for j in range(EB):
                    es = blk * EB + j
                    parity, jj = es % 2, j // 2
                    pbase = parity * 64
                    if j + 1 < EB:
                        emit_f1(j + 1)
                    rin_ = reg(rSC, "INN%d" % (j % 2))
                    for half in range(2):
                        pt_ = PD if half == 0 else PG
                        sli = rPB["PD"] if half == 0 else rPB["PG"]
                        bld.emit("t", lambda e, j=j, buf=buf, parity=parity, jj=jj, half=half, pt_=pt_, pbase=pbase: e.matmul(
                            pt_[0:64, 0:512],
                            SC[pbase:pbase + ESZ, S_INN + (j % 2) * 64:S_INN + (j % 2) * 64 + 64],
                            WA[p64(parity), OFF["f2_%d" % buf] + jj * 1024 + half * 512:
                               OFF["f2_%d" % buf] + jj * 1024 + (half + 1) * 512],
                            start=True, stop=True),
                            reads=[rin_, rf2], writes=[sli])
                        tk = bld.emit("v", lambda e, parity=parity, jj=jj, half=half, pt_=pt_: e.tensor_copy(
                            WA[p64(parity), OFF["inter"] + jj * 1024 + half * 512:
                               OFF["inter"] + jj * 1024 + (half + 1) * 512],
                            pt_[0:64, 0:512]),
                            reads=[sli], extra=rinter.rtoks + [rinter.wtok])
                        inter_fin.append(tk)
                rinter.wtok = inter_fin[-1]
                rinter.rtoks = []
                if blk == 0:
                    dbg("inter", WA[:, OFF["inter"]:OFF["inter"] + 8192], inter_fin, regs=[rinter])

                stg_emitted = [False] * NTT

                def emit_stage(tt2):
                    if tt2 >= NTT or stg_emitted[tt2]:
                        return
                    stg_emitted[tt2] = True
                    sp2 = S_STG if tt2 % 2 == 0 else 1024
                    bld.emit("y", [lambda e, tt2=tt2, parity=parity, sp2=sp2: e.dma_start(
                        out=SC[p64(parity), sp2:sp2 + 1024],
                        in_=WA[parity * 64 + tt2 * 8:parity * 64 + (tt2 + 1) * 8,
                               OFF["inter"]:OFF["inter"] + 8192]
                        .rearrange("p (j d) -> p j d", d=1024)) for parity in range(2)],
                        reads=[rinter], writes=[reg(rSC, "STG%d" % (tt2 % 2))], extra=inter_fin)

                emit_stage(0)
                emit_stage(1)

                def scat_prep(tt2):
                    if tt2 >= NTT:
                        return
                    par2 = tt2 % 2
                    pdst = S_P if par2 == 0 else 0
                    tdst = S_PT2 if par2 == 0 else 128
                    tb2 = PTB if par2 == 0 else PGB
                    tb2n = "PTB" if par2 == 0 else "PG"
                    rP_ = reg(rSC, "P" if par2 == 0 else "P1")
                    bld.emit("v", [lambda e, tt2=tt2, blk=blk, parity=parity, pdst=pdst: e.tensor_tensor(
                        SC[:, pdst + parity * 64:pdst + parity * 64 + 64].rearrange("p (g j) -> p g j", j=8),
                        SC[:, S_PERMT + tt2 * 64 + blk * 16:S_PERMT + tt2 * 64 + (blk + 1) * 16]
                        .rearrange("p (j two) -> p j two", two=2)[:, :, parity]
                        .rearrange("p j -> p () j").broadcast_to([128, 8, 8]),
                        CN["bdP"][:, :].rearrange("p g -> p g ()").broadcast_to([128, 8, 8]),
                        ALU.mult) for parity in range(2)],
                        reads=[rpt, rCN], writes=[rP_], extra=pc_)
                    bld.emit("t", lambda e, pdst=pdst, tb2=tb2: e.transpose(
                        tb2[:, 0:128], SC[:, pdst:pdst + 128], CN["ident"][:, :]),
                        reads=[rP_, rCN], writes=[rPB[tb2n]])
                    bld.emit("v", lambda e, tdst=tdst, tb2=tb2: e.tensor_copy(
                        SC[:, tdst:tdst + 128], tb2[:, 0:128]),
                        reads=[rPB[tb2n]],
                        writes=[reg(rSC, "PT2" if par2 == 0 else "PT21")])

                scat_prep(0)
                scat_prep(1)
                for tt in range(NTT):
                    rstg = reg(rSC, "STG%d" % (tt % 2))
                    stoff = S_STG if tt % 2 == 0 else 1024
                    tdst = S_PT2 if tt % 2 == 0 else 128
                    rPT2_ = reg(rSC, "PT2" if tt % 2 == 0 else "PT21")
                    for half in range(2):
                        pt_ = PR if half == 0 else PS_
                        slx = rPB["PR"] if half == 0 else rPB["PS"]
                        bld.emit("t", lambda e, half=half, pt_=pt_, stoff=stoff, tdst=tdst: e.matmul(
                            pt_[:, 0:512], SC[:, tdst:tdst + 128],
                            SC[:, stoff + half * 512:stoff + (half + 1) * 512],
                            start=True, stop=True),
                            reads=[rPT2_, rstg], writes=[slx])
                        if blk == 0:
                            bld.emit("v", lambda e, tt=tt, half=half, pt_=pt_: e.tensor_tensor(
                                SB[:, tt * 1024 + half * 512:tt * 1024 + (half + 1) * 512],
                                YT[:, tt * 1024 + half * 512:tt * 1024 + (half + 1) * 512],
                                pt_[:, 0:512], ALU.add),
                                reads=[slx, rSBtm[tt]], writes=[rSB[tt]])
                        else:
                            bld.emit("v", lambda e, tt=tt, half=half, pt_=pt_: e.tensor_tensor(
                                SB[:, tt * 1024 + half * 512:tt * 1024 + (half + 1) * 512],
                                SB[:, tt * 1024 + half * 512:tt * 1024 + (half + 1) * 512],
                                pt_[:, 0:512], ALU.add),
                                reads=[slx], writes=[rSB[tt]])
                    emit_stage(tt + 2)
                    scat_prep(tt + 2)
            dbg("moeacc", SB[:, 0:NKC * TC], [r.wtok for r in rSB], regs=list(rSB))

            for tt in range(NTT):
                racc = rSB[tt]
                rsq2 = reg(rSC, "SQL2")
                rlr = reg(rROWS, "l2r")
                bld.emit("v", [
                    lambda e, tt=tt: e.reduce_sum(ROWS[:, 0:1], SB[:, tt * 1024:(tt + 1) * 1024], AX.X),
                    lambda e, tt=tt: e.tensor_tensor(SC[:, 0:1024], SB[:, tt * 1024:(tt + 1) * 1024],
                                                     SB[:, tt * 1024:(tt + 1) * 1024], ALU.mult),
                ], reads=[racc], writes=[rsq2, rlr])
                bld.emit("v", lambda e: e.reduce_sum(ROWS[:, 1:2], SC[:, 0:1024], AX.X),
                         reads=[rsq2], writes=[rlr])
                bld.emit("v", lambda e: e.tensor_scalar_mul(ROWS[:, 2:3], ROWS[:, 0:1], 1.0 / D),
                         reads=[rlr], writes=[rlr])
                bld.emit("v", [
                    lambda e: e.tensor_tensor(ROWS[:, 3:4], ROWS[:, 2:3],
                                              ROWS[:, 2:3], ALU.mult),
                    lambda e: e.tensor_scalar_mul(ROWS[:, 4:5], ROWS[:, 1:2], 1.0 / D),
                ], reads=[rlr], writes=[rlr])
                bld.emit("v", lambda e: e.tensor_sub(ROWS[:, 4:5], ROWS[:, 4:5],
                                                     ROWS[:, 3:4]),
                         reads=[rlr], writes=[rlr])
                bld.emit("s", lambda e: e.activation(ROWS[:, 5:6], ROWS[:, 4:5],
                                                     AF.Ln, bias=CN["eps"][:, 1:2]), reads=[rlr], writes=[rlr])
                bld.emit("s", lambda e: e.activation(ROWS[:, 6:7], ROWS[:, 5:6],
                                                     AF.Exp, scale=-0.5), reads=[rlr], writes=[rlr])
                ryo = reg(rSF, "YO%d" % (tt % 2))
                bld.emit("v", lambda e, tt=tt: e.tensor_scalar(
                    SF[:, F_YO + (tt % 2) * 1024:F_YO + (tt % 2) * 1024 + 1024],
                    SB[:, tt * 1024:(tt + 1) * 1024],
                    ROWS[:, 2:3], ROWS[:, 6:7], ALU.subtract, ALU.mult),
                    reads=[racc, rlr], writes=[ryo])
                ot = bld.emit("y", lambda e, tt=tt, seq=seq, t0g=t0g: e.dma_start(
                    out=dy[seq, t0g + tt * 128:t0g + (tt + 1) * 128, :],
                    in_=SF[:, F_YO + (tt % 2) * 1024:F_YO + (tt % 2) * 1024 + 1024]),
                    reads=[ryo])
                out_toks.append(ot)

    bld.waits("y", out_toks + dbg_toks)

    with nc.Block() as block:
        for eng, meth in (("t", block.tensor), ("v", block.vector), ("s", block.scalar),
                          ("g", block.gpsimd), ("y", block.sync)):
            def runner(e, eng=eng):
                for fn in bld.ops[eng]:
                    fn(e)
            meth(runner)
    stack.close()
    return nc


def _host_consts():
    bf = ml_dtypes.bfloat16
    ident = np.eye(128, dtype=np.float32)
    maskT = np.zeros((128, 128), dtype=np.float32)
    for s_ in range(128):
        maskT[s_, s_:] = 1.0
    hsel = np.zeros((128, NKC * 16), dtype=np.float32)
    hrep = np.zeros((16, NKC * 128), dtype=np.float32)
    for hp in range(NKC):
        for p in range(128):
            h = 2 * hp + (1 if p >= 64 else 0)
            hsel[p, hp * 16 + h] = 1.0
            hrep[h, hp * 128 + p] = 1.0
    ones1 = np.ones((128, 1), dtype=np.float32)
    row1 = np.ones((1, 128), dtype=np.float32)
    bdP = np.zeros((128, 8), dtype=np.float32)
    for t_ in range(128):
        bdP[t_, t_ // 16] = 1.0
    maskT2 = np.zeros((128, 64), dtype=np.float32)
    for p_ in range(128):
        maskT2[p_, p_ % 64:] = 1.0
    seye = np.concatenate([np.eye(64, dtype=np.float32)] * 2, axis=1)
    bdm = np.zeros((128, 128), dtype=np.float32)
    bdm[0:64, 0:64] = 1.0
    bdm[64:128, 64:128] = 1.0
    tie = np.zeros((NES, GRP), dtype=np.float32)
    tie[:, :] = np.linspace(0.0, 1e-6, GRP, dtype=np.float32)[None, :]
    c = {"c_ident": ident, "c_maskT": maskT, "c_hsel": hsel, "c_hrep": hrep,
         "c_ones1": ones1, "c_row1": row1, "c_bdP": bdP, "c_maskT2": maskT2,
         "c_seye": seye, "c_bdm": bdm}
    c = {k: v.astype(bf) for k, v in c.items()}
    c["c_tie"] = tie
    eps = np.zeros((128, 3), dtype=np.float32)
    eps[:, 0] = EPS_RMS
    eps[:, 1] = EPS_LN
    eps[:, 2] = 1.0
    c["c_eps"] = eps
    return c


def kernel(L0_attn_x, L0_attn_Wq, L0_attn_Wk, L0_attn_Wv, L0_attn_Wg, L0_attn_Wo,
           L0_attn_w1, L0_attn_w2, L0_attn_w3, L0_attn_g_attn, L0_attn_g_ffn,
           L0_ffn_controller, L0_ffn_f1, L0_ffn_f2, L0_ffn_bias):
    nc = build_nc()
    consts = _host_consts()
    x = np.ascontiguousarray(np.asarray(L0_attn_x, dtype=np.float32))
    w = {"Wq": L0_attn_Wq, "Wk": L0_attn_Wk, "Wv": L0_attn_Wv, "Wg": L0_attn_Wg,
         "Wo": L0_attn_Wo, "w1": L0_attn_w1, "w2": L0_attn_w2, "w3": L0_attn_w3,
         "ga": L0_attn_g_attn, "gf": L0_attn_g_ffn, "ctl": L0_ffn_controller,
         "f1": L0_ffn_f1, "f2": L0_ffn_f2, "fb": L0_ffn_bias}
    w = {k: np.ascontiguousarray(np.asarray(v, dtype=np.float32)) for k, v in w.items()}
    in_maps = []
    for cid in range(NCORE):
        m = {"x": x[cid * SPC:(cid + 1) * SPC]}
        m.update(w)
        m.update(consts)
        in_maps.append(m)
    res = run_bass_kernel_spmd(nc, in_maps, core_ids=list(range(NCORE)))
    return np.concatenate([r["y"] for r in res.results], axis=0)


# version 20 (final)
# speedup vs baseline: 1.0648x; 1.0124x over previous
import contextlib
import numpy as np
import ml_dtypes
import concourse.bass as bass
import concourse.mybir as mybir
from concourse.bass_utils import run_bass_kernel_spmd

F32 = mybir.dt.float32
BF16 = mybir.dt.bfloat16
AF = mybir.ActivationFunctionType
ALU = mybir.AluOpType
AX = mybir.AxisListType

B, T, D = 16, 2048, 1024
H, DK, DV = 16, 64, 64
DFF = 2736
NE, ESL, ESZ = 16, 4, 64
NES = NE * ESL
GRP = 16
EPS_RMS, EPS_LN = 1e-6, 1e-5

NCORE = 8
SPC = B // NCORE
TC = 1024
NCH = T // TC
PT = 256
NPT = TC // PT
C = 64
NC_T = PT // C
NKC = D // 128
W3KC = (DFF + 127) // 128
EB = 16
NEB = NES // EB
NTT = TC // 128
NLN = TC // 512

DBG = {"name": None, "shape": None}


class Reg:
    __slots__ = ("wtok", "rtoks")

    def __init__(self):
        self.wtok = None
        self.rtoks = []


class Bld:
    ENGS = ("t", "v", "s", "g", "y")
    NLANES = {"g": 20, "y": 8}

    def __init__(self, nc, stack):
        self.nc = nc
        self.ops = {e: [] for e in self.ENGS}
        self.cnt = {e: 0 for e in self.ENGS}
        self.waited = {e: {} for e in self.ENGS}
        self.sems = {e: stack.enter_context(nc.semaphore("sem_" + e)) for e in ("t", "v", "s", "p")}
        self.cnt["p"] = 0
        self.rr = {"g": 0, "y": 0}
        for e in ("g", "y"):
            for i in range(self.NLANES[e]):
                k = e + str(i)
                self.sems[k] = stack.enter_context(nc.semaphore("sem_" + k))
                self.cnt[k] = 0
                for c in self.waited.values():
                    pass

    def _wait(self, eng, pe, v):
        if self.waited[eng].get(pe, 0) >= v:
            return
        self.waited[eng][pe] = v
        sem = self.sems[pe]
        self.ops[eng].append(lambda e, sem=sem, v=v: e.wait_ge(sem, v))

    def _dma_lane(self, eng):
        i = self.rr[eng] % self.NLANES[eng]
        self.rr[eng] += 1
        return eng + str(i)

    def waits(self, eng, toks):
        best = {}
        for t in toks:
            if t is None:
                continue
            pe, v = t
            if best.get(pe, 0) < v:
                best[pe] = v
        for pe, v in best.items():
            self._wait(eng, pe, v)

    def emit(self, eng, fns, reads=(), writes=(), extra=()):
        if not isinstance(fns, (list, tuple)):
            fns = [fns]
        def _add(n, t):
            if isinstance(t, list):
                n.extend(t)
            elif t is not None:
                n.append(t)
        need = []
        for t_ in extra:
            _add(need, t_)
        for r in reads:
            _add(need, r.wtok)
        for w in writes:
            _add(need, w.wtok)
            need.extend(w.rtoks)
        if eng == "p":
            self.waits("g", need)
            sem = self.sems["p"]
            for i, fn in enumerate(fns):
                if i == len(fns) - 1:
                    self.ops["g"].append(lambda e, fn=fn, sem=sem: fn(e).then_inc(sem, 1))
                else:
                    self.ops["g"].append(fn)
            self.cnt["p"] += 1
            tok = ("p", self.cnt["p"])
            for r in reads:
                r.rtoks.append(tok)
            for w in writes:
                w.wtok = tok
                w.rtoks = []
            return tok
        self.waits(eng, need)
        if eng in ("g", "y"):
            lane = self._dma_lane(eng)
            sem = self.sems[lane]
            self._wait(eng, lane, self.cnt[lane])
            for fn in fns:
                self.ops[eng].append(lambda e, fn=fn, sem=sem: fn(e).then_inc(sem, 16))
            self.cnt[lane] += 16 * len(fns)
            tok = (lane, self.cnt[lane])
        else:
            sem = self.sems[eng]
            for i, fn in enumerate(fns):
                if i == len(fns) - 1:
                    self.ops[eng].append(lambda e, fn=fn, sem=sem: fn(e).then_inc(sem, 1))
                else:
                    self.ops[eng].append(fn)
            self.cnt[eng] += 1
            tok = (eng, self.cnt[eng])
        for r in reads:
            r.rtoks.append(tok)
        for w in writes:
            w.wtok = tok
            w.rtoks = []
        return tok

    def tok(self, eng):
        return (eng, self.cnt[eng])

    def barrier(self, extra=()):
        pb = [self.tok("t"), self.tok("v"), self.tok("s"), self.tok("p")] + list(extra)
        for e in ("t", "v", "s"):
            self.waits(e, pb)
        return pb


def build_nc():
    nc = bass.Bass("TRN2", target_bir_lowering=False)
    stack = contextlib.ExitStack()

    dx = nc.dram_tensor("x", [SPC, T, D], F32, kind="ExternalInput").ap()
    dy = nc.dram_tensor("y", [SPC, T, D], F32, kind="ExternalOutput").ap()
    dW = {}
    for nm, shp in [("Wq", [D, D]), ("Wk", [D, D]), ("Wv", [D, D]), ("Wg", [D, D]),
                    ("Wo", [D, D]), ("w1", [D, DFF]), ("w2", [D, DFF]), ("w3", [DFF, D]),
                    ("ga", [D]), ("gf", [D]), ("ctl", [D, NE, ESL]),
                    ("f1", [D, NE, ESL, ESZ]), ("f2", [NE, ESL, ESZ, D]),
                    ("fb", [NE, ESL, ESZ])]:
        dW[nm] = nc.dram_tensor(nm, shp, F32, kind="ExternalInput").ap()
    dC = {}
    for nm, shp in [("ident", [128, 128]), ("maskT", [128, 128]), ("hsel", [128, NKC * 16]),
                    ("hrep", [16, NKC * 128]), ("ones1", [128, 1]), ("row1", [1, 128]),
                    ("bdP", [128, 8]), ("maskT2", [128, 64]), ("seye", [64, 128]),
                    ("bdm", [128, 128])]:
        dC[nm] = nc.dram_tensor("c_" + nm, shp, BF16, kind="ExternalInput").ap()
    dC["tie"] = nc.dram_tensor("c_tie", [NES, GRP], F32, kind="ExternalInput").ap()
    dC["eps"] = nc.dram_tensor("c_eps", [128, 3], F32, kind="ExternalInput").ap()
    ddbg = None
    if DBG["name"]:
        ddbg = nc.dram_tensor("dbg", DBG["shape"], DBG.get("dtype", F32), kind="ExternalOutput").ap()

    sbt = lambda name, shape, dt: stack.enter_context(nc.sbuf_tensor(name, shape, dt))
    WA = sbt("WA", [128, 66304], BF16)
    SB = sbt("SBs", [128, NKC * TC], BF16)
    YT = sbt("YT", [128, NTT * 1024], BF16)
    SC = sbt("SCb", [128, 4992], BF16)
    SF = sbt("SFf", [128, 3334], F32)
    STT_ = sbt("STATE", [128, NKC * 128 + 128], BF16)
    SPARE_ST = NKC * 128
    CN = {k: sbt("k" + k, list(dC[k].shape), dC[k].dtype) for k in dC}
    GA = sbt("gacol", [128, NKC], F32)
    GF = sbt("gfcol", [128, NKC], F32)
    FBIA = sbt("fbias", [ESZ, NES], F32)
    CTL = sbt("ctlw", [128, NKC * NES], BF16)
    ROWS = sbt("rows", [128, 1536], F32)
    RB = sbt("rowsb", [128, 1024], BF16)
    XS = [sbt("XS0", [128, 2 * D], BF16), sbt("XS1", [128, 2 * D], BF16)]

    pst = lambda name, shape, dt: stack.enter_context(nc.psum_tensor(name, shape, dt))
    PA = pst("PA", [128, 512], F32)
    PB = pst("PB", [128, 512], F32)
    PC = pst("PC", [128, 512], F32)
    PDB = pst("PDB", [128, 1024], BF16)
    PD = PDB.bitcast(F32)
    PS_ = pst("PS", [128, 512], F32)
    PR = pst("PR", [128, 512], F32)
    PRB = PR.bitcast(BF16)
    PG = pst("PG", [128, 512], F32)
    PGB = PG.bitcast(BF16)
    PTB = pst("PTB", [128, 1024], BF16)

    bld = Bld(nc, stack)

    S_SQ, S_IR, S_TMP, S_SCM, S_VH, S_EQ, S_EK = 0, 1024, 1280, 1536, 2048, 2304, 2816
    S_KTS = 1792
    S_VTS = 2048
    S_SMM = 3328
    S_KBD, S_VBD = 3456, 3968
    S_MR, S_VR = 1024, 1536
    S_PERM, S_PERMT = 2048, 3072
    S_P, S_PT2, S_INN, S_STG = 3584, 3712, 3840, 3968
    F_SP, F_SCUM, F_COLS = 0, 512, 1024
    F_LG, F_GM = 0, 1024
    F_YO = 1216
    F_PUT = 3270

    rWA = {}
    rXS = [Reg(), Reg()]
    rCN = Reg()
    rSB = [Reg() for _ in range(NKC)]
    rSBtm = [Reg() for _ in range(NTT)]
    rYT, rSC, rSF, rROWS, rRB = {}, {}, {}, {}, {}
    rST = Reg()
    rSTb = [Reg(), Reg()]
    rPB = {k: Reg() for k in ("PA", "PB", "PC", "PD", "PS", "PR", "PG", "PTB")}

    def reg(d, key):
        if key not in d:
            d[key] = Reg()
        return d[key]

    for k in dC:
        bld.emit("g", lambda e, a=CN[k], b=dC[k]: e.dma_start(out=a[:, :], in_=b), writes=[rCN])
    bld.emit("g", lambda e: e.dma_start(out=GA[:, :], in_=dW["ga"].rearrange("(c p) -> p c", p=128), allow_slow_non_contiguous=True), writes=[rCN])
    bld.emit("g", lambda e: e.dma_start(out=GF[:, :], in_=dW["gf"].rearrange("(c p) -> p c", p=128), allow_slow_non_contiguous=True), writes=[rCN])
    bld.emit("g", lambda e: e.dma_start(out=FBIA[:, :], in_=dW["fb"].rearrange("e s f -> f (e s)"), allow_slow_non_contiguous=True), writes=[rCN])
    bld.emit("g", lambda e: e.dma_start(out=CTL[:, :], in_=dW["ctl"].rearrange("(c p) e s -> p c (e s)", p=128)), writes=[rCN])

    OFF = {"Wq": 0, "Wk": 8192, "Wv": 16384, "Wg": 24576, "Wo": 32768,
           "w2": 0, "w3": 21888, "w1": 44416,
           "f1_0": 0, "f1_1": 8192, "f2_0": 16384, "f2_1": 24576,
           "sel": 32768, "inter": 49152}
    WCOLS = {"Wq": 1024, "Wk": 1024, "Wv": 1024, "Wg": 1024, "Wo": 1024,
             "w1": DFF, "w2": DFF, "w3": 1024}

    def load_w(name, src_fn, nchunks, ncols, off, extra=(), prows_fn=None):
        r = reg(rWA, name)
        war = (r.wtok if isinstance(r.wtok, list) else [r.wtok]) + r.rtoks
        toks = []
        for c in range(nchunks):
            pr = prows_fn(c) if prows_fn else 128
            toks.append(bld.emit("g", lambda e, c=c, pr=pr: e.dma_start(
                out=WA[0:pr, off + c * ncols: off + (c + 1) * ncols], in_=src_fn(c)),
                extra=(list(extra) + war) if c == 0 else []))
        r.wtok = toks
        r.rtoks = []
        return r

    def wsl(name, kc, col0, ncols, prows=128):
        o = OFF[name] + kc * WCOLS[name] + col0
        return WA[0:prows, o:o + ncols]

    def p64(hh):
        return slice(hh * 64, (hh + 1) * 64)

    dbg_toks = []

    def dbg(name, ap, deps, regs=()):
        if DBG["name"] != name or ddbg is None:
            return
        dbg_toks.append(bld.emit("g", lambda e: e.dma_start(out=ddbg, in_=ap),
                                 reads=list(regs), extra=deps))

    out_toks = []
    first_phase = True

    for seq in range(SPC):
        for ch in range(NCH):
            t0g = ch * TC
            pb = [] if first_phase else bld.barrier(extra=dbg_toks)
            first_phase = False
            if ch == 0:
                bld.emit("v", lambda e: e.memset(STT_[:, :], 0),
                         writes=[rSTb[0], rSTb[1]] + [reg(rYT, "ST%d" % hp) for hp in range(NKC)])
            bld.emit("v", lambda e: e.memset(SC[:, S_KBD:S_KBD + 512], 0),
                     writes=[reg(rSC, "KBD")], extra=pb)

            x2_done = []
            xp_emitted = [False] * NPT

            def emit_xload(ti2, seq2, base2):
                if ti2 >= NPT or xp_emitted[ti2]:
                    return
                xp_emitted[ti2] = True
                par2 = ti2 % 2
                t02 = base2 + ti2 * PT
                xwar = [rXS[par2].wtok] + rXS[par2].rtoks
                xtoks = []
                for a2 in range(2):
                    xtoks.append(bld.emit("g", lambda e, par2=par2, seq2=seq2, t02=t02, a2=a2: e.dma_start(
                        out=XS[par2][:, a2 * D:(a2 + 1) * D],
                        in_=dx[seq2, t02 + a2 * 128:t02 + (a2 + 1) * 128, :]),
                        extra=xwar if a2 == 0 else []))
                rXS[par2].wtok = xtoks
                rXS[par2].rtoks = []

            emit_xload(0, seq, t0g)
            emit_xload(1, seq, t0g)
            for wn in ("Wq", "Wk", "Wv", "Wg", "Wo"):
                load_w(wn, lambda c, wn=wn: dW[wn][c * 128:(c + 1) * 128, :], NKC, 1024,
                       OFF[wn], extra=pb)
            for ti in range(NPT):
                par = ti % 2
                t0 = t0g + ti * PT
                tl = ti * PT
                rh = reg(rYT, "A_H")
                rqh, rkh = reg(rYT, "A_QH"), reg(rYT, "A_KH")
                rvf, ro = reg(rYT, "A_VF"), reg(rYT, "A_H")
                A_H, A_QH, A_KH, A_O, A_VF = 0, 2048, 4096, 0, 6144
                for a in range(2):
                    bld.emit("t", [lambda e, a=a, kc=kc, par=par: e.transpose(
                        PTB[:, kc * 128:(kc + 1) * 128],
                        XS[par][:, a * D + kc * 128: a * D + (kc + 1) * 128],
                        CN["ident"][:, :]) for kc in range(NKC)],
                        reads=[rXS[par], rCN], writes=[rPB["PTB"]], extra=pb)
                    bld.emit("v", lambda e, a=a, tl=tl: e.tensor_copy(
                        SB[:, :].rearrange("p (c m) -> p c m", m=TC)[:, :, tl + a * 128:tl + (a + 1) * 128],
                        PTB[:, :].rearrange("p (c n) -> p c n", n=128)),
                        reads=[rPB["PTB"]],
                        writes=[rSB[kc2] for kc2 in range(NKC)] if a == 1 else [], extra=pb)
                emit_xload(ti + 2, seq, t0g)
                for kc in range(NKC):
                    rsq = reg(rSC, "SQ%d" % (kc % 2))
                    bld.emit("v", lambda e, kc=kc, tl=tl: e.tensor_tensor(
                        SC[:, S_SQ + (kc % 2) * 256:S_SQ + (kc % 2) * 256 + 256],
                        SB[:, kc * TC + tl:kc * TC + tl + 256],
                        SB[:, kc * TC + tl:kc * TC + tl + 256], ALU.mult),
                        reads=[rSB[kc]], writes=[rsq])
                    bld.emit("t", lambda e, kc=kc: e.matmul(
                        PS_[0:1, 0:256], CN["ones1"][:, :],
                        SC[:, S_SQ + (kc % 2) * 256:S_SQ + (kc % 2) * 256 + 256],
                        start=(kc == 0), stop=(kc == NKC - 1)),
                        reads=[rsq, rCN], writes=[rPB["PS"]] if kc in (0, NKC - 1) else [])
                rr1 = reg(rROWS, "r1")
                bld.emit("s", lambda e: e.activation(ROWS[0:1, 0:256], PS_[0:1, 0:256], AF.Ln,
                                                     bias=CN["eps"][0:1, 0:1], scale=1.0 / D),
                         reads=[rPB["PS"]], writes=[rr1])
                rb1 = reg(rRB, "r1")
                bld.emit("s", lambda e: e.activation(RB[0:1, 0:256], ROWS[0:1, 0:256], AF.Exp, scale=-0.5),
                         reads=[rr1], writes=[rb1])
                bld.emit("t", lambda e: e.matmul(PR[:, 0:256], CN["row1"][:, :], RB[0:1, 0:256],
                                                 start=True, stop=True),
                         reads=[rb1, rCN], writes=[rPB["PR"]])
                bld.emit("v", lambda e: e.tensor_copy(SC[:, S_IR:S_IR + 256], PR[:, 0:256]),
                         reads=[rPB["PR"]], writes=[reg(rSC, "IR")])
                bld.emit("v", [lambda e, kc=kc, tl=tl: e.scalar_tensor_tensor(
                    YT[:, A_H + kc * 256:A_H + (kc + 1) * 256],
                    SB[:, kc * TC + tl:kc * TC + tl + 256],
                    GA[:, kc:kc + 1], SC[:, S_IR:S_IR + 256], ALU.mult, ALU.mult)
                    for kc in range(NKC)],
                    reads=list(rSB) + [reg(rSC, "IR"), rCN], writes=[rh])
                dbg("h", YT[:, A_H:A_H + 2048], [rh.wtok], regs=[rh])

                accs = [("PA0", PA, 0), ("PA1", PA, 256), ("PB0", PB, 0), ("PB1", PB, 256)]
                na = [0]

                def proj_mm(wn, mc):
                    nm, pt_, c0 = accs[na[0] % 4]
                    na[0] += 1
                    sl = rPB[nm[:2]]
                    bld.emit("t", [lambda e, kc=kc, mc=mc, wn=wn, pt_=pt_, c0=c0: e.matmul(
                        pt_[:, c0:c0 + 256], wsl(wn, kc, mc * 128, 128),
                        YT[:, A_H + kc * 256:A_H + (kc + 1) * 256],
                        start=(kc == 0), stop=(kc == NKC - 1)) for kc in range(NKC)],
                        reads=[rh, reg(rWA, wn)], writes=[sl])
                    return sl, pt_, c0

                sq_toks = {}
                for wn, dst, rdst in (("Wq", A_QH, rqh), ("Wk", A_KH, rkh)):
                    for mc in range(NKC):
                        sl, pt_, c0 = proj_mm(wn, mc)
                        ev = bld.emit("s", lambda e, pt_=pt_, c0=c0, dst=dst, mc=mc: e.copy(
                            YT[:, dst + mc * 256:dst + (mc + 1) * 256], pt_[:, c0:c0 + 256]),
                            reads=[sl], writes=[rdst])
                for mc in range(NKC):
                    sl, pt_, c0 = proj_mm("Wv", mc)
                    bld.emit("s", lambda e, pt_=pt_, c0=c0, mc=mc: e.copy(
                        YT[:, A_VF + mc * 256:A_VF + (mc + 1) * 256], pt_[:, c0:c0 + 256]),
                        reads=[sl], writes=[rvf])
                sstat = rPB["PS"]
                for wn, dst, sc0, rdst in (("Wq", A_QH, 0, rqh), ("Wk", A_KH, 256, rkh)):
                    for mc in range(NKC):
                        off2 = S_TMP if mc % 2 == 0 else S_VH
                        rsqt = reg(rSC, "TMP%d" % (mc % 2))
                        bld.emit("s", lambda e, dst=dst, mc=mc, off2=off2: e.activation(
                            SC[:, off2:off2 + 256],
                            YT[:, dst + mc * 256:dst + (mc + 1) * 256], AF.Square),
                            reads=[rdst], writes=[rsqt])
                        bld.emit("t", lambda e, mc=mc, sc0=sc0, off2=off2: e.matmul(
                            PS_[0:16, sc0:sc0 + 256], CN["hsel"][:, mc * 16:(mc + 1) * 16],
                            SC[:, off2:off2 + 256],
                            start=(mc == 0), stop=(mc == NKC - 1)),
                            reads=[rsqt, rCN],
                            writes=[sstat] if mc in (0, NKC - 1) else [])

                for which, sc0, ro0 in (("q", 0, 256), ("k", 256, 768)):
                    rmx = reg(rROWS, "mx" + which)
                    bld.emit("v", lambda e, sc0=sc0, ro0=ro0: e.tensor_scalar_max(
                        ROWS[0:16, ro0:ro0 + 256], PS_[0:16, sc0:sc0 + 256], 1e-24),
                        reads=[rPB["PS"]], writes=[rmx])
                    bld.emit("s", lambda e, ro0=ro0: e.activation(
                        ROWS[0:16, ro0 + 256:ro0 + 512], ROWS[0:16, ro0:ro0 + 256], AF.Ln),
                        reads=[rmx], writes=[rmx])
                    rrb = reg(rRB, "rq" + which)
                    bld.emit("s", lambda e, ro0=ro0, rb0=(256 if ro0 == 256 else 512): e.activation(
                        RB[0:16, rb0:rb0 + 256], ROWS[0:16, ro0 + 256:ro0 + 512], AF.Exp, scale=-0.5),
                        reads=[rmx], writes=[rrb])

                gcols = []

                def gate_tail(mc):
                    rot = mc % 2
                    rbank = PR if rot == 0 else PS_
                    rslot = rPB["PR" if rot == 0 else "PS"]
                    bld.emit("t", lambda e, mc=mc, rbank=rbank: e.matmul(
                        rbank[:, 0:512], CN["hrep"][:, mc * 128:(mc + 1) * 128],
                        RB[0:16, 256:768], start=True, stop=True),
                        reads=[reg(rRB, "rqq"), reg(rRB, "rqk"), rCN], writes=[rslot])
                    for which, bc0, dst, rdst, eoff, toff, tnm in (
                            ("q", 0, A_QH, rqh, S_EQ, S_TMP, "TMP0"),
                            ("k", 256, A_KH, rkh, S_EK, S_VH, "TMP1")):
                        rexp = reg(rSC, ("EQ%d" if which == "q" else "EK%d") % rot)
                        bld.emit("v", lambda e, rot=rot, eoff=eoff, rbank=rbank, toff=toff, bc0=bc0: e.tensor_tensor(
                            SC[:, toff:toff + 256], rbank[:, bc0:bc0 + 256],
                            SC[:, eoff + rot * 256:eoff + rot * 256 + 256], ALU.mult),
                            reads=[rslot, rexp], writes=[reg(rSC, tnm)])
                        bld.emit("p", lambda e, mc=mc, dst=dst, toff=toff: e.tensor_tensor(
                            YT[:, dst + mc * 256:dst + (mc + 1) * 256],
                            YT[:, dst + mc * 256:dst + (mc + 1) * 256],
                            SC[:, toff:toff + 256], ALU.mult),
                            reads=[reg(rSC, tnm)], writes=[rdst], extra=pb)

                def st_spevac(mc, sl, pt_, c0):
                    rot = mc % 2
                    bld.emit("v", lambda e, pt_=pt_, c0=c0, rot=rot: e.tensor_scalar(
                        SF[:, F_SP + rot * 256:F_SP + rot * 256 + 256], pt_[:, c0:c0 + 256],
                        -1.0, 40.0, ALU.mult, ALU.min),
                        reads=[sl], writes=[reg(rSF, "SP%d" % rot)])

                def st_exp(mc):
                    rot = mc % 2
                    bld.emit("s", lambda e, rot=rot: e.activation(
                        SF[:, F_SCUM + rot * 256:F_SCUM + rot * 256 + 256],
                        SF[:, F_SP + rot * 256:F_SP + rot * 256 + 256], AF.Exp),
                        reads=[reg(rSF, "SP%d" % rot)], writes=[reg(rSF, "SCUM%d" % rot)])

                def st_ln(mc):
                    rot = mc % 2
                    bld.emit("s", lambda e, rot=rot: e.activation(
                        SF[:, F_SP + rot * 256:F_SP + rot * 256 + 256],
                        SF[:, F_SCUM + rot * 256:F_SCUM + rot * 256 + 256], AF.Ln,
                        bias=CN["eps"][0:128, 2:3]),
                        reads=[reg(rSF, "SCUM%d" % rot)], writes=[reg(rSF, "SP%d" % rot)])

                def st_scan(mc):
                    rot = mc % 2
                    bld.emit("v", lambda e, rot=rot: e.tensor_tensor_scan(
                        SF[:, F_SCUM + rot * 256:F_SCUM + rot * 256 + 256],
                        SF[:, F_SP + rot * 256:F_SP + rot * 256 + 256],
                        SF[:, F_SP + rot * 256:F_SP + rot * 256 + 256],
                        0.0, ALU.add, ALU.bypass),
                        reads=[reg(rSF, "SP%d" % rot)], writes=[reg(rSF, "SCUM%d" % rot)])

                def st_cols(mc):
                    rot = mc % 2
                    base = F_COLS + mc * 24
                    rcol = reg(rSF, "COL%d" % mc)
                    bld.emit("v", [
                        lambda e, rot=rot, base=base: e.tensor_copy(
                            SF[:, base:base + NC_T],
                            SF[:, F_SCUM + rot * 256:F_SCUM + rot * 256 + 256]
                            .rearrange("p (c n) -> p c n", n=C)[:, :, C // 2 - 1]),
                        lambda e, rot=rot, base=base: e.tensor_copy(
                            SF[:, base + 4:base + 8],
                            SF[:, F_SCUM + rot * 256:F_SCUM + rot * 256 + 256]
                            .rearrange("p (c n) -> p c n", n=C)[:, :, C - 1]),
                    ], reads=[reg(rSF, "SCUM%d" % rot)], writes=[rcol])
                    bld.emit("v", [
                        lambda e, base=base: e.tensor_scalar_mul(
                            SF[:, base + 8:base + 9], SF[:, base + 4:base + 5], -1.0),
                        lambda e, base=base: e.tensor_sub(
                            SF[:, base + 9:base + 12], SF[:, base + 4:base + 7], SF[:, base + 5:base + 8]),
                    ], reads=[rcol], writes=[rcol])

                def st_acexp(mc):
                    base = F_COLS + mc * 24
                    rcol = reg(rSF, "COL%d" % mc)
                    bld.emit("s", [
                        lambda e, base=base: e.activation(SF[:, base + 12:base + 16],
                                                          SF[:, base + 8:base + 12], AF.Exp),
                        lambda e, base=base: e.activation(SF[:, base + 16:base + 20],
                                                          SF[:, base:base + 4], AF.Exp, scale=-1.0),
                    ], reads=[rcol], writes=[rcol])

                def st_fksub(mc):
                    base = F_COLS + mc * 24
                    rcol = reg(rSF, "COL%d" % mc)
                    bld.emit("v", lambda e, base=base: e.tensor_sub(
                        SF[:, base + 8:base + 12], SF[:, base:base + 4], SF[:, base + 4:base + 8]),
                        reads=[rcol], writes=[rcol])

                def st_fkexp(mc):
                    base = F_COLS + mc * 24
                    rcol = reg(rSF, "COL%d" % mc)
                    bld.emit("s", lambda e, base=base: e.activation(
                        SF[:, base + 16:base + 20], SF[:, base + 8:base + 12], AF.Exp),
                        reads=[rcol], writes=[rcol])

                def st_gkprep(mc):
                    base = F_COLS + mc * 24
                    rcol = reg(rSF, "COL%d" % mc)
                    bld.emit("v", [
                        lambda e, base=base: e.tensor_scalar_mul(
                            SF[:, base + 8:base + 9], SF[:, base:base + 1], -1.0),
                        lambda e, base=base: e.tensor_sub(
                            SF[:, base + 9:base + 12], SF[:, base + 4:base + 7], SF[:, base + 1:base + 4]),
                    ], reads=[rcol], writes=[rcol])

                def st_gkexp(mc):
                    base = F_COLS + mc * 24
                    rcol = reg(rSF, "COL%d" % mc)
                    bld.emit("s", lambda e, base=base: e.activation(
                        SF[:, base + 20:base + 24], SF[:, base + 8:base + 12], AF.Exp),
                        reads=[rcol], writes=[rcol])

                def st_bsub(mc):
                    rot = mc % 2
                    base = F_COLS + mc * 24
                    rcol = reg(rSF, "COL%d" % mc)
                    bld.emit("v", lambda e, rot=rot, base=base: e.tensor_tensor(
                        SF[:, F_SP + rot * 256:F_SP + rot * 256 + 256].rearrange("p (c n) -> p c n", n=C),
                        SF[:, F_SCUM + rot * 256:F_SCUM + rot * 256 + 256].rearrange("p (c n) -> p c n", n=C),
                        SF[:, base:base + NC_T].rearrange("p c -> p c ()").broadcast_to([128, NC_T, C]),
                        ALU.subtract),
                        reads=[reg(rSF, "SCUM%d" % rot), rcol], writes=[reg(rSF, "SP%d" % rot)])

                def st_eqek(mc):
                    rot = mc % 2
                    rsp = reg(rSF, "SP%d" % rot)
                    bld.emit("s", lambda e, rot=rot: e.activation(
                        SC[:, S_EQ + rot * 256:S_EQ + rot * 256 + 256],
                        SF[:, F_SP + rot * 256:F_SP + rot * 256 + 256], AF.Exp, scale=-1.0),
                        reads=[rsp], writes=[reg(rSC, "EQ%d" % rot)])
                    bld.emit("s", lambda e, rot=rot: e.activation(
                        SC[:, S_EK + rot * 256:S_EK + rot * 256 + 256],
                        SF[:, F_SP + rot * 256:F_SP + rot * 256 + 256], AF.Exp),
                        reads=[rsp], writes=[reg(rSC, "EK%d" % rot)])

                for pr in range(NKC // 2):
                    m0, m1 = 2 * pr, 2 * pr + 1
                    prsl = {}
                    for mc in (m0, m1):
                        prsl[mc] = proj_mm("Wg", mc)
                        gcols.append((F_COLS + mc * 24, reg(rSF, "COL%d" % mc)))
                    for mc in (m0, m1):
                        st_spevac(mc, *prsl[mc])
                    for st in (st_exp, st_ln, st_scan, st_cols, st_acexp, st_fksub,
                               st_fkexp, st_gkprep, st_gkexp, st_bsub, st_eqek):
                        for mc in (m0, m1):
                            st(mc)
                    for mc in (m0, m1):
                        gate_tail(mc)
                dbg("qt", YT[:, A_QH:A_QH + 2048], [rqh.wtok], regs=[rqh])
                dbg("kt", YT[:, A_KH:A_KH + 2048], [rkh.wtok], regs=[rkh])

                GP = [dict(kk=4096, kts=S_KTS, vts=S_VTS, vbd=S_VBD, scm=S_SCM,
                           kkn="KK0", ktsn="KTS", vtsn="VTS", vbdn="VBD", scmn="SCM0",
                           tb=PTB, tbn="PTB", scb=PG, scbn="PG", smm=S_SMM, smmn="SMM",
                           pob=PC, pobn="PC", pub=PD, pubn="PD"),
                      dict(kk=4160, kts=4224, vts=4352, vbd=4480, scm=4608,
                           kkn="KK1", ktsn="KTS1", vtsn="VTS1", vbdn="VBD1", scmn="SCM1",
                           tb=PRB, tbn="PR", scb=PS_, scbn="PS", smm=4672, smmn="SMM1",
                           pob=PA, pobn="PA", pub=PB, pubn="PB")]
                for hp in range(NKC):
                    base, rcol = gcols[hp]
                    rkbd = reg(rSC, "KBD")
                    bld.emit("v", [lambda e, hp=hp, hh=hh: e.tensor_copy(
                        SC[p64(hh), S_KBD:S_KBD + 512].rearrange("p (c n) -> p c n", n=128)[:, :, hh * 64:(hh + 1) * 64],
                        YT[p64(hh), A_KH + hp * 256:A_KH + (hp + 1) * 256].rearrange("p (c n) -> p c n", n=64))
                        for hh in range(2)],
                        reads=[rkh], writes=[rkbd], extra=pb)

                    def gla_tr(cc):
                        g = GP[cc % 2]
                        rtb = rPB[g["tbn"]]
                        bld.emit("t", [
                            lambda e, hp=hp, cc=cc, g=g: e.transpose(
                                g["tb"][0:C, 0:128],
                                YT[:, A_KH + hp * 256 + cc * C:A_KH + hp * 256 + (cc + 1) * C],
                                CN["ident"][:, :]),
                            lambda e, hp=hp, cc=cc, g=g: e.transpose(
                                g["tb"][0:C, 128:256],
                                YT[:, A_VF + hp * 256 + cc * C:A_VF + hp * 256 + (cc + 1) * C],
                                CN["ident"][:, :]),
                        ], reads=[rkh, rvf, rCN], writes=[rtb])
                        rkts = reg(rSC, g["ktsn"])
                        rvts = reg(rSC, g["vtsn"])
                        bld.emit("s", [
                            lambda e, g=g: e.copy(SC[0:C, g["kts"]:g["kts"] + 128], g["tb"][0:C, 0:128]),
                            lambda e, g=g: e.copy(SC[0:C, g["vts"]:g["vts"] + 128], g["tb"][0:C, 128:256]),
                        ], reads=[rtb], writes=[rkts, rvts])
                        return (rkts, rvts)

                    def gla_prep(cc, tr_):
                        g = GP[cc % 2]
                        rkts, rvts = tr_
                        slc = rPB[g["scbn"]]
                        bld.emit("t", lambda e, g=g: e.matmul(
                            g["scb"][:, 256:384], CN["seye"][:, :], SC[0:C, g["vts"]:g["vts"] + 128],
                            start=True, stop=True),
                            reads=[rvts, rCN], writes=[slc])
                        rvbd = reg(rSC, g["vbdn"])
                        bld.emit("v", lambda e, g=g: e.tensor_tensor(
                            SC[:, g["vbd"]:g["vbd"] + 128], g["scb"][:, 256:384], CN["bdm"][:, :], ALU.mult),
                            reads=[slc, rCN], writes=[rvbd])
                        bld.emit("t", lambda e, hp=hp, cc=cc, g=g: e.matmul(
                            g["scb"][:, 0:C],
                            SC[:, S_KBD + cc * 128:S_KBD + (cc + 1) * 128],
                            YT[:, A_QH + hp * 256 + cc * C:A_QH + hp * 256 + (cc + 1) * C],
                            start=True, stop=True),
                            reads=[rkbd, rqh], writes=[slc])
                        rscm = reg(rSC, g["scmn"])
                        bld.emit("v", lambda e, g=g: e.tensor_tensor(
                            SC[:, g["scm"]:g["scm"] + C], g["scb"][:, 0:C],
                            CN["maskT2"][:, :], ALU.mult),
                            reads=[slc, rCN], writes=[rscm])
                        rqg = reg(rSC, g["smmn"])
                        bld.emit("v", lambda e, hp=hp, cc=cc, base=base, g=g: e.tensor_scalar(
                            SC[:, g["smm"]:g["smm"] + C],
                            YT[:, A_QH + hp * 256 + cc * C:A_QH + hp * 256 + (cc + 1) * C],
                            SF[:, base + 20 + cc:base + 21 + cc], None, ALU.mult),
                            reads=[rqh, rcol], writes=[rqg])
                        return (rkts, rvts, rvbd, rscm, rqg)

                    trs = {0: gla_tr(0), 1: gla_tr(1)}
                    pregs = {0: gla_prep(0, trs[0])}
                    for cc in range(NC_T):
                        if cc + 2 < NC_T:
                            trs[cc + 2] = gla_tr(cc + 2)
                        if cc + 1 < NC_T:
                            pregs[cc + 1] = gla_prep(cc + 1, trs[cc + 1])
                        g = GP[cc % 2]
                        rkts, rvts, rvbd, rscm, rqg = pregs[cc]
                        soff = hp * 128 if cc % 2 == 0 else SPARE_ST
                        doff = SPARE_ST if cc % 2 == 0 else hp * 128
                        rsS, rdS = rSTb[cc % 2], rSTb[(cc + 1) % 2]
                        slu = rPB[g["pubn"]]
                        puc0 = (cc & 7) * 64
                        bld.emit("t", [lambda e, hh=hh, g=g, puc0=puc0: e.matmul(
                            g["pub"][p64(hh), puc0:puc0 + 64],
                            SC[0:C, g["kts"] + hh * 64:g["kts"] + (hh + 1) * 64],
                            SC[0:C, g["vts"] + hh * 64:g["vts"] + (hh + 1) * 64],
                            start=True, stop=True) for hh in range(2)],
                            reads=[rkts, rvts], writes=[slu])
                        rput = reg(rSF, "PUT")
                        bld.emit("v", lambda e, base=base, cc=cc, g=g, puc0=puc0: e.tensor_scalar(
                            SF[:, F_PUT:F_PUT + 64], g["pub"][:, puc0:puc0 + 64],
                            SF[:, base + 16 + cc:base + 17 + cc], None, ALU.mult),
                            reads=[slu, rcol], writes=[rput])
                        bld.emit("v", [lambda e, cc=cc, base=base, hh=hh, soff=soff, doff=doff:
                                       e.scalar_tensor_tensor(
                            STT_[p64(hh), doff + hh * 64:doff + (hh + 1) * 64],
                            STT_[p64(hh), soff + hh * 64:soff + (hh + 1) * 64],
                            SF[p64(hh), base + 12 + cc:base + 13 + cc],
                            SF[p64(hh), F_PUT:F_PUT + 64],
                            ALU.mult, ALU.add) for hh in range(2)],
                            reads=[rput, rcol, rsS], writes=[rdS])
                        slo = rPB[g["pobn"]]
                        poc0 = (cc & 7) * 64
                        bld.emit("t", [
                            lambda e, g=g, poc0=poc0: e.matmul(
                                g["pob"][:, poc0:poc0 + C], SC[:, g["vbd"]:g["vbd"] + 128],
                                SC[:, g["scm"]:g["scm"] + C], start=True, stop=False),
                            lambda e, g=g, poc0=poc0, soff=soff: e.matmul(
                                g["pob"][:, poc0:poc0 + C], STT_[:, soff:soff + 128],
                                SC[:, g["smm"]:g["smm"] + C],
                                start=False, stop=True),
                        ], reads=[rvbd, rscm, rqg, rsS], writes=[slo])
                        bld.emit("s", lambda e, hp=hp, cc=cc, g=g, poc0=poc0: e.copy(
                            YT[:, A_O + hp * 256 + cc * C:A_O + hp * 256 + (cc + 1) * C],
                            g["pob"][:, poc0:poc0 + C]),
                            reads=[slo], writes=[ro])
                dbg("o", YT[:, A_O:A_O + 2048], [ro.wtok], regs=[ro])

                for mc in range(NKC):
                    nm, pt_, c0 = accs[na[0] % 4]
                    na[0] += 1
                    sl = rPB[nm[:2]]
                    bld.emit("t", [lambda e, kc=kc, mc=mc, pt_=pt_, c0=c0: e.matmul(
                        pt_[:, c0:c0 + 256], wsl("Wo", kc, mc * 128, 128),
                        YT[:, A_O + kc * 256:A_O + (kc + 1) * 256],
                        start=(kc == 0), stop=(kc == NKC - 1)) for kc in range(NKC)],
                        reads=[ro, reg(rWA, "Wo")], writes=[sl])
                    tkx = bld.emit("v", lambda e, pt_=pt_, c0=c0, mc=mc, tl=tl: e.tensor_tensor(
                        SB[:, mc * TC + tl:mc * TC + tl + PT],
                        SB[:, mc * TC + tl:mc * TC + tl + PT],
                        pt_[:, c0:c0 + 256], ALU.add),
                        reads=[sl], writes=[rSB[mc]])
                    x2_done.append(tkx)
            dbg("x2", SB[:, 0:NKC * TC], x2_done, regs=list(rSB))

            pb = bld.barrier(extra=dbg_toks)
            z_done = []
            xp_emitted = [False] * NPT
            emit_xload(0, seq, t0g)
            emit_xload(1, seq, t0g)
            load_w("w1", lambda c: dW["w1"][c * 128:(c + 1) * 128, :], NKC, DFF, OFF["w1"], extra=pb)
            load_w("w2", lambda c: dW["w2"][c * 128:(c + 1) * 128, :], NKC, DFF, OFF["w2"], extra=pb)
            load_w("w3", lambda c: dW["w3"][c * 128:min(DFF, (c + 1) * 128), :], W3KC, 1024, OFF["w3"],
                   extra=pb, prows_fn=lambda c: min(128, DFF - c * 128))
            for ti in range(NPT):
                par = ti % 2
                t0 = t0g + ti * PT
                tl = ti * PT
                rh2, ru = reg(rYT, "B_H2"), reg(rYT, "B_U")
                B_H2, B_U = 0, 2048
                for kc in range(NKC):
                    rsq = reg(rSC, "SQ%d" % (kc % 2))
                    bld.emit("v", lambda e, kc=kc, tl=tl: e.tensor_tensor(
                        SC[:, S_SQ + (kc % 2) * 256:S_SQ + (kc % 2) * 256 + 256],
                        SB[:, kc * TC + tl:kc * TC + tl + 256],
                        SB[:, kc * TC + tl:kc * TC + tl + 256], ALU.mult),
                        reads=[rSB[kc]], writes=[rsq], extra=pb)
                    bld.emit("t", lambda e, kc=kc: e.matmul(
                        PS_[0:1, 0:256], CN["ones1"][:, :],
                        SC[:, S_SQ + (kc % 2) * 256:S_SQ + (kc % 2) * 256 + 256],
                        start=(kc == 0), stop=(kc == NKC - 1)),
                        reads=[rsq, rCN], writes=[rPB["PS"]] if kc in (0, NKC - 1) else [])
                rr1 = reg(rROWS, "r1")
                bld.emit("s", lambda e: e.activation(ROWS[0:1, 0:256], PS_[0:1, 0:256], AF.Ln,
                                                     bias=CN["eps"][0:1, 0:1], scale=1.0 / D),
                         reads=[rPB["PS"]], writes=[rr1])
                rb1 = reg(rRB, "r1")
                bld.emit("s", lambda e: e.activation(RB[0:1, 0:256], ROWS[0:1, 0:256], AF.Exp, scale=-0.5),
                         reads=[rr1], writes=[rb1])
                bld.emit("t", lambda e: e.matmul(PR[:, 0:256], CN["row1"][:, :], RB[0:1, 0:256],
                                                 start=True, stop=True),
                         reads=[rb1, rCN], writes=[rPB["PR"]])
                bld.emit("v", lambda e: e.tensor_copy(SC[:, S_IR:S_IR + 256], PR[:, 0:256]),
                         reads=[rPB["PR"]], writes=[reg(rSC, "IR")])
                bld.emit("v", [lambda e, kc=kc, tl=tl: e.scalar_tensor_tensor(
                    YT[:, B_H2 + kc * 256:B_H2 + (kc + 1) * 256],
                    SB[:, kc * TC + tl:kc * TC + tl + 256],
                    GF[:, kc:kc + 1], SC[:, S_IR:S_IR + 256], ALU.mult, ALU.mult)
                    for kc in range(NKC)],
                    reads=list(rSB) + [reg(rSC, "IR"), rCN], writes=[rh2])
                for a in range(2):
                    bld.emit("t", [lambda e, a=a, kc=kc, par=par: e.transpose(
                        PTB[:, kc * 128:(kc + 1) * 128],
                        XS[par][:, a * D + kc * 128: a * D + (kc + 1) * 128],
                        CN["ident"][:, :]) for kc in range(NKC)],
                        reads=[rXS[par], rCN, rh2], writes=[rPB["PTB"]], extra=pb)
                    bld.emit("v", lambda e, a=a, tl=tl: e.tensor_tensor(
                        SB[:, :].rearrange("p (c m) -> p c m", m=TC)[:, :, tl + a * 128:tl + (a + 1) * 128],
                        SB[:, :].rearrange("p (c m) -> p c m", m=TC)[:, :, tl + a * 128:tl + (a + 1) * 128],
                        PTB[:, :].rearrange("p (c n) -> p c n", n=128), ALU.add),
                        reads=[rPB["PTB"]], writes=[rSB[kc2] for kc2 in range(NKC)])
                emit_xload(ti + 2, seq, t0g)

                accsB = [("PA0", PA, 0), ("PA1", PA, 256), ("PB0", PB, 0), ("PB1", PB, 256),
                         ("PC0", PC, 0), ("PC1", PC, 256), ("PD0", PD, 0), ("PD1", PD, 256)]
                nb = [0]
                for mc in range(W3KC):
                    rows = min(128, DFF - mc * 128)
                    nm, pt_, c0 = accsB[nb[0] % 8]
                    nb[0] += 1
                    sl = rPB[nm[:2]]
                    bld.emit("t", [lambda e, kc=kc, mc=mc, pt_=pt_, c0=c0, rows=rows: e.matmul(
                        pt_[0:rows, c0:c0 + 256], wsl("w1", kc, mc * 128, rows),
                        YT[:, B_H2 + kc * 256:B_H2 + (kc + 1) * 256],
                        start=(kc == 0), stop=(kc == NKC - 1)) for kc in range(NKC)],
                        reads=[rh2, reg(rWA, "w1")], writes=[sl])
                    bld.emit("s", lambda e, pt_=pt_, c0=c0, mc=mc, rows=rows: e.activation(
                        YT[0:rows, B_U + mc * 256:B_U + (mc + 1) * 256],
                        pt_[0:rows, c0:c0 + 256], AF.Silu),
                        reads=[sl], writes=[ru] if mc == W3KC - 1 else [])
                for mc in range(W3KC):
                    rows = min(128, DFF - mc * 128)
                    nm, pt_, c0 = accsB[nb[0] % 8]
                    nb[0] += 1
                    sl = rPB[nm[:2]]
                    bld.emit("t", [lambda e, kc=kc, mc=mc, pt_=pt_, c0=c0, rows=rows: e.matmul(
                        pt_[0:rows, c0:c0 + 256], wsl("w2", kc, mc * 128, rows),
                        YT[:, B_H2 + kc * 256:B_H2 + (kc + 1) * 256],
                        start=(kc == 0), stop=(kc == NKC - 1)) for kc in range(NKC)],
                        reads=[rh2, reg(rWA, "w2")], writes=[sl])
                    bld.emit("v", lambda e, pt_=pt_, c0=c0, mc=mc, rows=rows: e.tensor_tensor(
                        YT[0:rows, B_U + mc * 256:B_U + (mc + 1) * 256],
                        YT[0:rows, B_U + mc * 256:B_U + (mc + 1) * 256],
                        pt_[0:rows, c0:c0 + 256], ALU.mult),
                        reads=[sl, ru], writes=[ru] if mc == W3KC - 1 else [])
                for mc in range(NKC):
                    nm, pt_, c0 = accsB[nb[0] % 8]
                    nb[0] += 1
                    sl = rPB[nm[:2]]
                    bld.emit("t", [lambda e, kc=kc, mc=mc, pt_=pt_, c0=c0,
                                   rows=min(128, DFF - kc * 128): e.matmul(
                        pt_[:, c0:c0 + 256], wsl("w3", kc, mc * 128, 128, prows=rows),
                        YT[0:rows, B_U + kc * 256:B_U + (kc + 1) * 256],
                        start=(kc == 0), stop=(kc == W3KC - 1)) for kc in range(W3KC)],
                        reads=[ru, reg(rWA, "w3")], writes=[sl])
                    tkz = bld.emit("v", lambda e, pt_=pt_, c0=c0, mc=mc, tl=tl: e.tensor_tensor(
                        SB[:, mc * TC + tl:mc * TC + tl + 256],
                        SB[:, mc * TC + tl:mc * TC + tl + 256], pt_[:, c0:c0 + 256], ALU.add),
                        reads=[sl], writes=[rSB[mc]])
                    z_done.append(tkz)
            dbg("z", SB[:, 0:NKC * TC], z_done, regs=list(rSB))

            pb = bld.barrier(extra=dbg_toks)
            for lam in range(NLN):
                l0 = lam * 512
                bld.emit("v", lambda e, l0=l0: e.tensor_tensor(
                    SC[:, 0:4096].rearrange("p (c n) -> p c n", n=512),
                    SB[:, :].rearrange("p (c m) -> p c m", m=TC)[:, :, l0:l0 + 512],
                    SB[:, :].rearrange("p (c m) -> p c m", m=TC)[:, :, l0:l0 + 512], ALU.mult),
                    reads=list(rSB), writes=[reg(rSC, "SQ4")], extra=pb)
                bld.emit("t", [lambda e, kc=kc, l0=l0: e.matmul(
                    PS_[0:1, 0:512], CN["ones1"][:, :], SB[:, kc * TC + l0:kc * TC + l0 + 512],
                    start=(kc == 0), stop=(kc == NKC - 1)) for kc in range(NKC)],
                    reads=list(rSB) + [rCN], writes=[rPB["PS"]], extra=pb)
                bld.emit("t", [lambda e, kc=kc: e.matmul(
                    PR[0:1, 0:512], CN["ones1"][:, :], SC[:, kc * 512:(kc + 1) * 512],
                    start=(kc == 0), stop=(kc == NKC - 1)) for kc in range(NKC)],
                    reads=[reg(rSC, "SQ4"), rCN], writes=[rPB["PR"]])
                rln = reg(rROWS, "ln")
                bld.emit("v", lambda e: e.tensor_scalar_mul(ROWS[0:1, 0:512], PS_[0:1, 0:512], 1.0 / D),
                         reads=[rPB["PS"]], writes=[rln])
                bld.emit("v", [
                    lambda e: e.tensor_tensor(ROWS[0:1, 512:1024], ROWS[0:1, 0:512],
                                              ROWS[0:1, 0:512], ALU.mult),
                    lambda e: e.tensor_scalar_mul(ROWS[0:1, 1024:1536], PR[0:1, 0:512], 1.0 / D),
                ], reads=[rln, rPB["PR"]], writes=[rln])
                bld.emit("v", lambda e: e.tensor_sub(ROWS[0:1, 1024:1536], ROWS[0:1, 1024:1536],
                                                     ROWS[0:1, 512:1024]),
                         reads=[rln], writes=[rln])
                bld.emit("s", lambda e: e.activation(ROWS[0:1, 512:1024], ROWS[0:1, 1024:1536],
                                                     AF.Ln, bias=CN["eps"][0:1, 1:2]), reads=[rln], writes=[rln])
                rlb = reg(rRB, "ln")
                bld.emit("s", [
                    lambda e: e.activation(RB[0:1, 0:512], ROWS[0:1, 512:1024], AF.Exp, scale=-0.5),
                    lambda e: e.copy(RB[0:1, 512:1024], ROWS[0:1, 0:512]),
                ], reads=[rln], writes=[rlb])
                bld.emit("t", lambda e: e.matmul(PR[:, 0:512], CN["row1"][:, :], RB[0:1, 512:1024],
                                                 start=True, stop=True),
                         reads=[rlb, rCN], writes=[rPB["PR"]])
                bld.emit("v", lambda e: e.tensor_copy(SC[:, S_MR:S_MR + 512], PR[:, 0:512]),
                         reads=[rPB["PR"]], writes=[reg(rSC, "MR")])
                bld.emit("t", lambda e: e.matmul(PS_[:, 0:512], CN["row1"][:, :], RB[0:1, 0:512],
                                                 start=True, stop=True),
                         reads=[rlb, rCN], writes=[rPB["PS"]])
                bld.emit("v", lambda e: e.tensor_copy(SC[:, S_VR:S_VR + 512], PS_[:, 0:512]),
                         reads=[rPB["PS"]], writes=[reg(rSC, "VR")])
                for kc in range(NKC):
                    bld.emit("v", lambda e, kc=kc, l0=l0: e.tensor_tensor(
                        SB[:, kc * TC + l0:kc * TC + l0 + 512],
                        SB[:, kc * TC + l0:kc * TC + l0 + 512], SC[:, S_MR:S_MR + 512], ALU.subtract),
                        reads=[reg(rSC, "MR")], writes=[rSB[kc]])
                    bld.emit("v", lambda e, kc=kc, l0=l0: e.tensor_tensor(
                        SB[:, kc * TC + l0:kc * TC + l0 + 512],
                        SB[:, kc * TC + l0:kc * TC + l0 + 512], SC[:, S_VR:S_VR + 512], ALU.mult),
                        reads=[reg(rSC, "VR")], writes=[rSB[kc]])
            dbg("y1", SB[:, 0:NKC * TC], [r.wtok for r in rSB], regs=list(rSB))

            for tt in range(NTT):
                bld.emit("t", [lambda e, tt=tt, kc=kc: e.transpose(
                    PTB[:, kc * 128:(kc + 1) * 128],
                    SB[:, kc * TC + tt * 128:kc * TC + (tt + 1) * 128],
                    CN["ident"][:, :]) for kc in range(NKC)],
                    reads=list(rSB) + [rCN], writes=[rPB["PTB"]])
                if tt % 2 == 0:
                    bld.emit("v", lambda e, tt=tt: e.tensor_copy(
                        YT[:, tt * 1024:(tt + 1) * 1024], PTB[:, :]),
                        reads=[rPB["PTB"]], writes=[rSBtm[tt]])
                else:
                    bld.emit("s", lambda e, tt=tt: e.copy(
                        YT[:, tt * 1024:(tt + 1) * 1024], PTB[:, :]),
                        reads=[rPB["PTB"]], writes=[rSBtm[tt]])
            rlg = reg(rSF, "LG")
            rpm = reg(rSC, "PERM")
            for lam in range(NLN):
                l0 = lam * 512
                bld.emit("t", [lambda e, kc=kc, l0=l0: e.matmul(
                    PG[0:NES, 0:512], CTL[:, kc * NES:(kc + 1) * NES],
                    SB[:, kc * TC + l0:kc * TC + l0 + 512],
                    start=(kc == 0), stop=(kc == NKC - 1)) for kc in range(NKC)],
                    reads=list(rSB) + [rCN], writes=[rPB["PG"]])
                bld.emit("v", lambda e, l0=l0: e.tensor_tensor(
                    SF[0:NES, F_LG + l0:F_LG + l0 + 512].rearrange("p (g n) -> p g n", n=GRP),
                    PG[0:NES, 0:512].rearrange("p (g n) -> p g n", n=GRP),
                    CN["tie"][:, :].rearrange("p n -> p () n").broadcast_to([NES, 512 // GRP, GRP]),
                    ALU.add),
                    reads=[rPB["PG"], rCN], writes=[rlg])
                bld.emit("v", lambda e, l0=l0: e.tensor_reduce(
                    SF[0:NES, F_GM + l0 // GRP:F_GM + l0 // GRP + 32],
                    SF[0:NES, F_LG + l0:F_LG + l0 + 512].rearrange("p (g n) -> p g n", n=GRP),
                    AX.X, ALU.max), reads=[rlg], writes=[rlg])
                bld.emit("v", lambda e, l0=l0: e.tensor_tensor(
                    SC[0:NES, S_PERM + l0:S_PERM + l0 + 512].rearrange("p (g n) -> p g n", n=GRP),
                    SF[0:NES, F_LG + l0:F_LG + l0 + 512].rearrange("p (g n) -> p g n", n=GRP),
                    SF[0:NES, F_GM + l0 // GRP:F_GM + l0 // GRP + 32]
                    .rearrange("p g -> p g ()").broadcast_to([NES, 32, GRP]),
                    ALU.is_equal), reads=[rlg], writes=[rpm])
            dbg("perm", SC[0:NES, S_PERM:S_PERM + TC], [rpm.wtok], regs=[rpm])
            rpt = reg(rSC, "PERMT")
            for tt in range(NTT):
                bld.emit("t", lambda e, tt=tt: e.transpose(
                    PTB[:, 0:64], SC[0:NES, S_PERM + tt * 128:S_PERM + (tt + 1) * 128],
                    CN["ident"][0:NES, 0:NES]), reads=[rpm, rCN], writes=[rPB["PTB"]])
                bld.emit("v", lambda e, tt=tt: e.tensor_copy(
                    SC[:, S_PERMT + tt * 64:S_PERMT + (tt + 1) * 64], PTB[:, 0:64]),
                    reads=[rPB["PTB"]], writes=[rpt] if tt == NTT - 1 else [])

            pc_ = bld.barrier(extra=dbg_toks)
            for blk in range(NEB):
                buf = blk % 2
                e0 = (blk * EB) // ESL
                pre2 = pc_ if blk < 2 else [bld.tok("t")]
                rf1 = load_w("f1_%d" % buf, lambda c, e0=e0: dW["f1"][c * 128:(c + 1) * 128, e0:e0 + 4, :, :]
                             .rearrange("p e s f -> p (e s f)"), NKC, EB * ESZ, OFF["f1_%d" % buf],
                             extra=pre2)
                rf2 = reg(rWA, "f2_%d" % buf)
                bld.emit("g", lambda e, e0=e0, buf=buf: e.dma_start(
                    out=WA[:, OFF["f2_%d" % buf]:OFF["f2_%d" % buf] + 8192],
                    in_=dW["f2"][e0:e0 + 4, :, :, :].rearrange("e s f d -> (e s) f d")
                    .rearrange("(j two) f d -> (two f) j d", two=2)),
                    writes=[rf2], extra=pre2)

                rsel = reg(rWA, "sel")
                rP = reg(rSC, "P")
                rPT2 = reg(rSC, "PT2")
                sel_fin = []

                def gath_p(tt2):
                    if tt2 >= NTT:
                        return
                    gdst = S_P if tt2 % 2 == 0 else 0
                    rP_ = reg(rSC, "P" if tt2 % 2 == 0 else "P1")
                    bld.emit("v", [lambda e, tt2=tt2, blk=blk, parity=parity, gdst=gdst: e.tensor_tensor(
                        SC[:, gdst + parity * 64:gdst + parity * 64 + 64].rearrange("p (g j) -> p g j", j=8),
                        SC[:, S_PERMT + tt2 * 64 + blk * 16:S_PERMT + tt2 * 64 + (blk + 1) * 16]
                        .rearrange("p (j two) -> p j two", two=2)[:, :, parity]
                        .rearrange("p j -> p () j").broadcast_to([128, 8, 8]),
                        CN["bdP"][:, :].rearrange("p g -> p g ()").broadcast_to([128, 8, 8]),
                        ALU.mult) for parity in range(2)],
                        reads=[rpt, rCN], writes=[rP_])

                gath_p(0)
                gath_p(1)
                for tt in range(NTT):
                    gdst = S_P if tt % 2 == 0 else 0
                    rP_ = reg(rSC, "P" if tt % 2 == 0 else "P1")
                    for half in range(2):
                        pt_ = PA if half == 0 else PB
                        slg = rPB["PA" if half == 0 else "PB"]
                        bld.emit("t", [lambda e, tt=tt, q=q, half=half, pt_=pt_, gdst=gdst: e.matmul(
                            pt_[:, q * 128:(q + 1) * 128],
                            YT[:, tt * 1024 + (half * 4 + q) * 128:tt * 1024 + (half * 4 + q + 1) * 128],
                            SC[:, gdst:gdst + 128], start=True, stop=True) for q in range(4)],
                            reads=[rSBtm[tt], rP_], writes=[slg])
                        if half == 1:
                            gath_p(tt + 2)
                        eng = "v" if half == 0 else "s"
                        selw = WA[:, OFF["sel"]:OFF["sel"] + 8192].rearrange("p (d n) -> p d n", n=1024)
                        if eng == "v":
                            tk = bld.emit("v", lambda e, tt=tt, half=half, pt_=pt_, selw=selw: e.tensor_copy(
                                selw[:, half * 4:half * 4 + 4, tt * 128:(tt + 1) * 128],
                                pt_[:, :].rearrange("p (d n) -> p d n", n=128)),
                                reads=[slg], extra=rsel.rtoks + [rsel.wtok])
                        else:
                            tk = bld.emit("s", lambda e, tt=tt, half=half, pt_=pt_, selw=selw: e.copy(
                                selw[:, half * 4:half * 4 + 4, tt * 128:(tt + 1) * 128],
                                pt_[:, :].rearrange("p (d n) -> p d n", n=128)),
                                reads=[slg], extra=rsel.rtoks + [rsel.wtok])
                        sel_fin.append(tk)
                rsel.wtok = sel_fin[-1]
                rsel.rtoks = []
                if blk == 0:
                    dbg("sel", WA[:, OFF["sel"]:OFF["sel"] + 8192], sel_fin, regs=[rsel])

                rinter = reg(rWA, "inter")
                selv = WA[:, OFF["sel"]:OFF["sel"] + 8192].rearrange(
                    "p (d t pr g j2) -> p d t pr g j2", d=8, t=8, pr=2, g=8)
                inter_fin = []

                def emit_f1(j):
                    es = blk * EB + j
                    parity, jj = es % 2, j // 2
                    pcb = PC if j % 2 == 0 else PA
                    slf = rPB["PC" if j % 2 == 0 else "PA"]
                    pbase = parity * 64
                    bld.emit("t", [lambda e, kc=kc, j=j, buf=buf, parity=parity, jj=jj, pbase=pbase, pcb=pcb: e.matmul(
                        pcb[pbase:pbase + ESZ, (j % 4) * 128:(j % 4) * 128 + 64],
                        WA[:, OFF["f1_%d" % buf] + kc * 1024 + j * ESZ:
                           OFF["f1_%d" % buf] + kc * 1024 + (j + 1) * ESZ],
                        selv[:, kc, :, parity, :, jj],
                        start=(kc == 0), stop=(kc == NKC - 1)) for kc in range(NKC)],
                        reads=[rsel, rf1], writes=[slf], extra=sel_fin)
                    rin_ = reg(rSC, "INN%d" % (j % 2))
                    bld.emit("s", lambda e, j=j, es=es, pbase=pbase, pcb=pcb: e.activation(
                        SC[pbase:pbase + ESZ, S_INN + (j % 2) * 64:S_INN + (j % 2) * 64 + 64],
                        pcb[pbase:pbase + ESZ, (j % 4) * 128:(j % 4) * 128 + 64],
                        AF.Relu, bias=FBIA[:, es:es + 1]), reads=[slf, rCN], writes=[rin_])

                emit_f1(0)
                for j in range(EB):
                    es = blk * EB + j
                    parity, jj = es % 2, j // 2
                    pbase = parity * 64
                    if j + 1 < EB:
                        emit_f1(j + 1)
                    rin_ = reg(rSC, "INN%d" % (j % 2))
                    for half in range(2):
                        pt_ = PD if half == 0 else PG
                        sli = rPB["PD"] if half == 0 else rPB["PG"]
                        bld.emit("t", lambda e, j=j, buf=buf, parity=parity, jj=jj, half=half, pt_=pt_, pbase=pbase: e.matmul(
                            pt_[0:64, 0:512],
                            SC[pbase:pbase + ESZ, S_INN + (j % 2) * 64:S_INN + (j % 2) * 64 + 64],
                            WA[p64(parity), OFF["f2_%d" % buf] + jj * 1024 + half * 512:
                               OFF["f2_%d" % buf] + jj * 1024 + (half + 1) * 512],
                            start=True, stop=True),
                            reads=[rin_, rf2], writes=[sli])
                        tk = bld.emit("v", lambda e, parity=parity, jj=jj, half=half, pt_=pt_: e.tensor_copy(
                            WA[p64(parity), OFF["inter"] + jj * 1024 + half * 512:
                               OFF["inter"] + jj * 1024 + (half + 1) * 512],
                            pt_[0:64, 0:512]),
                            reads=[sli], extra=rinter.rtoks + [rinter.wtok])
                        inter_fin.append(tk)
                rinter.wtok = inter_fin[-1]
                rinter.rtoks = []
                if blk == 0:
                    dbg("inter", WA[:, OFF["inter"]:OFF["inter"] + 8192], inter_fin, regs=[rinter])

                stg_emitted = [False] * NTT

                def emit_stage(tt2):
                    if tt2 >= NTT or stg_emitted[tt2]:
                        return
                    stg_emitted[tt2] = True
                    sp2 = S_STG if tt2 % 2 == 0 else 1024
                    bld.emit("y", [lambda e, tt2=tt2, parity=parity, sp2=sp2: e.dma_start(
                        out=SC[p64(parity), sp2:sp2 + 1024],
                        in_=WA[parity * 64 + tt2 * 8:parity * 64 + (tt2 + 1) * 8,
                               OFF["inter"]:OFF["inter"] + 8192]
                        .rearrange("p (j d) -> p j d", d=1024)) for parity in range(2)],
                        reads=[rinter], writes=[reg(rSC, "STG%d" % (tt2 % 2))], extra=inter_fin)

                emit_stage(0)
                emit_stage(1)

                def scat_prep(tt2):
                    if tt2 >= NTT:
                        return
                    par2 = tt2 % 2
                    pdst = S_P if par2 == 0 else 0
                    tdst = S_PT2 if par2 == 0 else 128
                    tb2 = PTB if par2 == 0 else PGB
                    tb2n = "PTB" if par2 == 0 else "PG"
                    rP_ = reg(rSC, "P" if par2 == 0 else "P1")
                    bld.emit("v", [lambda e, tt2=tt2, blk=blk, parity=parity, pdst=pdst: e.tensor_tensor(
                        SC[:, pdst + parity * 64:pdst + parity * 64 + 64].rearrange("p (g j) -> p g j", j=8),
                        SC[:, S_PERMT + tt2 * 64 + blk * 16:S_PERMT + tt2 * 64 + (blk + 1) * 16]
                        .rearrange("p (j two) -> p j two", two=2)[:, :, parity]
                        .rearrange("p j -> p () j").broadcast_to([128, 8, 8]),
                        CN["bdP"][:, :].rearrange("p g -> p g ()").broadcast_to([128, 8, 8]),
                        ALU.mult) for parity in range(2)],
                        reads=[rpt, rCN], writes=[rP_], extra=pc_)
                    bld.emit("t", lambda e, pdst=pdst, tb2=tb2: e.transpose(
                        tb2[:, 0:128], SC[:, pdst:pdst + 128], CN["ident"][:, :]),
                        reads=[rP_, rCN], writes=[rPB[tb2n]])
                    bld.emit("v", lambda e, tdst=tdst, tb2=tb2: e.tensor_copy(
                        SC[:, tdst:tdst + 128], tb2[:, 0:128]),
                        reads=[rPB[tb2n]],
                        writes=[reg(rSC, "PT2" if par2 == 0 else "PT21")])

                scat_prep(0)
                scat_prep(1)
                for tt in range(NTT):
                    rstg = reg(rSC, "STG%d" % (tt % 2))
                    stoff = S_STG if tt % 2 == 0 else 1024
                    tdst = S_PT2 if tt % 2 == 0 else 128
                    rPT2_ = reg(rSC, "PT2" if tt % 2 == 0 else "PT21")
                    for half in range(2):
                        pt_ = PR if half == 0 else PS_
                        slx = rPB["PR"] if half == 0 else rPB["PS"]
                        bld.emit("t", lambda e, half=half, pt_=pt_, stoff=stoff, tdst=tdst: e.matmul(
                            pt_[:, 0:512], SC[:, tdst:tdst + 128],
                            SC[:, stoff + half * 512:stoff + (half + 1) * 512],
                            start=True, stop=True),
                            reads=[rPT2_, rstg], writes=[slx])
                        if blk == 0:
                            bld.emit("v", lambda e, tt=tt, half=half, pt_=pt_: e.tensor_tensor(
                                SB[:, tt * 1024 + half * 512:tt * 1024 + (half + 1) * 512],
                                YT[:, tt * 1024 + half * 512:tt * 1024 + (half + 1) * 512],
                                pt_[:, 0:512], ALU.add),
                                reads=[slx, rSBtm[tt]], writes=[rSB[tt]])
                        else:
                            bld.emit("v", lambda e, tt=tt, half=half, pt_=pt_: e.tensor_tensor(
                                SB[:, tt * 1024 + half * 512:tt * 1024 + (half + 1) * 512],
                                SB[:, tt * 1024 + half * 512:tt * 1024 + (half + 1) * 512],
                                pt_[:, 0:512], ALU.add),
                                reads=[slx], writes=[rSB[tt]])
                    emit_stage(tt + 2)
                    scat_prep(tt + 2)
            dbg("moeacc", SB[:, 0:NKC * TC], [r.wtok for r in rSB], regs=list(rSB))

            for tt in range(NTT):
                racc = rSB[tt]
                rsq2 = reg(rSC, "SQL2")
                rlr = reg(rROWS, "l2r")
                bld.emit("v", [
                    lambda e, tt=tt: e.reduce_sum(ROWS[:, 0:1], SB[:, tt * 1024:(tt + 1) * 1024], AX.X),
                    lambda e, tt=tt: e.tensor_tensor(SC[:, 0:1024], SB[:, tt * 1024:(tt + 1) * 1024],
                                                     SB[:, tt * 1024:(tt + 1) * 1024], ALU.mult),
                ], reads=[racc], writes=[rsq2, rlr])
                bld.emit("v", lambda e: e.reduce_sum(ROWS[:, 1:2], SC[:, 0:1024], AX.X),
                         reads=[rsq2], writes=[rlr])
                bld.emit("v", lambda e: e.tensor_scalar_mul(ROWS[:, 2:3], ROWS[:, 0:1], 1.0 / D),
                         reads=[rlr], writes=[rlr])
                bld.emit("v", [
                    lambda e: e.tensor_tensor(ROWS[:, 3:4], ROWS[:, 2:3],
                                              ROWS[:, 2:3], ALU.mult),
                    lambda e: e.tensor_scalar_mul(ROWS[:, 4:5], ROWS[:, 1:2], 1.0 / D),
                ], reads=[rlr], writes=[rlr])
                bld.emit("v", lambda e: e.tensor_sub(ROWS[:, 4:5], ROWS[:, 4:5],
                                                     ROWS[:, 3:4]),
                         reads=[rlr], writes=[rlr])
                bld.emit("s", lambda e: e.activation(ROWS[:, 5:6], ROWS[:, 4:5],
                                                     AF.Ln, bias=CN["eps"][:, 1:2]), reads=[rlr], writes=[rlr])
                bld.emit("s", lambda e: e.activation(ROWS[:, 6:7], ROWS[:, 5:6],
                                                     AF.Exp, scale=-0.5), reads=[rlr], writes=[rlr])
                ryo = reg(rSF, "YO%d" % (tt % 2))
                bld.emit("v", lambda e, tt=tt: e.tensor_scalar(
                    SF[:, F_YO + (tt % 2) * 1024:F_YO + (tt % 2) * 1024 + 1024],
                    SB[:, tt * 1024:(tt + 1) * 1024],
                    ROWS[:, 2:3], ROWS[:, 6:7], ALU.subtract, ALU.mult),
                    reads=[racc, rlr], writes=[ryo])
                ot = bld.emit("y", lambda e, tt=tt, seq=seq, t0g=t0g: e.dma_start(
                    out=dy[seq, t0g + tt * 128:t0g + (tt + 1) * 128, :],
                    in_=SF[:, F_YO + (tt % 2) * 1024:F_YO + (tt % 2) * 1024 + 1024]),
                    reads=[ryo])
                out_toks.append(ot)

    bld.waits("y", out_toks + dbg_toks)

    with nc.Block() as block:
        for eng, meth in (("t", block.tensor), ("v", block.vector), ("s", block.scalar),
                          ("g", block.gpsimd), ("y", block.sync)):
            def runner(e, eng=eng):
                for fn in bld.ops[eng]:
                    fn(e)
            meth(runner)
    stack.close()
    return nc


def _host_consts():
    bf = ml_dtypes.bfloat16
    ident = np.eye(128, dtype=np.float32)
    maskT = np.zeros((128, 128), dtype=np.float32)
    for s_ in range(128):
        maskT[s_, s_:] = 1.0
    hsel = np.zeros((128, NKC * 16), dtype=np.float32)
    hrep = np.zeros((16, NKC * 128), dtype=np.float32)
    for hp in range(NKC):
        for p in range(128):
            h = 2 * hp + (1 if p >= 64 else 0)
            hsel[p, hp * 16 + h] = 1.0
            hrep[h, hp * 128 + p] = 1.0
    ones1 = np.ones((128, 1), dtype=np.float32)
    row1 = np.ones((1, 128), dtype=np.float32)
    bdP = np.zeros((128, 8), dtype=np.float32)
    for t_ in range(128):
        bdP[t_, t_ // 16] = 1.0
    maskT2 = np.zeros((128, 64), dtype=np.float32)
    for p_ in range(128):
        maskT2[p_, p_ % 64:] = 1.0
    seye = np.concatenate([np.eye(64, dtype=np.float32)] * 2, axis=1)
    bdm = np.zeros((128, 128), dtype=np.float32)
    bdm[0:64, 0:64] = 1.0
    bdm[64:128, 64:128] = 1.0
    tie = np.zeros((NES, GRP), dtype=np.float32)
    tie[:, :] = np.linspace(0.0, 1e-6, GRP, dtype=np.float32)[None, :]
    c = {"c_ident": ident, "c_maskT": maskT, "c_hsel": hsel, "c_hrep": hrep,
         "c_ones1": ones1, "c_row1": row1, "c_bdP": bdP, "c_maskT2": maskT2,
         "c_seye": seye, "c_bdm": bdm}
    c = {k: v.astype(bf) for k, v in c.items()}
    c["c_tie"] = tie
    eps = np.zeros((128, 3), dtype=np.float32)
    eps[:, 0] = EPS_RMS
    eps[:, 1] = EPS_LN
    eps[:, 2] = 1.0
    c["c_eps"] = eps
    return c


def kernel(L0_attn_x, L0_attn_Wq, L0_attn_Wk, L0_attn_Wv, L0_attn_Wg, L0_attn_Wo,
           L0_attn_w1, L0_attn_w2, L0_attn_w3, L0_attn_g_attn, L0_attn_g_ffn,
           L0_ffn_controller, L0_ffn_f1, L0_ffn_f2, L0_ffn_bias):
    nc = build_nc()
    consts = _host_consts()
    x = np.ascontiguousarray(np.asarray(L0_attn_x, dtype=np.float32))
    w = {"Wq": L0_attn_Wq, "Wk": L0_attn_Wk, "Wv": L0_attn_Wv, "Wg": L0_attn_Wg,
         "Wo": L0_attn_Wo, "w1": L0_attn_w1, "w2": L0_attn_w2, "w3": L0_attn_w3,
         "ga": L0_attn_g_attn, "gf": L0_attn_g_ffn, "ctl": L0_ffn_controller,
         "f1": L0_ffn_f1, "f2": L0_ffn_f2, "fb": L0_ffn_bias}
    w = {k: np.ascontiguousarray(np.asarray(v, dtype=np.float32)) for k, v in w.items()}
    in_maps = []
    for cid in range(NCORE):
        m = {"x": x[cid * SPC:(cid + 1) * SPC]}
        m.update(w)
        m.update(consts)
        in_maps.append(m)
    res = run_bass_kernel_spmd(nc, in_maps, core_ids=list(range(NCORE)))
    return np.concatenate([r["y"] for r in res.results], axis=0)
